# version 11
# baseline (speedup 1.0000x reference)
import contextlib
import numpy as np
import ml_dtypes
import concourse.bass as bass
import concourse.mybir as mybir
from concourse.bass_utils import run_bass_kernel_spmd

F32 = mybir.dt.float32
BF16 = mybir.dt.bfloat16
AF = mybir.ActivationFunctionType
ALU = mybir.AluOpType
AX = mybir.AxisListType

D = 2048
DFF = 5632
B = 16
T = 2048
NCORE = 8
SEQ = B // NCORE
C = 512
NCH = (SEQ * T) // C
CH_PER_SEQ = T // C
NDC = D // 128
NFC = DFF // 128
EPS = 1e-5
MAGIC = 12582912.0

COOP = True

_NF4_RAW = [-1.0, -0.6962, -0.5251, -0.3949, -0.2844, -0.1848, -0.0911, 0.0,
            0.0796, 0.1609, 0.2461, 0.3379, 0.4407, 0.5626, 0.723, 1.0]
NF4 = np.array(_NF4_RAW, dtype=ml_dtypes.bfloat16).astype(np.float64)
L0 = float(np.float32(NF4[0]))
DELTA = [float(np.float32(NF4[k + 1] - NF4[k])) for k in range(15)]


def _bf16_threshold(mid):
    lo = ml_dtypes.bfloat16(mid)
    if np.float64(lo) > mid:
        up = np.float64(lo)
        dn = np.float64(np.nextafter(lo, ml_dtypes.bfloat16(-np.inf)))
    else:
        dn = np.float64(lo)
        up = np.float64(np.nextafter(lo, ml_dtypes.bfloat16(np.inf)))
    return float(np.float32((dn + up) / 2.0))


TAU = [_bf16_threshold((NF4[k] + NF4[k + 1]) / 2.0) for k in range(15)]

BLK = 64
W1_UNITS = DFF // 128
W3_UNITS = D // 128
NB_W1 = DFF * D // BLK
NB_W3 = D * DFF // BLK
NCB_W1 = D // BLK
NCB_W3 = DFF // BLK
G_W1 = 128 * NCB_W1 // 256
G_W3 = 128 * NCB_W3 // 256
K_POOL = [3, 7, 11, 14]
ACC0 = float(np.float32(np.float32(L0) + np.float32(sum(DELTA) / 2.0)))
HDELTA = [float(np.float32(d / 2.0)) for d in DELTA]
UPC_W1 = 6
UPC_W3 = 2
NQ = 4


class EngS:
    def __init__(self, stack, nc, name, e):
        self.name = name
        self.e = e
        self.sem = stack.enter_context(nc.semaphore("s_" + name))
        self.n = 0
        self.waited = {}


class DmaS:
    def __init__(self, stack, nc, name):
        self.name = name
        self.sem = stack.enter_context(nc.semaphore(name))
        self.n = 0


class Buf:
    __slots__ = ("name", "w", "rs")

    def __init__(self, name):
        self.name = name
        self.w = []
        self.rs = []


class K:
    def __init__(self, nc, stack):
        self.nc = nc
        self.eng = {
            "pe": EngS(stack, nc, "pe", nc.tensor),
            "dve": EngS(stack, nc, "dve", nc.vector),
            "act": EngS(stack, nc, "act", nc.scalar),
            "pool": EngS(stack, nc, "pool", nc.gpsimd),
            "sync": EngS(stack, nc, "sync", nc.sync),
        }
        self.stack = stack
        self.dmas = {}
        self.ccs = {}

    def dstream(self, name):
        if name not in self.dmas:
            self.dmas[name] = DmaS(self.stack, self.nc, name)
        return self.dmas[name]

    def ccstream(self, name):
        if name not in self.ccs:
            self.ccs[name] = DmaS(self.stack, self.nc, name)
        return self.ccs[name]

    def _semh(self, s):
        if s.startswith("s_"):
            return self.eng[s[2:]].sem
        if s in self.dmas:
            return self.dmas[s].sem
        if s in self.ccs:
            return self.ccs[s].sem
        raise KeyError(s)

    def _waits(self, en, toks):
        e = self.eng[en]
        need = {}
        for t in toks:
            if t is None:
                continue
            s, v = t
            if need.get(s, -1) < v:
                need[s] = v
        for s, v in need.items():
            if e.waited.get(s, -1) >= v:
                continue
            e.e.wait_ge(self._semh(s), v)
            e.waited[s] = v

    @staticmethod
    def _dep(reads, writes, waw=True):
        toks = []
        for b in reads:
            toks.extend(b.w)
        for b in writes:
            if waw:
                toks.extend(b.w)
            toks.extend(b.rs)
        return toks

    @staticmethod
    def _rec(t, reads, writes, waw=True):
        for b in writes:
            if waw:
                b.w = [t]
                b.rs = []
            else:
                b.w.append(t)
        for b in reads:
            b.rs.append(t)

    def op(self, en, f, reads=(), writes=()):
        e = self.eng[en]
        self._waits(en, self._dep(reads, writes))
        ins = f(e.e)
        e.n += 1
        ins.then_inc(e.sem, 1)
        t = ("s_" + en, e.n)
        self._rec(t, reads, writes)
        return t

    def group(self, en, fs, reads=(), writes=()):
        e = self.eng[en]
        self._waits(en, self._dep(reads, writes))
        last = None
        for f in fs:
            last = f(e.e)
        e.n += 1
        last.then_inc(e.sem, 1)
        t = ("s_" + en, e.n)
        self._rec(t, reads, writes)
        return t

    def dmav(self, stream, q, out_ap, in_ap, reads=(), writes=(), waw=True):
        e = self.eng[q]
        d = self.dstream(stream)
        self._waits(q, self._dep(reads, writes, waw))
        bb = self.nc.m.functions[0].blocks[-1]
        n0 = len(bb.instructions)
        ins = e.e.dma_start(out=out_ap, in_=in_ap)
        n1 = len(bb.instructions)
        pieces = sum(1 for i in range(n0, n1)
                     if "DMACopy" in type(bb.instructions[i]).__name__)
        assert pieces == 1, (stream, n0, n1)
        ins.then_inc(d.sem, 16)
        d.n += 16
        t = (d.name, d.n)
        self._rec(t, reads, writes, waw)
        return t

    def collective(self, stream, kind, alu, in_ap, out_ap, reads=(), writes=()):
        e = self.eng["pool"]
        cs = self.ccstream(stream)
        self._waits("pool", self._dep(reads, writes))
        ins = e.e.collective_compute(
            kind, alu, replica_groups=[list(range(NCORE))],
            ins=[in_ap], outs=[out_ap])
        cs.n += 1
        ins.then_inc(cs.sem)
        t = (cs.name, cs.n)
        self._rec(t, reads, writes)
        return t

    def snap(self):
        toks = [("s_" + n, s.n) for n, s in self.eng.items()]
        toks += [(d.name, d.n) for d in self.dmas.values()]
        toks += [(c.name, c.n) for c in self.ccs.values()]
        return toks


def _build(coop=COOP):
    nc = bass.Bass("TRN2", target_bir_lowering=False)

    x_in = nc.dram_tensor("x_in", [SEQ, D, T], F32, kind="ExternalInput").ap()
    y_out = nc.dram_tensor("y_out", [SEQ, D, T], F32, kind="ExternalOutput").ap()
    npc = 4 if coop else 32
    my_pc = nc.dram_tensor("my_pc", [npc * 128, NDC * 256], F32,
                           kind="ExternalInput").ap()
    nw1 = UPC_W1 if coop else W1_UNITS
    nw3 = UPC_W3 if coop else W3_UNITS
    my_w1 = nc.dram_tensor("my_w1", [nw1 * 128, D], BF16, kind="ExternalInput").ap()
    my_w2 = nc.dram_tensor("my_w2", [nw1 * 128, D], BF16, kind="ExternalInput").ap()
    my_w3 = nc.dram_tensor("my_w3", [nw3 * 128, DFF], BF16, kind="ExternalInput").ap()
    msk1 = nc.dram_tensor("msk1", [128, nw1], F32, kind="ExternalInput").ap()
    gamma_in = nc.dram_tensor("in_gamma_fm", [128, NDC], F32, kind="ExternalInput").ap()
    ident_in = nc.dram_tensor("in_ident_f32", [128, 128], F32, kind="ExternalInput").ap()
    onesc_bf_in = nc.dram_tensor("in_onesc_bf", [128, 1], BF16, kind="ExternalInput").ap()
    onesc_f_in = nc.dram_tensor("in_onesc_f", [128, 1], F32, kind="ExternalInput").ap()
    onesr_f_in = nc.dram_tensor("in_onesr_f", [128, 128], F32, kind="ExternalInput").ap()
    taus_in = nc.dram_tensor("in_taus", [128, 16], F32, kind="ExternalInput").ap()

    blk_all = nc.dram_tensor("blk_all", [32 * 128, NDC * 256], BF16).ap()
    KB = 8
    xb_dram = nc.dram_tensor("xb_dram", [KB * 128, NDC * C], F32).ap()
    if coop:
        agi_pc = nc.dram_tensor("agi_pc", [4 * 128, NDC * 256], BF16).ap()
    else:
        agi_pc = None
    sc_d = {"w1": nc.dram_tensor("sc_w1", [nw1, 128 * NCB_W1], F32).ap(),
            "w2": nc.dram_tensor("sc_w2", [nw1, 128 * NCB_W1], F32).ap(),
            "w3": nc.dram_tensor("sc_w3", [nw3, 128 * NCB_W3], F32).ap()}
    deq_d = {"w1": nc.dram_tensor("deq_w1", [nw1, 128 * NCB_W1], F32).ap(),
             "w2": nc.dram_tensor("deq_w2", [nw1, 128 * NCB_W1], F32).ap(),
             "w3": nc.dram_tensor("deq_w3", [nw3, 128 * NCB_W3], F32).ap()}
    if coop:
        red_in = nc.dram_tensor("red_in", [1, 4], F32).ap()
        red_out = nc.dram_tensor("red_out", [1, 4], F32).ap()
        agi = {"w1": nc.dram_tensor("agi_w1", [UPC_W1 * 128, D], BF16).ap(),
               "w2": nc.dram_tensor("agi_w2", [UPC_W1 * 128, D], BF16).ap(),
               "w3": nc.dram_tensor("agi_w3", [UPC_W3 * 128, DFF], BF16).ap()}
        wT = {"w1": nc.dram_tensor("ago_w1", [NCORE * UPC_W1 * 128, D], BF16).ap(),
              "w2": nc.dram_tensor("ago_w2", [NCORE * UPC_W1 * 128, D], BF16).ap(),
              "w3": nc.dram_tensor("ago_w3", [NCORE * UPC_W3 * 128, DFF], BF16).ap()}
    else:
        agi = None
        red_in = red_out = None
        wT = {"w1": nc.dram_tensor("w1T", [W1_UNITS * 128, D], BF16).ap(),
              "w2": nc.dram_tensor("w2T", [W1_UNITS * 128, D], BF16).ap(),
              "w3": nc.dram_tensor("w3T", [W3_UNITS * 128, DFF], BF16).ap()}

    stack = contextlib.ExitStack()
    with stack:
        k = K(nc, stack)
        sbuf = lambda name, shape, dt: stack.enter_context(nc.sbuf_tensor(name, shape, dt))
        psum = lambda name, shape, dt: stack.enter_context(nc.psum_tensor(name, shape, dt))

        X = sbuf("X", [128, NDC * C], F32)
        XN = sbuf("XN", [128, NDC * C], BF16)
        O = sbuf("O", [128, NDC * C], F32)
        O2 = sbuf("O2", [128, NDC * C], BF16)
        H = sbuf("H", [128, NFC * C], BF16)
        wsbig = sbuf("wsbig", [128, 4 * NDC * 256], BF16)
        ws = [wsbig[:, i * NDC * 256:(i + 1) * NDC * 256] for i in range(4)]
        w3sv = [wsbig[:, 0:NFC * 128],
                wsbig[:, 2 * NDC * 256:2 * NDC * 256 + NFC * 128]]
        w3s = w3sv[0]
        gt = [sbuf(f"gt{i}", [128, C], F32) for i in range(5)]
        xbt = [sbuf(f"xbt{i}", [128, C], BF16) for i in range(2)]
        x2t = [sbuf(f"x2t{i}", [128, C], BF16) for i in range(2)]
        mbc = sbuf("mbc", [128, C], F32)
        rbc = sbuf("rbc", [128, C], F32)
        smalls = sbuf("smalls", [128, C], F32)
        rows2 = sbuf("rows2", [128, 2 * C], F32)
        carry = sbuf("carry", [128, NDC], F32)
        gamma = sbuf("gamma", [128, NDC], F32)
        ident = sbuf("ident", [128, 128], F32)
        onesc_bf = sbuf("onesc_bf", [128, 1], BF16)
        onesc_f = sbuf("onesc_f", [128, 1], F32)
        onesr_f = sbuf("onesr_f", [128, 128], F32)
        mstage = sbuf("mstage", [1, 4], F32)
        meanbf = sbuf("meanbf", [1, 4], BF16)
        meanf = sbuf("meanf", [1, 4], F32)
        mskc = sbuf("mskc", [128, nw1], F32)
        tausb = sbuf("tausb", [128, 16], F32)

        QW3 = NCB_W3 // NQ * BLK
        hof = [0]

        def hview(cols_bf16, dtype=BF16):
            a = hof[0]
            hof[0] += cols_bf16
            assert hof[0] <= NFC * C, "H scratch overflow"
            v = H[:, a:a + cols_bf16]
            return v.bitcast(dtype) if dtype != BF16 else v

        raw = hview(QW3)
        scaled = hview(QW3)
        uout = hview(QW3)
        acc = hview(2 * QW3, F32)
        accp = hview(2 * QW3, F32)
        tmpd = hview(2 * QW3, F32)
        tmpp = hview(2 * QW3, F32)
        val = hview(2 * QW3, F32)
        scl = hview(2 * NCB_W3, F32)
        rsc = hview(2 * NCB_W3, F32)
        dqc = hview(2 * NCB_W3, F32)
        svec = hview(2, F32)

        def hviewp(cols_bf16, dtype, parts):
            a = hof[0]
            hof[0] += cols_bf16
            assert hof[0] <= NFC * C, "H scratch overflow"
            v = H[0:parts, a:a + cols_bf16]
            return v.bitcast(dtype) if dtype != BF16 else v

        pc_in = [hviewp(512, F32, 128) for _ in range(2)]
        pc_out = [hviewp(256, BF16, 128) for _ in range(2)]
        scg = hviewp(512, F32, G_W3)
        ctg = hviewp(256, BF16, G_W3)
        qg = hviewp(512, F32, G_W3)
        dqg_bf = hviewp(256, BF16, G_W3)
        dqg = hviewp(512, F32, G_W3)
        gcol = hviewp(16, F32, G_W3)
        gbf = hviewp(4, BF16, G_W3)

        pbt = [psum(f"pb{i}", [128, C], F32) for i in range(8)]
        pb = pbt

        BX = [Buf(f"X{i}") for i in range(NDC)]
        BXN = [Buf(f"XN{i}") for i in range(NDC)]
        BO = [Buf(f"O{i}") for i in range(NDC)]
        BO2 = [Buf(f"O2{i}") for i in range(NDC)]
        BH = [Buf(f"H{i}") for i in range(NFC)]
        Bw3sv = [Buf("w3sA"), Buf("w3sB")]
        Bw3s = Bw3sv[0]
        Bws = [Buf(f"ws{i}") for i in range(4)]
        Bgt = [Buf(f"gt{i}") for i in range(5)]
        Bxbt = [Buf(f"xbt{i}") for i in range(2)]
        Bx2t = [Buf(f"x2t{i}") for i in range(2)]
        Bmbc, Brbc, Brows = Buf("mbc"), Buf("rbc"), Buf("rows")
        Bcar = [Buf(f"car{i}") for i in range(NDC)]
        Bconst = Buf("const")
        Bmean = Buf("mean")
        Bpb = [Buf(f"pb{i}") for i in range(8)]
        Bd = {nm: Buf("d_" + nm) for nm in
              ("raw", "scl", "rsc", "dqc", "scaled", "acc", "accp", "tmpd",
               "tmpp", "val", "uout", "scg", "ctg", "qg", "dqg_bf", "dqg",
               "gcol", "gbf", "svec")}
        Bpc = [Buf(f"pc{i}") for i in range(2)]
        Bblk = Buf("blk_all")
        Bagi_pc = Buf("agi_pc")
        BscD = {w: Buf("scD_" + w) for w in ("w1", "w2", "w3")}
        BdeqD = {w: Buf("deqD_" + w) for w in ("w1", "w2", "w3")}
        BwT = {w: Buf("wT_" + w) for w in ("w1", "w2", "w3")}
        Bagi = {w: Buf("agi_" + w) for w in ("w1", "w2", "w3")}
        Bred = Buf("red")
        Bx_dram = Buf("x_dram")
        Bxb = [Buf(f"xb{i}") for i in range(8)]
        By_dram = Buf("y_dram")

        with nc.Block() as block:
            @block.sync
            def _(sync):
                k.dmav("c_const", "sync", gamma[:], gamma_in, writes=[Bconst], waw=False)
                k.dmav("c_const", "sync", ident[:], ident_in, writes=[Bconst], waw=False)
                k.dmav("c_const", "sync", onesc_bf[:], onesc_bf_in, writes=[Bconst], waw=False)
                k.dmav("c_const", "sync", onesc_f[:], onesc_f_in, writes=[Bconst], waw=False)
                k.dmav("c_const", "sync", onesr_f[:], onesr_f_in, writes=[Bconst], waw=False)
                k.dmav("c_const", "sync", mskc[:], msk1, writes=[Bconst], waw=False)
                k.dmav("c_const", "sync", tausb[:], taus_in, writes=[Bconst], waw=False)

                pc_dst = agi_pc if coop else blk_all
                pc_dstb = Bagi_pc if coop else Bblk
                pi = 0
                for bl in range(npc):
                    for dc in range(NDC):
                        bq = pi % 2
                        k.dmav(f"c_pc{bq}", "sync", pc_in[bq],
                               my_pc[bl * 128:(bl + 1) * 128,
                                     dc * 256:(dc + 1) * 256],
                               writes=[Bpc[bq]])
                        k.op("act", lambda e, bq=bq: e.copy(pc_out[bq], pc_in[bq]),
                             reads=[Bpc[bq]], writes=[Bpc[bq]])
                        k.dmav(f"c_out{bq}", "sync",
                               pc_dst[bl * 128:(bl + 1) * 128,
                                      dc * 256:(dc + 1) * 256],
                               pc_out[bq],
                               reads=[Bpc[bq]], writes=[pc_dstb], waw=False)
                        pi += 1
                if coop:
                    k.collective("cc_pc", "AllGather", ALU.bypass,
                                 agi_pc.opt(), blk_all.opt(),
                                 reads=[Bagi_pc], writes=[Bblk])

                wcfg = {
                    "w1": dict(src=my_w1, units=nw1, cols=D, ncb=NCB_W1,
                               G=G_W1, nb=NB_W1, ntile=NDC, masked=True),
                    "w2": dict(src=my_w2, units=nw1, cols=D, ncb=NCB_W1,
                               G=G_W1, nb=NB_W1, ntile=NDC, masked=True),
                    "w3": dict(src=my_w3, units=nw3, cols=DFF, ncb=NCB_W3,
                               G=G_W3, nb=NB_W3, ntile=NFC, masked=False),
                }

                def dq_gen():
                    k.op("dve", lambda e: e.memset(mstage[:], 0.0), writes=[Bmean])
                    for wi_, (w, cfg) in enumerate(wcfg.items()):
                        src_, units, cols, ncb = (cfg["src"], cfg["units"],
                                                  cfg["cols"], cfg["ncb"])
                        for u in range(units):
                            qc = cols // NQ
                            qcb = ncb // NQ
                            for h in range(NQ):
                                k.dmav("c_raw", "pool", raw[:, 0:qc],
                                       src_[u * 128:(u + 1) * 128,
                                            h * qc:(h + 1) * qc],
                                       writes=[Bd["raw"]])
                                k.op("dve", lambda e, h=h, qcb=qcb, qc=qc: e.tensor_reduce(
                                        scl[:, h * qcb:(h + 1) * qcb].rearrange(
                                            "p (a o) -> p a o", o=1),
                                        raw[:, 0:qc].rearrange("p (a b) -> p a b", b=BLK),
                                        AX.X, ALU.max, apply_absolute_value=True),
                                     reads=[Bd["raw"]], writes=[Bd["scl"]])
                            k.dmav("c_scout", "pool",
                                   sc_d[w][u].rearrange("(p j) -> p j", j=ncb),
                                   scl[:, 0:ncb],
                                   reads=[Bd["scl"]], writes=[BscD[w]], waw=False)
                            k.op("dve", lambda e, ncb=ncb: e.reduce_sum(
                                    svec.rearrange("p (a o) -> p a o", o=1),
                                    scl[:, 0:ncb].rearrange("p (a b) -> p a b", b=ncb),
                                    AX.X),
                                 reads=[Bd["scl"]], writes=[Bd["svec"]])
                            lhs_ap = (mskc[:, u:u + 1] if cfg["masked"]
                                      else onesc_f[:, 0:1])
                            k.op("pe", lambda e, lhs_ap=lhs_ap: e.matmul(
                                    pb[5][0:1, 0:1], lhs_ap, svec,
                                    start=True, stop=True),
                                 reads=[Bd["svec"], Bconst], writes=[Bpb[5]])
                            k.op("dve", lambda e, wi_=wi_: e.tensor_tensor(
                                    mstage[0:1, wi_:wi_ + 1],
                                    mstage[0:1, wi_:wi_ + 1],
                                    pb[5][0:1, 0:1], ALU.add),
                                 reads=[Bpb[5], Bmean], writes=[Bmean])
                            yield

                    if coop:
                        k.dmav("p_red", "pool", red_in, mstage[:],
                               reads=[Bmean], writes=[Bred])
                        k.collective("cc_red", "AllReduce", ALU.add, red_in.opt(),
                                     red_out.opt(), reads=[Bred], writes=[Bred])
                        k.dmav("c_mst", "pool", mstage[:], red_out,
                               reads=[Bred], writes=[Bmean])
                    k.op("dve", lambda e: e.memset(meanbf[:], 0.0), writes=[Bmean])
                    for wi_, w in enumerate(("w1", "w2", "w3")):
                        k.op("dve", lambda e, wi_=wi_, w=w: e.tensor_scalar(
                                meanbf[0:1, wi_:wi_ + 1], mstage[0:1, wi_:wi_ + 1],
                                1.0 / wcfg[w]["nb"], None, ALU.mult),
                             reads=[Bmean], writes=[Bmean])
                    k.op("dve", lambda e: e.tensor_copy(meanf[:], meanbf[:]),
                         reads=[Bmean], writes=[Bmean])
                    yield

                    for wi_, (w, cfg) in enumerate(wcfg.items()):
                        units, ncb, G = cfg["units"], cfg["ncb"], cfg["G"]
                        for u in range(units):
                            k.dmav("c_scg", "pool", scg[0:G, :],
                                   sc_d[w][u].rearrange("(g j) -> g j", j=256),
                                   reads=[BscD[w]], writes=[Bd["scg"]])
                            k.op("pe", lambda e, G=G, wi_=wi_: e.matmul(
                                    pb[7][0:G, 0:1], onesr_f[0:1, 0:G],
                                    meanf[0:1, wi_:wi_ + 1], start=True, stop=True),
                                 reads=[Bmean, Bconst], writes=[Bpb[7]])
                            k.op("dve", lambda e, G=G: e.tensor_copy(
                                    gcol[0:G, 3:4], pb[7][0:G, 0:1]),
                                 reads=[Bpb[7]], writes=[Bd["gcol"]])
                            k.op("dve", lambda e, G=G: e.tensor_scalar(
                                    ctg[0:G, :], scg[0:G, :], gcol[0:G, 3:4], None,
                                    ALU.subtract),
                                 reads=[Bd["scg"], Bd["gcol"]], writes=[Bd["ctg"]])
                            k.op("dve", lambda e, G=G: e.tensor_reduce(
                                    gcol[0:G, 0:1].rearrange("p (a o) -> p a o", o=1),
                                    ctg[0:G, :].rearrange("p (a b) -> p a b", b=256),
                                    AX.X, ALU.max, apply_absolute_value=True),
                                 reads=[Bd["ctg"]], writes=[Bd["gcol"]])
                            k.op("dve", lambda e, G=G: e.reciprocal(
                                    gcol[0:G, 4:5], gcol[0:G, 0:1]),
                                 reads=[Bd["gcol"]], writes=[Bd["gcol"]])
                            k.op("dve", lambda e, G=G: e.tensor_scalar(
                                    gbf[0:G, 0:1], gcol[0:G, 4:5], 128.0, None,
                                    ALU.mult),
                                 reads=[Bd["gcol"]], writes=[Bd["gbf"]])
                            k.op("dve", lambda e, G=G: e.tensor_copy(
                                    gcol[0:G, 1:2], gbf[0:G, 0:1]),
                                 reads=[Bd["gbf"]], writes=[Bd["gcol"]])
                            k.op("dve", lambda e, G=G: e.reciprocal(
                                    gcol[0:G, 2:3], gcol[0:G, 1:2]),
                                 reads=[Bd["gcol"]], writes=[Bd["gcol"]])
                            k.op("dve", lambda e, G=G: e.tensor_scalar(
                                    ctg[0:G, :], ctg[0:G, :], gcol[0:G, 1:2], None,
                                    ALU.mult),
                                 reads=[Bd["gcol"], Bd["ctg"]], writes=[Bd["ctg"]])
                            k.op("dve", lambda e, G=G: e.tensor_scalar(
                                    qg[0:G, :], ctg[0:G, :], MAGIC, -MAGIC,
                                    ALU.add, ALU.add),
                                 reads=[Bd["ctg"]], writes=[Bd["qg"]])
                            k.op("dve", lambda e, G=G: e.tensor_scalar(
                                    qg[0:G, :], qg[0:G, :], -128.0, 127.0,
                                    ALU.max, ALU.min),
                                 reads=[Bd["qg"]], writes=[Bd["qg"]])
                            k.op("dve", lambda e, G=G: e.tensor_scalar(
                                    dqg_bf[0:G, :], qg[0:G, :], gcol[0:G, 2:3], None,
                                    ALU.mult),
                                 reads=[Bd["qg"], Bd["gcol"]], writes=[Bd["dqg_bf"]])
                            k.op("dve", lambda e, G=G: e.tensor_scalar(
                                    dqg_bf[0:G, :], dqg_bf[0:G, :], gcol[0:G, 3:4],
                                    None, ALU.add),
                                 reads=[Bd["dqg_bf"], Bd["gcol"]], writes=[Bd["dqg_bf"]])
                            k.op("dve", lambda e, G=G: e.tensor_copy(
                                    dqg[0:G, :], dqg_bf[0:G, :]),
                                 reads=[Bd["dqg_bf"]], writes=[Bd["dqg"]])
                            k.dmav("c_dqout", "pool",
                                   deq_d[w][u].rearrange("(g j) -> g j", j=256),
                                   dqg[0:G, :],
                                   reads=[Bd["dqg"]], writes=[BdeqD[w]], waw=False)
                            yield

                    for w, cfg in wcfg.items():
                        src_, units, cols, ncb = (cfg["src"], cfg["units"],
                                                  cfg["cols"], cfg["ncb"])
                        ntile = cfg["ntile"]
                        for u in range(units):
                            k.dmav("c_scl", "pool", scl[:, 0:ncb],
                                   sc_d[w][u].rearrange("(p j) -> p j", j=ncb),
                                   reads=[BscD[w]], writes=[Bd["scl"]])
                            k.op("dve", lambda e, ncb=ncb: e.reciprocal(
                                    rsc[:, 0:ncb], scl[:, 0:ncb]),
                                 reads=[Bd["scl"]], writes=[Bd["rsc"]])
                            k.dmav("c_dqc", "pool", dqc[:, 0:ncb],
                                   deq_d[w][u].rearrange("(p j) -> p j", j=ncb),
                                   reads=[BdeqD[w]], writes=[Bd["dqc"]])
                            qc = cols // NQ
                            qcb = ncb // NQ
                            qt = ntile // NQ
                            for h in range(NQ):
                                k.dmav("c_raw", "pool", raw[:, 0:qc],
                                       src_[u * 128:(u + 1) * 128,
                                            h * qc:(h + 1) * qc],
                                       writes=[Bd["raw"]])
                                k.op("dve", lambda e, h=h, qc=qc, qcb=qcb: e.tensor_tensor(
                                        scaled[:, 0:qc].rearrange("p (a b) -> p a b", b=BLK),
                                        raw[:, 0:qc].rearrange("p (a b) -> p a b", b=BLK),
                                        rsc[:, h * qcb:(h + 1) * qcb].rearrange(
                                            "p (a o) -> p a o", o=1).broadcast_to(
                                            [128, qcb, BLK]),
                                        ALU.mult),
                                     reads=[Bd["raw"], Bd["rsc"]], writes=[Bd["scaled"]])
                                k.op("dve", lambda e, qc=qc: e.memset(acc[:, 0:qc], ACC0),
                                     writes=[Bd["acc"]])
                                k.op("pool", lambda e, qc=qc: e.memset(accp[:, 0:qc], 0.0),
                                     writes=[Bd["accp"]])
                                for kk in range(15):
                                    sg = tmpd if (kk % 2 == 0) else tmpp
                                    bsg = Bd["tmpd"] if (kk % 2 == 0) else Bd["tmpp"]
                                    k.op("act", lambda e, kk=kk, qc=qc, sg=sg: e.activation(
                                            sg[:, 0:qc], scaled[:, 0:qc], AF.Sign,
                                            bias=tausb[:, kk:kk + 1], scale=1.0),
                                         reads=[Bd["scaled"], Bconst], writes=[bsg])
                                    if kk in K_POOL:
                                        k.op("dve", lambda e, kk=kk, qc=qc, sg=sg: e.tensor_scalar(
                                                val[:, 0:qc], sg[:, 0:qc],
                                                HDELTA[kk], None, ALU.mult),
                                             reads=[bsg], writes=[Bd["val"]])
                                        k.op("pool", lambda e, qc=qc: e.tensor_tensor(
                                                accp[:, 0:qc], accp[:, 0:qc],
                                                val[:, 0:qc], ALU.add),
                                             reads=[Bd["val"], Bd["accp"]],
                                             writes=[Bd["accp"]])
                                    else:
                                        k.op("dve", lambda e, kk=kk, qc=qc, sg=sg: e.scalar_tensor_tensor(
                                                acc[:, 0:qc], sg[:, 0:qc],
                                                HDELTA[kk], acc[:, 0:qc],
                                                ALU.mult, ALU.add),
                                             reads=[bsg, Bd["acc"]], writes=[Bd["acc"]])
                                k.op("dve", lambda e, qc=qc: e.tensor_tensor(
                                        acc[:, 0:qc], acc[:, 0:qc], accp[:, 0:qc],
                                        ALU.add),
                                     reads=[Bd["accp"], Bd["acc"]], writes=[Bd["acc"]])
                                k.op("dve", lambda e, h=h, qc=qc, qcb=qcb: e.tensor_tensor(
                                        val[:, 0:qc].rearrange("p (a b) -> p a b", b=BLK),
                                        acc[:, 0:qc].rearrange("p (a b) -> p a b", b=BLK),
                                        dqc[:, h * qcb:(h + 1) * qcb].rearrange(
                                            "p (a o) -> p a o", o=1).broadcast_to(
                                            [128, qcb, BLK]),
                                        ALU.mult),
                                     reads=[Bd["acc"], Bd["dqc"]], writes=[Bd["val"]])
                                for tq in range(0, qt, 4):
                                    ntq = min(4, qt - tq)
                                    fs = []
                                    for tt in range(ntq):
                                        fs.append(lambda e, t_=tq + tt, tt=tt: e.transpose(
                                            pb[7][:, tt * 128:(tt + 1) * 128],
                                            val[:, t_ * 128:(t_ + 1) * 128],
                                            ident[:]))
                                    k.group("pe", fs, reads=[Bd["val"], Bconst],
                                            writes=[Bpb[7]])
                                    k.op("dve", lambda e, tq=tq, ntq=ntq: e.tensor_copy(
                                            uout[:, tq * 128:tq * 128 + ntq * 128],
                                            pb[7][:, 0:ntq * 128]),
                                         reads=[Bpb[7]], writes=[Bd["uout"]])
                                dst = agi[w] if coop else wT[w]
                                dstb = Bagi[w] if coop else BwT[w]
                                k.dmav("c_uout", "pool",
                                       dst[u * 128:(u + 1) * 128,
                                           h * qt * 128:(h + 1) * qt * 128],
                                       uout[:, 0:qt * 128],
                                       reads=[Bd["uout"]], writes=[dstb], waw=False)
                                yield
                        if coop:
                            k.collective("cc_" + w, "AllGather", ALU.bypass,
                                         agi[w].opt(), wT[w].opt(),
                                         reads=[Bagi[w]], writes=[BwT[w]])

                    snap = k.snap()
                    for b_ in BH + Bw3sv + Bws:
                        b_.rs = b_.rs + list(snap)
                    yield

                dqit = dq_gen()

                def pump(n=1):
                    for _ in range(n):
                        if next(dqit, "DONE") == "DONE":
                            return

                def ln_head(src_bufs, src_sb, dc):
                    bq = dc % 2
                    sl = src_sb[:, dc * C:(dc + 1) * C]
                    k.op("dve", lambda e, sl=sl, bq=bq: e.tensor_copy(
                            xbt[bq][:], sl),
                         reads=[src_bufs[dc]], writes=[Bxbt[bq]])
                    k.op("act", lambda e, sl=sl, bq=bq: e.activation(
                            x2t[bq][:], sl, AF.Square, bias=0.0, scale=1.0),
                         reads=[src_bufs[dc]], writes=[Bx2t[bq]])
                    k.group("pe", [
                        lambda e, bq=bq, dc=dc: e.matmul(
                            pb[6][0:1, :], onesc_bf[:], xbt[bq][:],
                            start=(dc == 0), stop=(dc == NDC - 1)),
                        lambda e, bq=bq, dc=dc: e.matmul(
                            pb[7][0:1, :], onesc_bf[:], x2t[bq][:],
                            start=(dc == 0), stop=(dc == NDC - 1)),
                    ], reads=[Bxbt[bq], Bx2t[bq], Bconst],
                       writes=[Bpb[6], Bpb[7]])

                def emit_ln(src_bufs, src_sb, dst_bufs, dst_sb, skip_head=False):
                    mrow = rows2[0:1, 0:C]
                    vrow = rows2[0:1, C:2 * C]
                    trow = smalls[0:1, :]
                    rrow = vrow
                    if not skip_head:
                        for dc in range(NDC):
                            ln_head(src_bufs, src_sb, dc)
                    k.op("dve", lambda e: e.tensor_scalar(
                            mrow, pb[6][0:1, :], 1.0 / D, None, ALU.mult),
                         reads=[Bpb[6]], writes=[Brows])
                    k.op("dve", lambda e: e.tensor_scalar(
                            vrow, pb[7][0:1, :], 1.0 / D, None, ALU.mult),
                         reads=[Bpb[7]], writes=[Brows])
                    k.op("dve", lambda e: e.tensor_tensor(
                            trow, mrow, mrow, ALU.mult),
                         reads=[Brows], writes=[Brows])
                    k.op("dve", lambda e: e.tensor_scalar(
                            trow, trow, -1.0, EPS, ALU.mult, ALU.add),
                         reads=[Brows], writes=[Brows])
                    k.op("dve", lambda e: e.tensor_tensor(
                            vrow, vrow, trow, ALU.add),
                         reads=[Brows], writes=[Brows])
                    k.op("act", lambda e: e.activation(
                            trow, vrow, AF.Sqrt, bias=0.0, scale=1.0),
                         reads=[Brows], writes=[Brows])
                    k.op("dve", lambda e: e.reciprocal(rrow, trow),
                         reads=[Brows], writes=[Brows])
                    k.op("pe", lambda e: e.matmul(
                            pb[6][:], onesr_f[0:1, :], mrow, start=True, stop=True),
                         reads=[Brows, Bconst], writes=[Bpb[6]])
                    k.op("pe", lambda e: e.matmul(
                            pb[7][:], onesr_f[0:1, :], rrow, start=True, stop=True),
                         reads=[Brows, Bconst], writes=[Bpb[7]])
                    k.op("dve", lambda e: e.tensor_copy(mbc[:], pb[6][:]),
                         reads=[Bpb[6]], writes=[Bmbc])
                    k.op("dve", lambda e: e.tensor_copy(rbc[:], pb[7][:]),
                         reads=[Bpb[7]], writes=[Brbc])
                    for dc in range(NDC):
                        en = "pool" if dc % 3 == 2 else "dve"
                        bq = dc % 2 + (2 if en == "pool" else 0)
                        sl = src_sb[:, dc * C:(dc + 1) * C]
                        dl = dst_sb[:, dc * C:(dc + 1) * C]
                        k.op(en, lambda e, sl=sl, bq=bq: e.tensor_tensor(
                                gt[bq][:], sl, mbc[:], ALU.subtract),
                             reads=[src_bufs[dc], Bmbc], writes=[Bgt[bq]])
                        k.op(en, lambda e, dl=dl, bq=bq: e.tensor_tensor(
                                dl, gt[bq][:], rbc[:], ALU.mult),
                             reads=[Bgt[bq], Brbc], writes=[dst_bufs[dc]])

                WNUM = {"wi": 0, "wf": 1, "wg": 2, "wo": 3}

                def load_wave(nm, wv, wsl):
                    b = WNUM[nm] * 8 + wv
                    return k.dmav(f"c_ws{wsl}", "sync", ws[wsl],
                                  blk_all[b * 128:(b + 1) * 128, :],
                                  reads=[Bblk], writes=[Bws[wsl]])

                def emit_attn(ch, bounce):
                    seq = ch // CH_PER_SEQ
                    t0 = (ch % CH_PER_SEQ) * C
                    first = (ch % CH_PER_SEQ) == 0
                    pump_here = ch < KB

                    for dc in range(NDC):
                        k.dmav(f"c_x{dc}", "sync",
                               X[:, dc * C:(dc + 1) * C],
                               x_in[seq, dc * 128:(dc + 1) * 128, t0:t0 + C],
                               reads=[Bx_dram], writes=[BX[dc]])

                    emit_ln(BX, X, BXN, XN)

                    for wv in range(8):
                        if pump_here:
                            pump(1)
                        pf0 = (wv % 2) * 4
                        pi0 = pf0 + 2
                        wf_s = (wv % 2) * 2
                        wi_s = wf_s + 1
                        load_wave("wf", wv, wf_s)
                        load_wave("wi", wv, wi_s)
                        for wsl, pq in ((wf_s, pf0), (wi_s, pi0)):
                            fs = []
                            for dc in range(NDC):
                                for ecl in range(2):
                                    fs.append(lambda e, dc=dc, ecl=ecl, wsl=wsl,
                                              pq=pq: e.matmul(
                                        pb[pq + ecl][:],
                                        ws[wsl][:, dc * 256 + ecl * 128:
                                                dc * 256 + (ecl + 1) * 128],
                                        XN[:, dc * C:(dc + 1) * C],
                                        start=(dc == 0), stop=(dc == NDC - 1)))
                            k.group("pe", fs, reads=[Bws[wsl]] + BXN,
                                    writes=[Bpb[pq], Bpb[pq + 1]])
                        for ecl in range(2):
                            ec = 2 * wv + ecl
                            pf = pf0 + ecl
                            pi_ = pi0 + ecl
                            ta, tnf, tsi = gt[2 * ecl], gt[4], gt[2 * ecl + 1]
                            ga, gnf, gsi = Bgt[2 * ecl], Bgt[4], Bgt[2 * ecl + 1]
                            k.op("act", lambda e, pf=pf, ta=ta: e.activation(
                                    ta[:], pb[pf][:], AF.Sigmoid, bias=0.0, scale=1.0),
                                 reads=[Bpb[pf]], writes=[ga])
                            k.op("act", lambda e, pf=pf, tnf=tnf: e.activation(
                                    tnf[:], pb[pf][:], AF.Sigmoid, bias=0.0, scale=-1.0),
                                 reads=[Bpb[pf]], writes=[gnf])
                            k.op("act", lambda e, pi_=pi_, tsi=tsi: e.activation(
                                    tsi[:], pb[pi_][:], AF.Silu, bias=0.0, scale=1.0),
                                 reads=[Bpb[pi_]], writes=[gsi])
                            k.op("dve", lambda e, tsi=tsi, tnf=tnf: e.tensor_tensor(
                                    tsi[:], tsi[:], tnf[:], ALU.mult),
                                 reads=[gnf, gsi], writes=[gsi])
                            init = 0.0 if first else carry[:, ec:ec + 1]
                            rd = [ga, gsi] + ([] if first else [Bcar[ec]])
                            k.op("dve", lambda e, ec=ec, ta=ta, tsi=tsi, init=init:
                                    e.tensor_tensor_scan(
                                        O[:, ec * C:(ec + 1) * C], ta[:], tsi[:],
                                        init, ALU.mult, ALU.add),
                                 reads=rd, writes=[BO[ec]])
                            k.op("dve", lambda e, ec=ec: e.tensor_copy(
                                    carry[:, ec:ec + 1],
                                    O[:, ec * C + C - 1:ec * C + C]),
                                 reads=[BO[ec]], writes=[Bcar[ec]])

                    for ec in range(NDC):
                        bq = ec % 2
                        k.op("act", lambda e, ec=ec, bq=bq: e.activation(
                                x2t[bq][:], O[:, ec * C:(ec + 1) * C],
                                AF.Square, bias=0.0, scale=1.0),
                             reads=[BO[ec]], writes=[Bx2t[bq]])
                        k.op("pe", lambda e, ec=ec, bq=bq: e.matmul(
                                pb[6][0:1, :], onesc_bf[:], x2t[bq][:],
                                start=(ec == 0), stop=(ec == NDC - 1)),
                             reads=[Bx2t[bq], Bconst], writes=[Bpb[6]])
                    trow = smalls[0:1, :]
                    rrow = rows2[0:1, C:2 * C]
                    k.op("dve", lambda e: e.tensor_scalar(
                            trow, pb[6][0:1, :], 1.0 / D, EPS, ALU.mult, ALU.add),
                         reads=[Bpb[6]], writes=[Brows])
                    k.op("act", lambda e: e.activation(
                            trow, trow, AF.Sqrt, bias=0.0, scale=1.0),
                         reads=[Brows], writes=[Brows])
                    k.op("dve", lambda e: e.reciprocal(rrow, trow),
                         reads=[Brows], writes=[Brows])
                    k.op("pe", lambda e: e.matmul(
                            pb[7][:], onesr_f[0:1, :], rrow, start=True, stop=True),
                         reads=[Brows, Bconst], writes=[Bpb[7]])
                    k.op("dve", lambda e: e.tensor_copy(rbc[:], pb[7][:]),
                         reads=[Bpb[7]], writes=[Brbc])

                    for wv in range(8):
                        if pump_here:
                            pump(1)
                        pg0 = (wv % 2) * 4
                        wsl = (wv % 2) * 2
                        load_wave("wg", wv, wsl)
                        fs = []
                        for dc in range(NDC):
                            for ecl in range(2):
                                fs.append(lambda e, dc=dc, ecl=ecl, wsl=wsl,
                                          pg0=pg0: e.matmul(
                                    pb[pg0 + ecl][:],
                                    ws[wsl][:, dc * 256 + ecl * 128:
                                            dc * 256 + (ecl + 1) * 128],
                                    XN[:, dc * C:(dc + 1) * C],
                                    start=(dc == 0), stop=(dc == NDC - 1)))
                        k.group("pe", fs, reads=[Bws[wsl]] + BXN,
                                writes=[Bpb[pg0], Bpb[pg0 + 1]])
                        for ecl in range(2):
                            ec = 2 * wv + ecl
                            tsg, ttmp = gt[2 * ecl], gt[2 * ecl + 1]
                            gsg, gtmp = Bgt[2 * ecl], Bgt[2 * ecl + 1]
                            k.op("act", lambda e, p_=pg0 + ecl, tsg=tsg: e.activation(
                                    tsg[:], pb[p_][:], AF.Silu, bias=0.0, scale=1.0),
                                 reads=[Bpb[pg0 + ecl]], writes=[gsg])
                            k.op("dve", lambda e, ec=ec, ttmp=ttmp:
                                    e.tensor_scalar(
                                        ttmp[:], O[:, ec * C:(ec + 1) * C],
                                        gamma[:, ec:ec + 1], None, ALU.mult),
                                 reads=[BO[ec], Bconst], writes=[gtmp])
                            k.op("pool", lambda e, tsg=tsg, ttmp=ttmp:
                                    e.tensor_tensor(
                                        ttmp[:], ttmp[:], tsg[:], ALU.mult),
                                 reads=[gsg, gtmp], writes=[gtmp])
                            k.op("dve", lambda e, ec=ec, ttmp=ttmp: e.tensor_tensor(
                                    O2[:, ec * C:(ec + 1) * C], ttmp[:], rbc[:],
                                    ALU.mult),
                                 reads=[gtmp, Brbc], writes=[BO2[ec]])

                    for wv in range(8):
                        po0 = (wv % 2) * 4
                        wsl = (wv % 2) * 2 + 1
                        load_wave("wo", wv, wsl)
                        fs = []
                        for dc in range(NDC):
                            for ecl in range(2):
                                fs.append(lambda e, dc=dc, ecl=ecl, wsl=wsl,
                                          po0=po0: e.matmul(
                                    pb[po0 + ecl][:],
                                    ws[wsl][:, dc * 256 + ecl * 128:
                                            dc * 256 + (ecl + 1) * 128],
                                    O2[:, dc * C:(dc + 1) * C],
                                    start=(dc == 0), stop=(dc == NDC - 1)))
                        k.group("pe", fs, reads=[Bws[wsl]] + BO2,
                                writes=[Bpb[po0], Bpb[po0 + 1]])
                        for ecl in range(2):
                            ec = 2 * wv + ecl
                            k.op("dve", lambda e, ec=ec, p_=po0 + ecl: e.tensor_tensor(
                                    X[:, ec * C:(ec + 1) * C],
                                    X[:, ec * C:(ec + 1) * C],
                                    pb[p_][:], ALU.add),
                                 reads=[Bpb[po0 + ecl]], writes=[BX[ec]])

                    if bounce:
                        k.dmav("p_xb", "pool",
                               xb_dram[ch * 128:(ch + 1) * 128, :], X[:],
                               reads=BX, writes=[Bxb[ch]])

                def emit_ffn(ch, bounced):
                    seq = ch // CH_PER_SEQ
                    t0 = (ch % CH_PER_SEQ) * C

                    if bounced:
                        for dc in range(NDC):
                            k.dmav(f"c_x{dc}", "sync",
                                   X[:, dc * C:(dc + 1) * C],
                                   xb_dram[ch * 128:(ch + 1) * 128,
                                           dc * C:(dc + 1) * C],
                                   reads=[Bxb[ch]], writes=[BX[dc]])

                    emit_ln(BX, X, BXN, XN)

                    for wv in range(22):
                        pa0 = (wv % 2) * 4
                        pb0_ = pa0 + 2
                        ws1 = (wv % 2) * 2
                        ws2 = ws1 + 1
                        u0 = 2 * wv
                        k.dmav(f"c_ws{ws1}", "sync",
                               ws[ws1].rearrange("p (a f) -> p a f", a=2),
                               wT["w1"][u0 * 128:(u0 + 2) * 128, :].rearrange(
                                   "(a p) f -> p a f", p=128),
                               reads=[BwT["w1"]], writes=[Bws[ws1]])
                        k.dmav(f"c_ws{ws2}", "sync",
                               ws[ws2].rearrange("p (a f) -> p a f", a=2),
                               wT["w2"][u0 * 128:(u0 + 2) * 128, :].rearrange(
                                   "(a p) f -> p a f", p=128),
                               reads=[BwT["w2"]], writes=[Bws[ws2]])
                        for wsl, pq in ((ws1, pa0), (ws2, pb0_)):
                            fs = []
                            for dc in range(NDC):
                                for fcl in range(2):
                                    fs.append(lambda e, dc=dc, fcl=fcl, wsl=wsl,
                                              pq=pq: e.matmul(
                                        pb[pq + fcl][:],
                                        ws[wsl][:, fcl * 2048 + dc * 128:
                                                fcl * 2048 + (dc + 1) * 128],
                                        XN[:, dc * C:(dc + 1) * C],
                                        start=(dc == 0), stop=(dc == NDC - 1)))
                            k.group("pe", fs, reads=[Bws[wsl]] + BXN,
                                    writes=[Bpb[pq], Bpb[pq + 1]])
                        for fcl in range(2):
                            fc = 2 * wv + fcl
                            pa = pa0 + fcl
                            pbb = pb0_ + fcl
                            tsa = gt[2 * fcl]
                            k.op("act", lambda e, pa=pa, tsa=tsa: e.activation(
                                    tsa[:], pb[pa][:], AF.Silu, bias=0.0, scale=1.0),
                                 reads=[Bpb[pa]], writes=[Bgt[2 * fcl]])
                            k.op("dve", lambda e, fc=fc, tsa=tsa, pbb=pbb:
                                    e.tensor_tensor(
                                        H[:, fc * C:(fc + 1) * C], tsa[:],
                                        pb[pbb][:], ALU.mult),
                                 reads=[Bgt[2 * fcl], Bpb[pbb]], writes=[BH[fc]])

                    w3alias = [[Bw3sv[0], Bws[0], Bws[1]],
                               [Bw3sv[1], Bws[2], Bws[3]]]
                    for dc in range(NDC):
                        pv = pb[dc % 2]
                        sl3 = dc % 2
                        k.dmav(f"c_w3s{sl3}", "sync", w3sv[sl3],
                               wT["w3"][dc * 128:(dc + 1) * 128, :],
                               reads=[BwT["w3"]], writes=w3alias[sl3])
                        fs = []
                        for fc in range(NFC):
                            fs.append(lambda e, fc=fc, pv=pv, sl3=sl3: e.matmul(
                                pv[:], w3sv[sl3][:, fc * 128:(fc + 1) * 128],
                                H[:, fc * C:(fc + 1) * C],
                                start=(fc == 0), stop=(fc == NFC - 1)))
                        k.group("pe", fs, reads=w3alias[sl3] + BH,
                                writes=[Bpb[dc % 2]])
                        k.op("dve", lambda e, dc=dc, pv=pv: e.tensor_tensor(
                                X[:, dc * C:(dc + 1) * C],
                                X[:, dc * C:(dc + 1) * C], pv[:], ALU.add),
                             reads=[Bpb[dc % 2]], writes=[BX[dc]])

                    k.dmav("p_y", "pool",
                           y_out[seq, :, t0:t0 + C].rearrange(
                               "(dc p) t -> p dc t", p=128),
                           X[:].rearrange("p (dc t) -> p dc t", t=C),
                           reads=BX, writes=[By_dram], waw=False)

                for ch in range(KB):
                    emit_attn(ch, True)
                pump(10 ** 6)
                for ch in range(KB):
                    emit_ffn(ch, True)
                for ch in range(KB, NCH):
                    emit_attn(ch, False)
                    emit_ffn(ch, False)

                dy = k.dstream("p_y")
                k.eng["pool"].e.wait_ge(dy.sem, dy.n)
                k.eng["sync"].e.wait_ge(dy.sem, dy.n)

    return nc


_NC_CACHE = {}


def _get_nc(coop=COOP):
    if coop not in _NC_CACHE:
        _NC_CACHE[coop] = _build(coop)
    return _NC_CACHE[coop]


def kernel(L0_attn_hidden_states, L0_attn_Wi, L0_attn_Wf, L0_attn_Wg,
           L0_attn_Wo, L0_attn_g_norm_weight, L0_ffn_weight1, L0_ffn_weight2,
           L0_ffn_weight3):
    hs = np.asarray(L0_attn_hidden_states)
    wlist = [np.asarray(w, dtype=np.float32) for w in
             (L0_attn_Wi, L0_attn_Wf, L0_attn_Wg, L0_attn_Wo)]
    blk_imgs = []
    for w in wlist:
        a4 = w.reshape(NDC, 128, 8, 256)
        for wv in range(8):
            blk_imgs.append(np.ascontiguousarray(
                a4[:, :, wv, :].transpose(1, 0, 2).reshape(128, NDC * 256)))
    gnw = np.asarray(L0_attn_g_norm_weight, dtype=np.float32)
    w1 = np.asarray(L0_ffn_weight1)
    w2 = np.asarray(L0_ffn_weight2)
    w3 = np.asarray(L0_ffn_weight3)

    gamma_fm = np.ascontiguousarray(gnw.reshape(NDC, 128).T)
    ident_f32 = np.eye(128, dtype=np.float32)
    onesc_bf = np.ones((128, 1), dtype=ml_dtypes.bfloat16)
    onesc_f = np.ones((128, 1), dtype=np.float32)
    onesr_f = np.ones((128, 128), dtype=np.float32)
    taus_arr = np.zeros((128, 16), dtype=np.float32)
    for kk in range(15):
        taus_arr[:, kk] = np.float32(-TAU[kk])

    nc = _get_nc(COOP)
    in_maps = []
    for c in range(NCORE):
        x_fm = np.ascontiguousarray(
            hs[SEQ * c:SEQ * (c + 1)].transpose(0, 2, 1))
        if COOP:
            my_pc_arr = np.ascontiguousarray(
                np.concatenate(blk_imgs[4 * c:4 * c + 4], axis=0))
            r0 = c * UPC_W1 * 128
            r1 = min(DFF, r0 + UPC_W1 * 128)
            mw1 = np.ones((UPC_W1 * 128, D), dtype=ml_dtypes.bfloat16)
            mw2 = np.ones((UPC_W1 * 128, D), dtype=ml_dtypes.bfloat16)
            if r1 > r0:
                mw1[0:r1 - r0] = w1[r0:r1]
                mw2[0:r1 - r0] = w2[r0:r1]
            msk = np.zeros((128, UPC_W1), dtype=np.float32)
            for ul in range(UPC_W1):
                if (c * UPC_W1 + ul) < W1_UNITS:
                    msk[:, ul] = 1.0
            q0 = c * UPC_W3 * 128
            mw3 = np.ascontiguousarray(w3[q0:q0 + UPC_W3 * 128])
        else:
            my_pc_arr = np.ascontiguousarray(np.concatenate(blk_imgs, axis=0))
            mw1 = np.ascontiguousarray(w1)
            mw2 = np.ascontiguousarray(w2)
            mw3 = np.ascontiguousarray(w3)
            msk = np.ones((128, W1_UNITS), dtype=np.float32)
        in_maps.append({
            "x_in": x_fm,
            "my_pc": my_pc_arr,
            "my_w1": mw1, "my_w2": mw2, "my_w3": mw3, "msk1": msk,
            "in_gamma_fm": gamma_fm, "in_ident_f32": ident_f32,
            "in_onesc_bf": onesc_bf, "in_onesc_f": onesc_f, "in_onesr_f": onesr_f,
            "in_taus": taus_arr,
        })

    res = run_bass_kernel_spmd(nc, in_maps, core_ids=list(range(NCORE)))
    out = np.empty((B, T, D), dtype=np.float32)
    for c in range(NCORE):
        y = np.asarray(res.results[c]["y_out"])
        out[SEQ * c:SEQ * (c + 1)] = y.transpose(0, 2, 1)
    return out


# version 12
# speedup vs baseline: 1.0411x; 1.0411x over previous
import contextlib
import numpy as np
import ml_dtypes
import concourse.bass as bass
import concourse.mybir as mybir
from concourse.bass_utils import run_bass_kernel_spmd

F32 = mybir.dt.float32
BF16 = mybir.dt.bfloat16
AF = mybir.ActivationFunctionType
ALU = mybir.AluOpType
AX = mybir.AxisListType

D = 2048
DFF = 5632
B = 16
T = 2048
NCORE = 8
SEQ = B // NCORE
C = 512
NCH = (SEQ * T) // C
CH_PER_SEQ = T // C
NDC = D // 128
NFC = DFF // 128
EPS = 1e-5
MAGIC = 12582912.0

COOP = True

_NF4_RAW = [-1.0, -0.6962, -0.5251, -0.3949, -0.2844, -0.1848, -0.0911, 0.0,
            0.0796, 0.1609, 0.2461, 0.3379, 0.4407, 0.5626, 0.723, 1.0]
NF4 = np.array(_NF4_RAW, dtype=ml_dtypes.bfloat16).astype(np.float64)
L0 = float(np.float32(NF4[0]))
DELTA = [float(np.float32(NF4[k + 1] - NF4[k])) for k in range(15)]


def _bf16_threshold(mid):
    lo = ml_dtypes.bfloat16(mid)
    if np.float64(lo) > mid:
        up = np.float64(lo)
        dn = np.float64(np.nextafter(lo, ml_dtypes.bfloat16(-np.inf)))
    else:
        dn = np.float64(lo)
        up = np.float64(np.nextafter(lo, ml_dtypes.bfloat16(np.inf)))
    return float(np.float32((dn + up) / 2.0))


TAU = [_bf16_threshold((NF4[k] + NF4[k + 1]) / 2.0) for k in range(15)]

BLK = 64
W1_UNITS = DFF // 128
W3_UNITS = D // 128
NB_W1 = DFF * D // BLK
NB_W3 = D * DFF // BLK
NCB_W1 = D // BLK
NCB_W3 = DFF // BLK
G_W1 = 128 * NCB_W1 // 256
G_W3 = 128 * NCB_W3 // 256
K_POOL = [3, 7, 11, 14]
ACC0 = float(np.float32(np.float32(L0) + np.float32(sum(DELTA) / 2.0)))
HDELTA = [float(np.float32(d / 2.0)) for d in DELTA]
UPC_W1 = 6
UPC_W3 = 2
NQ = 4


class EngS:
    def __init__(self, stack, nc, name, e):
        self.name = name
        self.e = e
        self.sem = stack.enter_context(nc.semaphore("s_" + name))
        self.n = 0
        self.waited = {}


class DmaS:
    def __init__(self, stack, nc, name):
        self.name = name
        self.sem = stack.enter_context(nc.semaphore(name))
        self.n = 0


class Buf:
    __slots__ = ("name", "w", "rs")

    def __init__(self, name):
        self.name = name
        self.w = []
        self.rs = []


class K:
    def __init__(self, nc, stack):
        self.nc = nc
        self.eng = {
            "pe": EngS(stack, nc, "pe", nc.tensor),
            "dve": EngS(stack, nc, "dve", nc.vector),
            "act": EngS(stack, nc, "act", nc.scalar),
            "pool": EngS(stack, nc, "pool", nc.gpsimd),
            "sync": EngS(stack, nc, "sync", nc.sync),
        }
        self.stack = stack
        self.dmas = {}
        self.ccs = {}

    def dstream(self, name):
        if name not in self.dmas:
            self.dmas[name] = DmaS(self.stack, self.nc, name)
        return self.dmas[name]

    def ccstream(self, name):
        if name not in self.ccs:
            self.ccs[name] = DmaS(self.stack, self.nc, name)
        return self.ccs[name]

    def _semh(self, s):
        if s.startswith("s_"):
            return self.eng[s[2:]].sem
        if s in self.dmas:
            return self.dmas[s].sem
        if s in self.ccs:
            return self.ccs[s].sem
        raise KeyError(s)

    def _waits(self, en, toks):
        e = self.eng[en]
        need = {}
        for t in toks:
            if t is None:
                continue
            s, v = t
            if need.get(s, -1) < v:
                need[s] = v
        for s, v in need.items():
            if e.waited.get(s, -1) >= v:
                continue
            e.e.wait_ge(self._semh(s), v)
            e.waited[s] = v

    @staticmethod
    def _dep(reads, writes, waw=True):
        toks = []
        for b in reads:
            toks.extend(b.w)
        for b in writes:
            if waw:
                toks.extend(b.w)
            toks.extend(b.rs)
        return toks

    @staticmethod
    def _rec(t, reads, writes, waw=True):
        for b in writes:
            if waw:
                b.w = [t]
                b.rs = []
            else:
                b.w.append(t)
        for b in reads:
            b.rs.append(t)

    def op(self, en, f, reads=(), writes=()):
        e = self.eng[en]
        self._waits(en, self._dep(reads, writes))
        ins = f(e.e)
        e.n += 1
        ins.then_inc(e.sem, 1)
        t = ("s_" + en, e.n)
        self._rec(t, reads, writes)
        return t

    def group(self, en, fs, reads=(), writes=()):
        e = self.eng[en]
        self._waits(en, self._dep(reads, writes))
        last = None
        for f in fs:
            last = f(e.e)
        e.n += 1
        last.then_inc(e.sem, 1)
        t = ("s_" + en, e.n)
        self._rec(t, reads, writes)
        return t

    def dmav(self, stream, q, out_ap, in_ap, reads=(), writes=(), waw=True):
        e = self.eng[q]
        d = self.dstream(stream)
        self._waits(q, self._dep(reads, writes, waw))
        bb = self.nc.m.functions[0].blocks[-1]
        n0 = len(bb.instructions)
        ins = e.e.dma_start(out=out_ap, in_=in_ap)
        n1 = len(bb.instructions)
        pieces = sum(1 for i in range(n0, n1)
                     if "DMACopy" in type(bb.instructions[i]).__name__)
        assert pieces == 1, (stream, n0, n1)
        ins.then_inc(d.sem, 16)
        d.n += 16
        t = (d.name, d.n)
        self._rec(t, reads, writes, waw)
        return t

    def collective(self, stream, kind, alu, in_ap, out_ap, reads=(), writes=()):
        e = self.eng["pool"]
        cs = self.ccstream(stream)
        self._waits("pool", self._dep(reads, writes))
        ins = e.e.collective_compute(
            kind, alu, replica_groups=[list(range(NCORE))],
            ins=[in_ap], outs=[out_ap])
        cs.n += 1
        ins.then_inc(cs.sem)
        t = (cs.name, cs.n)
        self._rec(t, reads, writes)
        return t

    def snap(self):
        toks = [("s_" + n, s.n) for n, s in self.eng.items()]
        toks += [(d.name, d.n) for d in self.dmas.values()]
        toks += [(c.name, c.n) for c in self.ccs.values()]
        return toks


def _build(coop=COOP):
    nc = bass.Bass("TRN2", target_bir_lowering=False)

    x_in = nc.dram_tensor("x_in", [SEQ, D, T], F32, kind="ExternalInput").ap()
    y_out = nc.dram_tensor("y_out", [SEQ, D, T], F32, kind="ExternalOutput").ap()
    npc = 4 if coop else 32
    my_pc = nc.dram_tensor("my_pc", [npc * 128, NDC * 256], F32,
                           kind="ExternalInput").ap()
    nw1 = UPC_W1 if coop else W1_UNITS
    nw3 = UPC_W3 if coop else W3_UNITS
    my_w1 = nc.dram_tensor("my_w1", [nw1 * 128, D], BF16, kind="ExternalInput").ap()
    my_w2 = nc.dram_tensor("my_w2", [nw1 * 128, D], BF16, kind="ExternalInput").ap()
    my_w3 = nc.dram_tensor("my_w3", [nw3 * 128, DFF], BF16, kind="ExternalInput").ap()
    msk1 = nc.dram_tensor("msk1", [128, nw1], F32, kind="ExternalInput").ap()
    gamma_in = nc.dram_tensor("in_gamma_fm", [128, NDC], F32, kind="ExternalInput").ap()
    ident_in = nc.dram_tensor("in_ident_f32", [128, 128], F32, kind="ExternalInput").ap()
    onesc_bf_in = nc.dram_tensor("in_onesc_bf", [128, 1], BF16, kind="ExternalInput").ap()
    onesc_f_in = nc.dram_tensor("in_onesc_f", [128, 1], F32, kind="ExternalInput").ap()
    onesr_f_in = nc.dram_tensor("in_onesr_f", [128, 128], F32, kind="ExternalInput").ap()
    taus_in = nc.dram_tensor("in_taus", [128, 16], F32, kind="ExternalInput").ap()

    blk_all = nc.dram_tensor("blk_all", [32 * 128, NDC * 256], BF16).ap()
    KB = 8
    xb_dram = nc.dram_tensor("xb_dram", [KB * 128, NDC * C], F32).ap()
    if coop:
        agi_pc = nc.dram_tensor("agi_pc", [4 * 128, NDC * 256], BF16).ap()
    else:
        agi_pc = None
    sc_d = {"w1": nc.dram_tensor("sc_w1", [nw1, 128 * NCB_W1], F32).ap(),
            "w2": nc.dram_tensor("sc_w2", [nw1, 128 * NCB_W1], F32).ap(),
            "w3": nc.dram_tensor("sc_w3", [nw3, 128 * NCB_W3], F32).ap()}
    deq_d = {"w1": nc.dram_tensor("deq_w1", [nw1, 128 * NCB_W1], F32).ap(),
             "w2": nc.dram_tensor("deq_w2", [nw1, 128 * NCB_W1], F32).ap(),
             "w3": nc.dram_tensor("deq_w3", [nw3, 128 * NCB_W3], F32).ap()}
    if coop:
        red_in = nc.dram_tensor("red_in", [1, 4], F32).ap()
        red_out = nc.dram_tensor("red_out", [1, 4], F32).ap()
        agi = {"w1": nc.dram_tensor("agi_w1", [UPC_W1 * 128, D], BF16).ap(),
               "w2": nc.dram_tensor("agi_w2", [UPC_W1 * 128, D], BF16).ap(),
               "w3": nc.dram_tensor("agi_w3", [UPC_W3 * 128, DFF], BF16).ap()}
        wT = {"w1": nc.dram_tensor("ago_w1", [NCORE * UPC_W1 * 128, D], BF16).ap(),
              "w2": nc.dram_tensor("ago_w2", [NCORE * UPC_W1 * 128, D], BF16).ap(),
              "w3": nc.dram_tensor("ago_w3", [NCORE * UPC_W3 * 128, DFF], BF16).ap()}
    else:
        agi = None
        red_in = red_out = None
        wT = {"w1": nc.dram_tensor("w1T", [W1_UNITS * 128, D], BF16).ap(),
              "w2": nc.dram_tensor("w2T", [W1_UNITS * 128, D], BF16).ap(),
              "w3": nc.dram_tensor("w3T", [W3_UNITS * 128, DFF], BF16).ap()}

    stack = contextlib.ExitStack()
    with stack:
        k = K(nc, stack)
        sbuf = lambda name, shape, dt: stack.enter_context(nc.sbuf_tensor(name, shape, dt))
        psum = lambda name, shape, dt: stack.enter_context(nc.psum_tensor(name, shape, dt))

        X = sbuf("X", [128, NDC * C], F32)
        XN = sbuf("XN", [128, NDC * C], BF16)
        O = sbuf("O", [128, NDC * C], F32)
        O2 = sbuf("O2", [128, NDC * C], BF16)
        H = sbuf("H", [128, NFC * C], BF16)
        wsbig = sbuf("wsbig", [128, 4 * NDC * 256], BF16)
        ws = [wsbig[:, i * NDC * 256:(i + 1) * NDC * 256] for i in range(4)]
        w3sv = [wsbig[:, 0:NFC * 128],
                wsbig[:, 2 * NDC * 256:2 * NDC * 256 + NFC * 128]]
        w3s = w3sv[0]
        gt = [sbuf(f"gt{i}", [128, C], F32) for i in range(5)]
        xbt = [sbuf(f"xbt{i}", [128, C], BF16) for i in range(2)]
        x2t = [sbuf(f"x2t{i}", [128, C], BF16) for i in range(2)]
        mbc = sbuf("mbc", [128, C], F32)
        rbc = sbuf("rbc", [128, C], F32)
        smalls = sbuf("smalls", [128, C], F32)
        rows2 = sbuf("rows2", [128, 2 * C], F32)
        carry = sbuf("carry", [128, NDC], F32)
        gamma = sbuf("gamma", [128, NDC], F32)
        ident = sbuf("ident", [128, 128], F32)
        onesc_bf = sbuf("onesc_bf", [128, 1], BF16)
        onesc_f = sbuf("onesc_f", [128, 1], F32)
        onesr_f = sbuf("onesr_f", [128, 128], F32)
        mstage = sbuf("mstage", [1, 4], F32)
        meanbf = sbuf("meanbf", [1, 4], BF16)
        meanf = sbuf("meanf", [1, 4], F32)
        mskc = sbuf("mskc", [128, nw1], F32)
        tausb = sbuf("tausb", [128, 16], F32)

        QW3 = NCB_W3 // NQ * BLK
        hof = [0]

        def hview(cols_bf16, dtype=BF16):
            a = hof[0]
            hof[0] += cols_bf16
            assert hof[0] <= NFC * C, "H scratch overflow"
            v = H[:, a:a + cols_bf16]
            return v.bitcast(dtype) if dtype != BF16 else v

        raw = hview(QW3)
        scaled = hview(QW3)
        uout = hview(QW3)
        acc = hview(2 * QW3, F32)
        accp = hview(2 * QW3, F32)
        tmpd = hview(2 * QW3, F32)
        tmpp = hview(2 * QW3, F32)
        val = hview(2 * QW3, F32)
        scl = hview(2 * NCB_W3, F32)
        rsc = hview(2 * NCB_W3, F32)
        dqc = hview(2 * NCB_W3, F32)
        svec = hview(2, F32)

        def hviewp(cols_bf16, dtype, parts):
            a = hof[0]
            hof[0] += cols_bf16
            assert hof[0] <= NFC * C, "H scratch overflow"
            v = H[0:parts, a:a + cols_bf16]
            return v.bitcast(dtype) if dtype != BF16 else v

        pc_in = [hviewp(512, F32, 128) for _ in range(2)]
        pc_out = [hviewp(256, BF16, 128) for _ in range(2)]
        scg = hviewp(512, F32, G_W3)
        ctg = hviewp(256, BF16, G_W3)
        qg = hviewp(512, F32, G_W3)
        dqg_bf = hviewp(256, BF16, G_W3)
        dqg = hviewp(512, F32, G_W3)
        gcol = hviewp(16, F32, G_W3)
        gbf = hviewp(4, BF16, G_W3)

        pbt = [psum(f"pb{i}", [128, C], F32) for i in range(8)]
        pb = pbt

        BX = [Buf(f"X{i}") for i in range(NDC)]
        BXN = [Buf(f"XN{i}") for i in range(NDC)]
        BO = [Buf(f"O{i}") for i in range(NDC)]
        BO2 = [Buf(f"O2{i}") for i in range(NDC)]
        BH = [Buf(f"H{i}") for i in range(NFC)]
        Bw3sv = [Buf("w3sA"), Buf("w3sB")]
        Bw3s = Bw3sv[0]
        Bws = [Buf(f"ws{i}") for i in range(4)]
        Bgt = [Buf(f"gt{i}") for i in range(5)]
        Bxbt = [Buf(f"xbt{i}") for i in range(2)]
        Bx2t = [Buf(f"x2t{i}") for i in range(2)]
        Bmbc, Brbc, Brows = Buf("mbc"), Buf("rbc"), Buf("rows")
        Bcar = [Buf(f"car{i}") for i in range(NDC)]
        Bconst = Buf("const")
        Bmean = Buf("mean")
        Bpb = [Buf(f"pb{i}") for i in range(8)]
        Bd = {nm: Buf("d_" + nm) for nm in
              ("raw", "scl", "rsc", "dqc", "scaled", "acc", "accp", "tmpd",
               "tmpp", "val", "uout", "scg", "ctg", "qg", "dqg_bf", "dqg",
               "gcol", "gbf", "svec")}
        Bpc = [Buf(f"pc{i}") for i in range(2)]
        Bblk = Buf("blk_all")
        Bagi_pc = Buf("agi_pc")
        BscD = {w: Buf("scD_" + w) for w in ("w1", "w2", "w3")}
        BdeqD = {w: Buf("deqD_" + w) for w in ("w1", "w2", "w3")}
        BwT = {w: Buf("wT_" + w) for w in ("w1", "w2", "w3")}
        Bagi = {w: Buf("agi_" + w) for w in ("w1", "w2", "w3")}
        Bred = Buf("red")
        Bx_dram = Buf("x_dram")
        Bxb = [Buf(f"xb{i}") for i in range(8)]
        By_dram = Buf("y_dram")

        with nc.Block() as block:
            @block.sync
            def _(sync):
                k.dmav("c_const", "sync", gamma[:], gamma_in, writes=[Bconst], waw=False)
                k.dmav("c_const", "sync", ident[:], ident_in, writes=[Bconst], waw=False)
                k.dmav("c_const", "sync", onesc_bf[:], onesc_bf_in, writes=[Bconst], waw=False)
                k.dmav("c_const", "sync", onesc_f[:], onesc_f_in, writes=[Bconst], waw=False)
                k.dmav("c_const", "sync", onesr_f[:], onesr_f_in, writes=[Bconst], waw=False)
                k.dmav("c_const", "sync", mskc[:], msk1, writes=[Bconst], waw=False)
                k.dmav("c_const", "sync", tausb[:], taus_in, writes=[Bconst], waw=False)

                pc_dst = agi_pc if coop else blk_all
                pc_dstb = Bagi_pc if coop else Bblk
                pi = 0
                for bl in range(npc):
                    for dc in range(NDC):
                        bq = pi % 2
                        k.dmav(f"c_pc{bq}", "sync", pc_in[bq],
                               my_pc[bl * 128:(bl + 1) * 128,
                                     dc * 256:(dc + 1) * 256],
                               writes=[Bpc[bq]])
                        k.op("act", lambda e, bq=bq: e.copy(pc_out[bq], pc_in[bq]),
                             reads=[Bpc[bq]], writes=[Bpc[bq]])
                        k.dmav(f"c_out{bq}", "sync",
                               pc_dst[bl * 128:(bl + 1) * 128,
                                      dc * 256:(dc + 1) * 256],
                               pc_out[bq],
                               reads=[Bpc[bq]], writes=[pc_dstb], waw=False)
                        pi += 1
                if coop:
                    k.collective("cc_pc", "AllGather", ALU.bypass,
                                 agi_pc.opt(), blk_all.opt(),
                                 reads=[Bagi_pc], writes=[Bblk])

                wcfg = {
                    "w1": dict(src=my_w1, units=nw1, cols=D, ncb=NCB_W1,
                               G=G_W1, nb=NB_W1, ntile=NDC, masked=True),
                    "w2": dict(src=my_w2, units=nw1, cols=D, ncb=NCB_W1,
                               G=G_W1, nb=NB_W1, ntile=NDC, masked=True),
                    "w3": dict(src=my_w3, units=nw3, cols=DFF, ncb=NCB_W3,
                               G=G_W3, nb=NB_W3, ntile=NFC, masked=False),
                }

                def dq_gen():
                    k.op("dve", lambda e: e.memset(mstage[:], 0.0), writes=[Bmean])
                    for wi_, (w, cfg) in enumerate(wcfg.items()):
                        src_, units, cols, ncb = (cfg["src"], cfg["units"],
                                                  cfg["cols"], cfg["ncb"])
                        for u in range(units):
                            qc = cols // NQ
                            qcb = ncb // NQ
                            for h in range(NQ):
                                k.dmav("c_raw", "pool", raw[:, 0:qc],
                                       src_[u * 128:(u + 1) * 128,
                                            h * qc:(h + 1) * qc],
                                       writes=[Bd["raw"]])
                                k.op("dve", lambda e, h=h, qcb=qcb, qc=qc: e.tensor_reduce(
                                        scl[:, h * qcb:(h + 1) * qcb].rearrange(
                                            "p (a o) -> p a o", o=1),
                                        raw[:, 0:qc].rearrange("p (a b) -> p a b", b=BLK),
                                        AX.X, ALU.max, apply_absolute_value=True),
                                     reads=[Bd["raw"]], writes=[Bd["scl"]])
                            k.dmav("c_scout", "pool",
                                   sc_d[w][u].rearrange("(p j) -> p j", j=ncb),
                                   scl[:, 0:ncb],
                                   reads=[Bd["scl"]], writes=[BscD[w]], waw=False)
                            k.op("dve", lambda e, ncb=ncb: e.reduce_sum(
                                    svec.rearrange("p (a o) -> p a o", o=1),
                                    scl[:, 0:ncb].rearrange("p (a b) -> p a b", b=ncb),
                                    AX.X),
                                 reads=[Bd["scl"]], writes=[Bd["svec"]])
                            lhs_ap = (mskc[:, u:u + 1] if cfg["masked"]
                                      else onesc_f[:, 0:1])
                            k.op("pe", lambda e, lhs_ap=lhs_ap: e.matmul(
                                    pb[5][0:1, 0:1], lhs_ap, svec,
                                    start=True, stop=True),
                                 reads=[Bd["svec"], Bconst], writes=[Bpb[5]])
                            k.op("dve", lambda e, wi_=wi_: e.tensor_tensor(
                                    mstage[0:1, wi_:wi_ + 1],
                                    mstage[0:1, wi_:wi_ + 1],
                                    pb[5][0:1, 0:1], ALU.add),
                                 reads=[Bpb[5], Bmean], writes=[Bmean])
                            yield

                    if coop:
                        k.dmav("p_red", "pool", red_in, mstage[:],
                               reads=[Bmean], writes=[Bred])
                        k.collective("cc_red", "AllReduce", ALU.add, red_in.opt(),
                                     red_out.opt(), reads=[Bred], writes=[Bred])
                        k.dmav("c_mst", "pool", mstage[:], red_out,
                               reads=[Bred], writes=[Bmean])
                    k.op("dve", lambda e: e.memset(meanbf[:], 0.0), writes=[Bmean])
                    for wi_, w in enumerate(("w1", "w2", "w3")):
                        k.op("dve", lambda e, wi_=wi_, w=w: e.tensor_scalar(
                                meanbf[0:1, wi_:wi_ + 1], mstage[0:1, wi_:wi_ + 1],
                                1.0 / wcfg[w]["nb"], None, ALU.mult),
                             reads=[Bmean], writes=[Bmean])
                    k.op("dve", lambda e: e.tensor_copy(meanf[:], meanbf[:]),
                         reads=[Bmean], writes=[Bmean])
                    yield

                    for wi_, (w, cfg) in enumerate(wcfg.items()):
                        units, ncb, G = cfg["units"], cfg["ncb"], cfg["G"]
                        for u in range(units):
                            k.dmav("c_scg", "pool", scg[0:G, :],
                                   sc_d[w][u].rearrange("(g j) -> g j", j=256),
                                   reads=[BscD[w]], writes=[Bd["scg"]])
                            k.op("pe", lambda e, G=G, wi_=wi_: e.matmul(
                                    pb[7][0:G, 0:1], onesr_f[0:1, 0:G],
                                    meanf[0:1, wi_:wi_ + 1], start=True, stop=True),
                                 reads=[Bmean, Bconst], writes=[Bpb[7]])
                            k.op("dve", lambda e, G=G: e.tensor_copy(
                                    gcol[0:G, 3:4], pb[7][0:G, 0:1]),
                                 reads=[Bpb[7]], writes=[Bd["gcol"]])
                            k.op("dve", lambda e, G=G: e.tensor_scalar(
                                    ctg[0:G, :], scg[0:G, :], gcol[0:G, 3:4], None,
                                    ALU.subtract),
                                 reads=[Bd["scg"], Bd["gcol"]], writes=[Bd["ctg"]])
                            k.op("dve", lambda e, G=G: e.tensor_reduce(
                                    gcol[0:G, 0:1].rearrange("p (a o) -> p a o", o=1),
                                    ctg[0:G, :].rearrange("p (a b) -> p a b", b=256),
                                    AX.X, ALU.max, apply_absolute_value=True),
                                 reads=[Bd["ctg"]], writes=[Bd["gcol"]])
                            k.op("dve", lambda e, G=G: e.reciprocal(
                                    gcol[0:G, 4:5], gcol[0:G, 0:1]),
                                 reads=[Bd["gcol"]], writes=[Bd["gcol"]])
                            k.op("dve", lambda e, G=G: e.tensor_scalar(
                                    gbf[0:G, 0:1], gcol[0:G, 4:5], 128.0, None,
                                    ALU.mult),
                                 reads=[Bd["gcol"]], writes=[Bd["gbf"]])
                            k.op("dve", lambda e, G=G: e.tensor_copy(
                                    gcol[0:G, 1:2], gbf[0:G, 0:1]),
                                 reads=[Bd["gbf"]], writes=[Bd["gcol"]])
                            k.op("dve", lambda e, G=G: e.reciprocal(
                                    gcol[0:G, 2:3], gcol[0:G, 1:2]),
                                 reads=[Bd["gcol"]], writes=[Bd["gcol"]])
                            k.op("dve", lambda e, G=G: e.tensor_scalar(
                                    ctg[0:G, :], ctg[0:G, :], gcol[0:G, 1:2], None,
                                    ALU.mult),
                                 reads=[Bd["gcol"], Bd["ctg"]], writes=[Bd["ctg"]])
                            k.op("dve", lambda e, G=G: e.tensor_scalar(
                                    qg[0:G, :], ctg[0:G, :], MAGIC, -MAGIC,
                                    ALU.add, ALU.add),
                                 reads=[Bd["ctg"]], writes=[Bd["qg"]])
                            k.op("dve", lambda e, G=G: e.tensor_scalar(
                                    qg[0:G, :], qg[0:G, :], -128.0, 127.0,
                                    ALU.max, ALU.min),
                                 reads=[Bd["qg"]], writes=[Bd["qg"]])
                            k.op("dve", lambda e, G=G: e.tensor_scalar(
                                    dqg_bf[0:G, :], qg[0:G, :], gcol[0:G, 2:3], None,
                                    ALU.mult),
                                 reads=[Bd["qg"], Bd["gcol"]], writes=[Bd["dqg_bf"]])
                            k.op("dve", lambda e, G=G: e.tensor_scalar(
                                    dqg_bf[0:G, :], dqg_bf[0:G, :], gcol[0:G, 3:4],
                                    None, ALU.add),
                                 reads=[Bd["dqg_bf"], Bd["gcol"]], writes=[Bd["dqg_bf"]])
                            k.op("dve", lambda e, G=G: e.tensor_copy(
                                    dqg[0:G, :], dqg_bf[0:G, :]),
                                 reads=[Bd["dqg_bf"]], writes=[Bd["dqg"]])
                            k.dmav("c_dqout", "pool",
                                   deq_d[w][u].rearrange("(g j) -> g j", j=256),
                                   dqg[0:G, :],
                                   reads=[Bd["dqg"]], writes=[BdeqD[w]], waw=False)
                            yield

                    for w, cfg in wcfg.items():
                        src_, units, cols, ncb = (cfg["src"], cfg["units"],
                                                  cfg["cols"], cfg["ncb"])
                        ntile = cfg["ntile"]
                        for u in range(units):
                            k.dmav("c_scl", "pool", scl[:, 0:ncb],
                                   sc_d[w][u].rearrange("(p j) -> p j", j=ncb),
                                   reads=[BscD[w]], writes=[Bd["scl"]])
                            k.op("dve", lambda e, ncb=ncb: e.reciprocal(
                                    rsc[:, 0:ncb], scl[:, 0:ncb]),
                                 reads=[Bd["scl"]], writes=[Bd["rsc"]])
                            k.dmav("c_dqc", "pool", dqc[:, 0:ncb],
                                   deq_d[w][u].rearrange("(p j) -> p j", j=ncb),
                                   reads=[BdeqD[w]], writes=[Bd["dqc"]])
                            qc = cols // NQ
                            qcb = ncb // NQ
                            qt = ntile // NQ
                            for h in range(NQ):
                                k.dmav("c_raw", "pool", raw[:, 0:qc],
                                       src_[u * 128:(u + 1) * 128,
                                            h * qc:(h + 1) * qc],
                                       writes=[Bd["raw"]])
                                k.op("dve", lambda e, h=h, qc=qc, qcb=qcb: e.tensor_tensor(
                                        scaled[:, 0:qc].rearrange("p (a b) -> p a b", b=BLK),
                                        raw[:, 0:qc].rearrange("p (a b) -> p a b", b=BLK),
                                        rsc[:, h * qcb:(h + 1) * qcb].rearrange(
                                            "p (a o) -> p a o", o=1).broadcast_to(
                                            [128, qcb, BLK]),
                                        ALU.mult),
                                     reads=[Bd["raw"], Bd["rsc"]], writes=[Bd["scaled"]])
                                k.op("dve", lambda e, qc=qc: e.memset(acc[:, 0:qc], ACC0),
                                     writes=[Bd["acc"]])
                                k.op("pool", lambda e, qc=qc: e.memset(accp[:, 0:qc], 0.0),
                                     writes=[Bd["accp"]])
                                yield
                                for kk in range(15):
                                    if kk % 3 == 0:
                                        yield
                                    sg = tmpd if (kk % 2 == 0) else tmpp
                                    bsg = Bd["tmpd"] if (kk % 2 == 0) else Bd["tmpp"]
                                    k.op("act", lambda e, kk=kk, qc=qc, sg=sg: e.activation(
                                            sg[:, 0:qc], scaled[:, 0:qc], AF.Sign,
                                            bias=tausb[:, kk:kk + 1], scale=1.0),
                                         reads=[Bd["scaled"], Bconst], writes=[bsg])
                                    if kk in K_POOL:
                                        k.op("dve", lambda e, kk=kk, qc=qc, sg=sg: e.tensor_scalar(
                                                val[:, 0:qc], sg[:, 0:qc],
                                                HDELTA[kk], None, ALU.mult),
                                             reads=[bsg], writes=[Bd["val"]])
                                        k.op("pool", lambda e, qc=qc: e.tensor_tensor(
                                                accp[:, 0:qc], accp[:, 0:qc],
                                                val[:, 0:qc], ALU.add),
                                             reads=[Bd["val"], Bd["accp"]],
                                             writes=[Bd["accp"]])
                                    else:
                                        k.op("dve", lambda e, kk=kk, qc=qc, sg=sg: e.scalar_tensor_tensor(
                                                acc[:, 0:qc], sg[:, 0:qc],
                                                HDELTA[kk], acc[:, 0:qc],
                                                ALU.mult, ALU.add),
                                             reads=[bsg, Bd["acc"]], writes=[Bd["acc"]])
                                k.op("dve", lambda e, qc=qc: e.tensor_tensor(
                                        acc[:, 0:qc], acc[:, 0:qc], accp[:, 0:qc],
                                        ALU.add),
                                     reads=[Bd["accp"], Bd["acc"]], writes=[Bd["acc"]])
                                yield
                                k.op("dve", lambda e, h=h, qc=qc, qcb=qcb: e.tensor_tensor(
                                        val[:, 0:qc].rearrange("p (a b) -> p a b", b=BLK),
                                        acc[:, 0:qc].rearrange("p (a b) -> p a b", b=BLK),
                                        dqc[:, h * qcb:(h + 1) * qcb].rearrange(
                                            "p (a o) -> p a o", o=1).broadcast_to(
                                            [128, qcb, BLK]),
                                        ALU.mult),
                                     reads=[Bd["acc"], Bd["dqc"]], writes=[Bd["val"]])
                                for tq in range(0, qt, 4):
                                    ntq = min(4, qt - tq)
                                    fs = []
                                    for tt in range(ntq):
                                        fs.append(lambda e, t_=tq + tt, tt=tt: e.transpose(
                                            pb[7][:, tt * 128:(tt + 1) * 128],
                                            val[:, t_ * 128:(t_ + 1) * 128],
                                            ident[:]))
                                    k.group("pe", fs, reads=[Bd["val"], Bconst],
                                            writes=[Bpb[7]])
                                    k.op("dve", lambda e, tq=tq, ntq=ntq: e.tensor_copy(
                                            uout[:, tq * 128:tq * 128 + ntq * 128],
                                            pb[7][:, 0:ntq * 128]),
                                         reads=[Bpb[7]], writes=[Bd["uout"]])
                                dst = agi[w] if coop else wT[w]
                                dstb = Bagi[w] if coop else BwT[w]
                                k.dmav("c_uout", "pool",
                                       dst[u * 128:(u + 1) * 128,
                                           h * qt * 128:(h + 1) * qt * 128],
                                       uout[:, 0:qt * 128],
                                       reads=[Bd["uout"]], writes=[dstb], waw=False)
                                yield
                        if coop:
                            k.collective("cc_" + w, "AllGather", ALU.bypass,
                                         agi[w].opt(), wT[w].opt(),
                                         reads=[Bagi[w]], writes=[BwT[w]])

                    snap = k.snap()
                    for b_ in BH + Bw3sv + Bws:
                        b_.rs = b_.rs + list(snap)
                    yield

                dqit = dq_gen()

                def pump(n=1):
                    for _ in range(n):
                        if next(dqit, "DONE") == "DONE":
                            return

                def ln_head(src_bufs, src_sb, dc):
                    bq = dc % 2
                    sl = src_sb[:, dc * C:(dc + 1) * C]
                    k.op("dve", lambda e, sl=sl, bq=bq: e.tensor_copy(
                            xbt[bq][:], sl),
                         reads=[src_bufs[dc]], writes=[Bxbt[bq]])
                    k.op("act", lambda e, sl=sl, bq=bq: e.activation(
                            x2t[bq][:], sl, AF.Square, bias=0.0, scale=1.0),
                         reads=[src_bufs[dc]], writes=[Bx2t[bq]])
                    k.group("pe", [
                        lambda e, bq=bq, dc=dc: e.matmul(
                            pb[6][0:1, :], onesc_bf[:], xbt[bq][:],
                            start=(dc == 0), stop=(dc == NDC - 1)),
                        lambda e, bq=bq, dc=dc: e.matmul(
                            pb[7][0:1, :], onesc_bf[:], x2t[bq][:],
                            start=(dc == 0), stop=(dc == NDC - 1)),
                    ], reads=[Bxbt[bq], Bx2t[bq], Bconst],
                       writes=[Bpb[6], Bpb[7]])

                def emit_ln(src_bufs, src_sb, dst_bufs, dst_sb, skip_head=False):
                    mrow = rows2[0:1, 0:C]
                    vrow = rows2[0:1, C:2 * C]
                    trow = smalls[0:1, :]
                    rrow = vrow
                    if not skip_head:
                        for dc in range(NDC):
                            ln_head(src_bufs, src_sb, dc)
                    k.op("dve", lambda e: e.tensor_scalar(
                            mrow, pb[6][0:1, :], 1.0 / D, None, ALU.mult),
                         reads=[Bpb[6]], writes=[Brows])
                    k.op("dve", lambda e: e.tensor_scalar(
                            vrow, pb[7][0:1, :], 1.0 / D, None, ALU.mult),
                         reads=[Bpb[7]], writes=[Brows])
                    k.op("dve", lambda e: e.tensor_tensor(
                            trow, mrow, mrow, ALU.mult),
                         reads=[Brows], writes=[Brows])
                    k.op("dve", lambda e: e.tensor_scalar(
                            trow, trow, -1.0, EPS, ALU.mult, ALU.add),
                         reads=[Brows], writes=[Brows])
                    k.op("dve", lambda e: e.tensor_tensor(
                            vrow, vrow, trow, ALU.add),
                         reads=[Brows], writes=[Brows])
                    k.op("act", lambda e: e.activation(
                            trow, vrow, AF.Sqrt, bias=0.0, scale=1.0),
                         reads=[Brows], writes=[Brows])
                    k.op("dve", lambda e: e.reciprocal(rrow, trow),
                         reads=[Brows], writes=[Brows])
                    k.op("pe", lambda e: e.matmul(
                            pb[6][:], onesr_f[0:1, :], mrow, start=True, stop=True),
                         reads=[Brows, Bconst], writes=[Bpb[6]])
                    k.op("pe", lambda e: e.matmul(
                            pb[7][:], onesr_f[0:1, :], rrow, start=True, stop=True),
                         reads=[Brows, Bconst], writes=[Bpb[7]])
                    k.op("dve", lambda e: e.tensor_copy(mbc[:], pb[6][:]),
                         reads=[Bpb[6]], writes=[Bmbc])
                    k.op("dve", lambda e: e.tensor_copy(rbc[:], pb[7][:]),
                         reads=[Bpb[7]], writes=[Brbc])
                    for dc in range(NDC):
                        en = "pool" if dc % 3 == 2 else "dve"
                        bq = dc % 2 + (2 if en == "pool" else 0)
                        sl = src_sb[:, dc * C:(dc + 1) * C]
                        dl = dst_sb[:, dc * C:(dc + 1) * C]
                        k.op(en, lambda e, sl=sl, bq=bq: e.tensor_tensor(
                                gt[bq][:], sl, mbc[:], ALU.subtract),
                             reads=[src_bufs[dc], Bmbc], writes=[Bgt[bq]])
                        k.op(en, lambda e, dl=dl, bq=bq: e.tensor_tensor(
                                dl, gt[bq][:], rbc[:], ALU.mult),
                             reads=[Bgt[bq], Brbc], writes=[dst_bufs[dc]])

                WNUM = {"wi": 0, "wf": 1, "wg": 2, "wo": 3}

                def load_wave(nm, wv, wsl):
                    b = WNUM[nm] * 8 + wv
                    return k.dmav(f"c_ws{wsl}", "sync", ws[wsl],
                                  blk_all[b * 128:(b + 1) * 128, :],
                                  reads=[Bblk], writes=[Bws[wsl]])

                def emit_attn(ch, bounce):
                    seq = ch // CH_PER_SEQ
                    t0 = (ch % CH_PER_SEQ) * C
                    first = (ch % CH_PER_SEQ) == 0
                    pump_here = ch < KB

                    for dc in range(NDC):
                        k.dmav(f"c_x{dc}", "sync",
                               X[:, dc * C:(dc + 1) * C],
                               x_in[seq, dc * 128:(dc + 1) * 128, t0:t0 + C],
                               reads=[Bx_dram], writes=[BX[dc]])

                    emit_ln(BX, X, BXN, XN)

                    for wv in range(8):
                        if pump_here:
                            pump(1)
                        pf0 = (wv % 2) * 4
                        pi0 = pf0 + 2
                        wf_s = (wv % 2) * 2
                        wi_s = wf_s + 1
                        load_wave("wf", wv, wf_s)
                        load_wave("wi", wv, wi_s)
                        for wsl, pq in ((wf_s, pf0), (wi_s, pi0)):
                            fs = []
                            for dc in range(NDC):
                                for ecl in range(2):
                                    fs.append(lambda e, dc=dc, ecl=ecl, wsl=wsl,
                                              pq=pq: e.matmul(
                                        pb[pq + ecl][:],
                                        ws[wsl][:, dc * 256 + ecl * 128:
                                                dc * 256 + (ecl + 1) * 128],
                                        XN[:, dc * C:(dc + 1) * C],
                                        start=(dc == 0), stop=(dc == NDC - 1)))
                            k.group("pe", fs, reads=[Bws[wsl]] + BXN,
                                    writes=[Bpb[pq], Bpb[pq + 1]])
                        for ecl in range(2):
                            ec = 2 * wv + ecl
                            pf = pf0 + ecl
                            pi_ = pi0 + ecl
                            ta, tnf, tsi = gt[2 * ecl], gt[4], gt[2 * ecl + 1]
                            ga, gnf, gsi = Bgt[2 * ecl], Bgt[4], Bgt[2 * ecl + 1]
                            k.op("act", lambda e, pf=pf, ta=ta: e.activation(
                                    ta[:], pb[pf][:], AF.Sigmoid, bias=0.0, scale=1.0),
                                 reads=[Bpb[pf]], writes=[ga])
                            k.op("act", lambda e, pf=pf, tnf=tnf: e.activation(
                                    tnf[:], pb[pf][:], AF.Sigmoid, bias=0.0, scale=-1.0),
                                 reads=[Bpb[pf]], writes=[gnf])
                            k.op("act", lambda e, pi_=pi_, tsi=tsi: e.activation(
                                    tsi[:], pb[pi_][:], AF.Silu, bias=0.0, scale=1.0),
                                 reads=[Bpb[pi_]], writes=[gsi])
                            k.op("dve", lambda e, tsi=tsi, tnf=tnf: e.tensor_tensor(
                                    tsi[:], tsi[:], tnf[:], ALU.mult),
                                 reads=[gnf, gsi], writes=[gsi])
                            init = 0.0 if first else carry[:, ec:ec + 1]
                            rd = [ga, gsi] + ([] if first else [Bcar[ec]])
                            k.op("dve", lambda e, ec=ec, ta=ta, tsi=tsi, init=init:
                                    e.tensor_tensor_scan(
                                        O[:, ec * C:(ec + 1) * C], ta[:], tsi[:],
                                        init, ALU.mult, ALU.add),
                                 reads=rd, writes=[BO[ec]])
                            k.op("dve", lambda e, ec=ec: e.tensor_copy(
                                    carry[:, ec:ec + 1],
                                    O[:, ec * C + C - 1:ec * C + C]),
                                 reads=[BO[ec]], writes=[Bcar[ec]])
                            if pump_here:
                                pump(2)

                    for ec in range(NDC):
                        bq = ec % 2
                        k.op("act", lambda e, ec=ec, bq=bq: e.activation(
                                x2t[bq][:], O[:, ec * C:(ec + 1) * C],
                                AF.Square, bias=0.0, scale=1.0),
                             reads=[BO[ec]], writes=[Bx2t[bq]])
                        k.op("pe", lambda e, ec=ec, bq=bq: e.matmul(
                                pb[6][0:1, :], onesc_bf[:], x2t[bq][:],
                                start=(ec == 0), stop=(ec == NDC - 1)),
                             reads=[Bx2t[bq], Bconst], writes=[Bpb[6]])
                    trow = smalls[0:1, :]
                    rrow = rows2[0:1, C:2 * C]
                    k.op("dve", lambda e: e.tensor_scalar(
                            trow, pb[6][0:1, :], 1.0 / D, EPS, ALU.mult, ALU.add),
                         reads=[Bpb[6]], writes=[Brows])
                    k.op("act", lambda e: e.activation(
                            trow, trow, AF.Sqrt, bias=0.0, scale=1.0),
                         reads=[Brows], writes=[Brows])
                    k.op("dve", lambda e: e.reciprocal(rrow, trow),
                         reads=[Brows], writes=[Brows])
                    k.op("pe", lambda e: e.matmul(
                            pb[7][:], onesr_f[0:1, :], rrow, start=True, stop=True),
                         reads=[Brows, Bconst], writes=[Bpb[7]])
                    k.op("dve", lambda e: e.tensor_copy(rbc[:], pb[7][:]),
                         reads=[Bpb[7]], writes=[Brbc])

                    for wv in range(8):
                        if pump_here:
                            pump(1)
                        pg0 = (wv % 2) * 4
                        wsl = (wv % 2) * 2
                        load_wave("wg", wv, wsl)
                        fs = []
                        for dc in range(NDC):
                            for ecl in range(2):
                                fs.append(lambda e, dc=dc, ecl=ecl, wsl=wsl,
                                          pg0=pg0: e.matmul(
                                    pb[pg0 + ecl][:],
                                    ws[wsl][:, dc * 256 + ecl * 128:
                                            dc * 256 + (ecl + 1) * 128],
                                    XN[:, dc * C:(dc + 1) * C],
                                    start=(dc == 0), stop=(dc == NDC - 1)))
                        k.group("pe", fs, reads=[Bws[wsl]] + BXN,
                                writes=[Bpb[pg0], Bpb[pg0 + 1]])
                        for ecl in range(2):
                            ec = 2 * wv + ecl
                            tsg, ttmp = gt[2 * ecl], gt[2 * ecl + 1]
                            gsg, gtmp = Bgt[2 * ecl], Bgt[2 * ecl + 1]
                            k.op("act", lambda e, p_=pg0 + ecl, tsg=tsg: e.activation(
                                    tsg[:], pb[p_][:], AF.Silu, bias=0.0, scale=1.0),
                                 reads=[Bpb[pg0 + ecl]], writes=[gsg])
                            k.op("dve", lambda e, ec=ec, ttmp=ttmp:
                                    e.tensor_scalar(
                                        ttmp[:], O[:, ec * C:(ec + 1) * C],
                                        gamma[:, ec:ec + 1], None, ALU.mult),
                                 reads=[BO[ec], Bconst], writes=[gtmp])
                            k.op("pool", lambda e, tsg=tsg, ttmp=ttmp:
                                    e.tensor_tensor(
                                        ttmp[:], ttmp[:], tsg[:], ALU.mult),
                                 reads=[gsg, gtmp], writes=[gtmp])
                            k.op("dve", lambda e, ec=ec, ttmp=ttmp: e.tensor_tensor(
                                    O2[:, ec * C:(ec + 1) * C], ttmp[:], rbc[:],
                                    ALU.mult),
                                 reads=[gtmp, Brbc], writes=[BO2[ec]])
                            if pump_here:
                                pump(2)

                    for wv in range(8):
                        po0 = (wv % 2) * 4
                        wsl = (wv % 2) * 2 + 1
                        load_wave("wo", wv, wsl)
                        fs = []
                        for dc in range(NDC):
                            for ecl in range(2):
                                fs.append(lambda e, dc=dc, ecl=ecl, wsl=wsl,
                                          po0=po0: e.matmul(
                                    pb[po0 + ecl][:],
                                    ws[wsl][:, dc * 256 + ecl * 128:
                                            dc * 256 + (ecl + 1) * 128],
                                    O2[:, dc * C:(dc + 1) * C],
                                    start=(dc == 0), stop=(dc == NDC - 1)))
                        k.group("pe", fs, reads=[Bws[wsl]] + BO2,
                                writes=[Bpb[po0], Bpb[po0 + 1]])
                        for ecl in range(2):
                            ec = 2 * wv + ecl
                            k.op("dve", lambda e, ec=ec, p_=po0 + ecl: e.tensor_tensor(
                                    X[:, ec * C:(ec + 1) * C],
                                    X[:, ec * C:(ec + 1) * C],
                                    pb[p_][:], ALU.add),
                                 reads=[Bpb[po0 + ecl]], writes=[BX[ec]])

                    if bounce:
                        k.dmav("p_xb", "pool",
                               xb_dram[ch * 128:(ch + 1) * 128, :], X[:],
                               reads=BX, writes=[Bxb[ch]])

                def emit_ffn(ch, bounced):
                    seq = ch // CH_PER_SEQ
                    t0 = (ch % CH_PER_SEQ) * C

                    if bounced:
                        for dc in range(NDC):
                            k.dmav(f"c_x{dc}", "sync",
                                   X[:, dc * C:(dc + 1) * C],
                                   xb_dram[ch * 128:(ch + 1) * 128,
                                           dc * C:(dc + 1) * C],
                                   reads=[Bxb[ch]], writes=[BX[dc]])

                    emit_ln(BX, X, BXN, XN)

                    for wv in range(22):
                        pa0 = (wv % 2) * 4
                        pb0_ = pa0 + 2
                        ws1 = (wv % 2) * 2
                        ws2 = ws1 + 1
                        u0 = 2 * wv
                        k.dmav(f"c_ws{ws1}", "sync",
                               ws[ws1].rearrange("p (a f) -> p a f", a=2),
                               wT["w1"][u0 * 128:(u0 + 2) * 128, :].rearrange(
                                   "(a p) f -> p a f", p=128),
                               reads=[BwT["w1"]], writes=[Bws[ws1]])
                        k.dmav(f"c_ws{ws2}", "sync",
                               ws[ws2].rearrange("p (a f) -> p a f", a=2),
                               wT["w2"][u0 * 128:(u0 + 2) * 128, :].rearrange(
                                   "(a p) f -> p a f", p=128),
                               reads=[BwT["w2"]], writes=[Bws[ws2]])
                        for wsl, pq in ((ws1, pa0), (ws2, pb0_)):
                            fs = []
                            for dc in range(NDC):
                                for fcl in range(2):
                                    fs.append(lambda e, dc=dc, fcl=fcl, wsl=wsl,
                                              pq=pq: e.matmul(
                                        pb[pq + fcl][:],
                                        ws[wsl][:, fcl * 2048 + dc * 128:
                                                fcl * 2048 + (dc + 1) * 128],
                                        XN[:, dc * C:(dc + 1) * C],
                                        start=(dc == 0), stop=(dc == NDC - 1)))
                            k.group("pe", fs, reads=[Bws[wsl]] + BXN,
                                    writes=[Bpb[pq], Bpb[pq + 1]])
                        for fcl in range(2):
                            fc = 2 * wv + fcl
                            pa = pa0 + fcl
                            pbb = pb0_ + fcl
                            tsa = gt[2 * fcl]
                            k.op("act", lambda e, pa=pa, tsa=tsa: e.activation(
                                    tsa[:], pb[pa][:], AF.Silu, bias=0.0, scale=1.0),
                                 reads=[Bpb[pa]], writes=[Bgt[2 * fcl]])
                            k.op("dve", lambda e, fc=fc, tsa=tsa, pbb=pbb:
                                    e.tensor_tensor(
                                        H[:, fc * C:(fc + 1) * C], tsa[:],
                                        pb[pbb][:], ALU.mult),
                                 reads=[Bgt[2 * fcl], Bpb[pbb]], writes=[BH[fc]])

                    w3alias = [[Bw3sv[0], Bws[0], Bws[1]],
                               [Bw3sv[1], Bws[2], Bws[3]]]
                    for dc in range(NDC):
                        pv = pb[dc % 2]
                        sl3 = dc % 2
                        k.dmav(f"c_w3s{sl3}", "sync", w3sv[sl3],
                               wT["w3"][dc * 128:(dc + 1) * 128, :],
                               reads=[BwT["w3"]], writes=w3alias[sl3])
                        fs = []
                        for fc in range(NFC):
                            fs.append(lambda e, fc=fc, pv=pv, sl3=sl3: e.matmul(
                                pv[:], w3sv[sl3][:, fc * 128:(fc + 1) * 128],
                                H[:, fc * C:(fc + 1) * C],
                                start=(fc == 0), stop=(fc == NFC - 1)))
                        k.group("pe", fs, reads=w3alias[sl3] + BH,
                                writes=[Bpb[dc % 2]])
                        k.op("dve", lambda e, dc=dc, pv=pv: e.tensor_tensor(
                                X[:, dc * C:(dc + 1) * C],
                                X[:, dc * C:(dc + 1) * C], pv[:], ALU.add),
                             reads=[Bpb[dc % 2]], writes=[BX[dc]])

                    k.dmav("p_y", "pool",
                           y_out[seq, :, t0:t0 + C].rearrange(
                               "(dc p) t -> p dc t", p=128),
                           X[:].rearrange("p (dc t) -> p dc t", t=C),
                           reads=BX, writes=[By_dram], waw=False)

                for ch in range(KB):
                    emit_attn(ch, True)
                pump(10 ** 6)
                for ch in range(KB):
                    emit_ffn(ch, True)
                for ch in range(KB, NCH):
                    emit_attn(ch, False)
                    emit_ffn(ch, False)

                dy = k.dstream("p_y")
                k.eng["pool"].e.wait_ge(dy.sem, dy.n)
                k.eng["sync"].e.wait_ge(dy.sem, dy.n)

    return nc


_NC_CACHE = {}


def _get_nc(coop=COOP):
    if coop not in _NC_CACHE:
        _NC_CACHE[coop] = _build(coop)
    return _NC_CACHE[coop]


def kernel(L0_attn_hidden_states, L0_attn_Wi, L0_attn_Wf, L0_attn_Wg,
           L0_attn_Wo, L0_attn_g_norm_weight, L0_ffn_weight1, L0_ffn_weight2,
           L0_ffn_weight3):
    hs = np.asarray(L0_attn_hidden_states)
    wlist = [np.asarray(w, dtype=np.float32) for w in
             (L0_attn_Wi, L0_attn_Wf, L0_attn_Wg, L0_attn_Wo)]
    blk_imgs = []
    for w in wlist:
        a4 = w.reshape(NDC, 128, 8, 256)
        for wv in range(8):
            blk_imgs.append(np.ascontiguousarray(
                a4[:, :, wv, :].transpose(1, 0, 2).reshape(128, NDC * 256)))
    gnw = np.asarray(L0_attn_g_norm_weight, dtype=np.float32)
    w1 = np.asarray(L0_ffn_weight1)
    w2 = np.asarray(L0_ffn_weight2)
    w3 = np.asarray(L0_ffn_weight3)

    gamma_fm = np.ascontiguousarray(gnw.reshape(NDC, 128).T)
    ident_f32 = np.eye(128, dtype=np.float32)
    onesc_bf = np.ones((128, 1), dtype=ml_dtypes.bfloat16)
    onesc_f = np.ones((128, 1), dtype=np.float32)
    onesr_f = np.ones((128, 128), dtype=np.float32)
    taus_arr = np.zeros((128, 16), dtype=np.float32)
    for kk in range(15):
        taus_arr[:, kk] = np.float32(-TAU[kk])

    nc = _get_nc(COOP)
    in_maps = []
    for c in range(NCORE):
        x_fm = np.ascontiguousarray(
            hs[SEQ * c:SEQ * (c + 1)].transpose(0, 2, 1))
        if COOP:
            my_pc_arr = np.ascontiguousarray(
                np.concatenate(blk_imgs[4 * c:4 * c + 4], axis=0))
            r0 = c * UPC_W1 * 128
            r1 = min(DFF, r0 + UPC_W1 * 128)
            mw1 = np.ones((UPC_W1 * 128, D), dtype=ml_dtypes.bfloat16)
            mw2 = np.ones((UPC_W1 * 128, D), dtype=ml_dtypes.bfloat16)
            if r1 > r0:
                mw1[0:r1 - r0] = w1[r0:r1]
                mw2[0:r1 - r0] = w2[r0:r1]
            msk = np.zeros((128, UPC_W1), dtype=np.float32)
            for ul in range(UPC_W1):
                if (c * UPC_W1 + ul) < W1_UNITS:
                    msk[:, ul] = 1.0
            q0 = c * UPC_W3 * 128
            mw3 = np.ascontiguousarray(w3[q0:q0 + UPC_W3 * 128])
        else:
            my_pc_arr = np.ascontiguousarray(np.concatenate(blk_imgs, axis=0))
            mw1 = np.ascontiguousarray(w1)
            mw2 = np.ascontiguousarray(w2)
            mw3 = np.ascontiguousarray(w3)
            msk = np.ones((128, W1_UNITS), dtype=np.float32)
        in_maps.append({
            "x_in": x_fm,
            "my_pc": my_pc_arr,
            "my_w1": mw1, "my_w2": mw2, "my_w3": mw3, "msk1": msk,
            "in_gamma_fm": gamma_fm, "in_ident_f32": ident_f32,
            "in_onesc_bf": onesc_bf, "in_onesc_f": onesc_f, "in_onesr_f": onesr_f,
            "in_taus": taus_arr,
        })

    res = run_bass_kernel_spmd(nc, in_maps, core_ids=list(range(NCORE)))
    out = np.empty((B, T, D), dtype=np.float32)
    for c in range(NCORE):
        y = np.asarray(res.results[c]["y_out"])
        out[SEQ * c:SEQ * (c + 1)] = y.transpose(0, 2, 1)
    return out


# version 13
# speedup vs baseline: 1.0454x; 1.0041x over previous
import contextlib
import numpy as np
import ml_dtypes
import concourse.bass as bass
import concourse.mybir as mybir
from concourse.bass_utils import run_bass_kernel_spmd

F32 = mybir.dt.float32
BF16 = mybir.dt.bfloat16
AF = mybir.ActivationFunctionType
ALU = mybir.AluOpType
AX = mybir.AxisListType

D = 2048
DFF = 5632
B = 16
T = 2048
NCORE = 8
SEQ = B // NCORE
C = 512
NCH = (SEQ * T) // C
CH_PER_SEQ = T // C
NDC = D // 128
NFC = DFF // 128
EPS = 1e-5
MAGIC = 12582912.0

COOP = True

_NF4_RAW = [-1.0, -0.6962, -0.5251, -0.3949, -0.2844, -0.1848, -0.0911, 0.0,
            0.0796, 0.1609, 0.2461, 0.3379, 0.4407, 0.5626, 0.723, 1.0]
NF4 = np.array(_NF4_RAW, dtype=ml_dtypes.bfloat16).astype(np.float64)
L0 = float(np.float32(NF4[0]))
DELTA = [float(np.float32(NF4[k + 1] - NF4[k])) for k in range(15)]


def _bf16_threshold(mid):
    lo = ml_dtypes.bfloat16(mid)
    if np.float64(lo) > mid:
        up = np.float64(lo)
        dn = np.float64(np.nextafter(lo, ml_dtypes.bfloat16(-np.inf)))
    else:
        dn = np.float64(lo)
        up = np.float64(np.nextafter(lo, ml_dtypes.bfloat16(np.inf)))
    return float(np.float32((dn + up) / 2.0))


TAU = [_bf16_threshold((NF4[k] + NF4[k + 1]) / 2.0) for k in range(15)]

BLK = 64
W1_UNITS = DFF // 128
W3_UNITS = D // 128
NB_W1 = DFF * D // BLK
NB_W3 = D * DFF // BLK
NCB_W1 = D // BLK
NCB_W3 = DFF // BLK
G_W1 = 128 * NCB_W1 // 256
G_W3 = 128 * NCB_W3 // 256
K_POOL = [3, 7, 11, 14]
ACC0 = float(np.float32(np.float32(L0) + np.float32(sum(DELTA) / 2.0)))
HDELTA = [float(np.float32(d / 2.0)) for d in DELTA]
UPC_W1 = 6
UPC_W3 = 2
NQ = 4


class EngS:
    def __init__(self, stack, nc, name, e):
        self.name = name
        self.e = e
        self.sem = stack.enter_context(nc.semaphore("s_" + name))
        self.n = 0
        self.waited = {}


class DmaS:
    def __init__(self, stack, nc, name):
        self.name = name
        self.sem = stack.enter_context(nc.semaphore(name))
        self.n = 0


class Buf:
    __slots__ = ("name", "w", "rs")

    def __init__(self, name):
        self.name = name
        self.w = []
        self.rs = []


class K:
    def __init__(self, nc, stack):
        self.nc = nc
        self.eng = {
            "pe": EngS(stack, nc, "pe", nc.tensor),
            "dve": EngS(stack, nc, "dve", nc.vector),
            "act": EngS(stack, nc, "act", nc.scalar),
            "pool": EngS(stack, nc, "pool", nc.gpsimd),
            "sync": EngS(stack, nc, "sync", nc.sync),
        }
        self.stack = stack
        self.dmas = {}
        self.ccs = {}

    def dstream(self, name):
        if name not in self.dmas:
            self.dmas[name] = DmaS(self.stack, self.nc, name)
        return self.dmas[name]

    def ccstream(self, name):
        if name not in self.ccs:
            self.ccs[name] = DmaS(self.stack, self.nc, name)
        return self.ccs[name]

    def _semh(self, s):
        if s.startswith("s_"):
            return self.eng[s[2:]].sem
        if s in self.dmas:
            return self.dmas[s].sem
        if s in self.ccs:
            return self.ccs[s].sem
        raise KeyError(s)

    def _waits(self, en, toks):
        e = self.eng[en]
        need = {}
        for t in toks:
            if t is None:
                continue
            s, v = t
            if need.get(s, -1) < v:
                need[s] = v
        for s, v in need.items():
            if e.waited.get(s, -1) >= v:
                continue
            e.e.wait_ge(self._semh(s), v)
            e.waited[s] = v

    @staticmethod
    def _dep(reads, writes, waw=True):
        toks = []
        for b in reads:
            toks.extend(b.w)
        for b in writes:
            if waw:
                toks.extend(b.w)
            toks.extend(b.rs)
        return toks

    @staticmethod
    def _rec(t, reads, writes, waw=True):
        for b in writes:
            if waw:
                b.w = [t]
                b.rs = []
            else:
                b.w.append(t)
        for b in reads:
            b.rs.append(t)

    def op(self, en, f, reads=(), writes=()):
        e = self.eng[en]
        self._waits(en, self._dep(reads, writes))
        ins = f(e.e)
        e.n += 1
        ins.then_inc(e.sem, 1)
        t = ("s_" + en, e.n)
        self._rec(t, reads, writes)
        return t

    def group(self, en, fs, reads=(), writes=()):
        e = self.eng[en]
        self._waits(en, self._dep(reads, writes))
        last = None
        for f in fs:
            last = f(e.e)
        e.n += 1
        last.then_inc(e.sem, 1)
        t = ("s_" + en, e.n)
        self._rec(t, reads, writes)
        return t

    def dmav(self, stream, q, out_ap, in_ap, reads=(), writes=(), waw=True):
        e = self.eng[q]
        d = self.dstream(stream)
        self._waits(q, self._dep(reads, writes, waw))
        bb = self.nc.m.functions[0].blocks[-1]
        n0 = len(bb.instructions)
        ins = e.e.dma_start(out=out_ap, in_=in_ap)
        n1 = len(bb.instructions)
        pieces = sum(1 for i in range(n0, n1)
                     if "DMACopy" in type(bb.instructions[i]).__name__)
        assert pieces == 1, (stream, n0, n1)
        ins.then_inc(d.sem, 16)
        d.n += 16
        t = (d.name, d.n)
        self._rec(t, reads, writes, waw)
        return t

    def collective(self, stream, kind, alu, in_ap, out_ap, reads=(), writes=()):
        e = self.eng["pool"]
        cs = self.ccstream(stream)
        self._waits("pool", self._dep(reads, writes))
        ins = e.e.collective_compute(
            kind, alu, replica_groups=[list(range(NCORE))],
            ins=[in_ap], outs=[out_ap])
        cs.n += 1
        ins.then_inc(cs.sem)
        t = (cs.name, cs.n)
        self._rec(t, reads, writes)
        return t

    def snap(self):
        toks = [("s_" + n, s.n) for n, s in self.eng.items()]
        toks += [(d.name, d.n) for d in self.dmas.values()]
        toks += [(c.name, c.n) for c in self.ccs.values()]
        return toks


def _build(coop=COOP):
    nc = bass.Bass("TRN2", target_bir_lowering=False)

    x_in = nc.dram_tensor("x_in", [SEQ, D, T], F32, kind="ExternalInput").ap()
    y_out = nc.dram_tensor("y_out", [SEQ, D, T], F32, kind="ExternalOutput").ap()
    npc = 4 if coop else 32
    my_pc = nc.dram_tensor("my_pc", [npc * 128, NDC * 256], F32,
                           kind="ExternalInput").ap()
    nw1 = UPC_W1 if coop else W1_UNITS
    nw3 = UPC_W3 if coop else W3_UNITS
    my_w1 = nc.dram_tensor("my_w1", [nw1 * 128, D], BF16, kind="ExternalInput").ap()
    my_w2 = nc.dram_tensor("my_w2", [nw1 * 128, D], BF16, kind="ExternalInput").ap()
    my_w3 = nc.dram_tensor("my_w3", [nw3 * 128, DFF], BF16, kind="ExternalInput").ap()
    msk1 = nc.dram_tensor("msk1", [128, nw1], F32, kind="ExternalInput").ap()
    gamma_in = nc.dram_tensor("in_gamma_fm", [128, NDC], F32, kind="ExternalInput").ap()
    ident_in = nc.dram_tensor("in_ident_f32", [128, 128], F32, kind="ExternalInput").ap()
    onesc_bf_in = nc.dram_tensor("in_onesc_bf", [128, 1], BF16, kind="ExternalInput").ap()
    onesc_f_in = nc.dram_tensor("in_onesc_f", [128, 1], F32, kind="ExternalInput").ap()
    onesr_f_in = nc.dram_tensor("in_onesr_f", [128, 128], F32, kind="ExternalInput").ap()
    taus_in = nc.dram_tensor("in_taus", [128, 16], F32, kind="ExternalInput").ap()

    blk_all = nc.dram_tensor("blk_all", [32 * 128, NDC * 256], BF16).ap()
    KB = 8
    xb_dram = nc.dram_tensor("xb_dram", [KB * 128, NDC * C], F32).ap()
    if coop:
        agi_pc = nc.dram_tensor("agi_pc", [4 * 128, NDC * 256], BF16).ap()
    else:
        agi_pc = None
    sc_d = {"w1": nc.dram_tensor("sc_w1", [nw1, 128 * NCB_W1], F32).ap(),
            "w2": nc.dram_tensor("sc_w2", [nw1, 128 * NCB_W1], F32).ap(),
            "w3": nc.dram_tensor("sc_w3", [nw3, 128 * NCB_W3], F32).ap()}
    deq_d = {"w1": nc.dram_tensor("deq_w1", [nw1, 128 * NCB_W1], F32).ap(),
             "w2": nc.dram_tensor("deq_w2", [nw1, 128 * NCB_W1], F32).ap(),
             "w3": nc.dram_tensor("deq_w3", [nw3, 128 * NCB_W3], F32).ap()}
    if coop:
        red_in = nc.dram_tensor("red_in", [1, 4], F32).ap()
        red_out = nc.dram_tensor("red_out", [1, 4], F32).ap()
        agi = {"w1": nc.dram_tensor("agi_w1", [UPC_W1 * 128, D], BF16).ap(),
               "w2": nc.dram_tensor("agi_w2", [UPC_W1 * 128, D], BF16).ap(),
               "w3": nc.dram_tensor("agi_w3", [UPC_W3 * 128, DFF], BF16).ap()}
        wT = {"w1": nc.dram_tensor("ago_w1", [NCORE * UPC_W1 * 128, D], BF16).ap(),
              "w2": nc.dram_tensor("ago_w2", [NCORE * UPC_W1 * 128, D], BF16).ap(),
              "w3": nc.dram_tensor("ago_w3", [NCORE * UPC_W3 * 128, DFF], BF16).ap()}
    else:
        agi = None
        red_in = red_out = None
        wT = {"w1": nc.dram_tensor("w1T", [W1_UNITS * 128, D], BF16).ap(),
              "w2": nc.dram_tensor("w2T", [W1_UNITS * 128, D], BF16).ap(),
              "w3": nc.dram_tensor("w3T", [W3_UNITS * 128, DFF], BF16).ap()}

    stack = contextlib.ExitStack()
    with stack:
        k = K(nc, stack)
        sbuf = lambda name, shape, dt: stack.enter_context(nc.sbuf_tensor(name, shape, dt))
        psum = lambda name, shape, dt: stack.enter_context(nc.psum_tensor(name, shape, dt))

        X = sbuf("X", [128, NDC * C], F32)
        XN = sbuf("XN", [128, NDC * C], BF16)
        O = sbuf("O", [128, NDC * C], F32)
        O2 = sbuf("O2", [128, NDC * C], BF16)
        H = sbuf("H", [128, NFC * C], BF16)
        wsbig = sbuf("wsbig", [128, 4 * NDC * 256], BF16)
        ws = [wsbig[:, i * NDC * 256:(i + 1) * NDC * 256] for i in range(4)]
        w3sv = [wsbig[:, 0:NFC * 128],
                wsbig[:, 2 * NDC * 256:2 * NDC * 256 + NFC * 128]]
        w3s = w3sv[0]
        gt = [sbuf(f"gt{i}", [128, C], F32) for i in range(5)]
        xbt = [sbuf(f"xbt{i}", [128, C], BF16) for i in range(2)]
        x2t = [sbuf(f"x2t{i}", [128, C], BF16) for i in range(2)]
        mbc = sbuf("mbc", [128, C], F32)
        rbc = sbuf("rbc", [128, C], F32)
        smalls = sbuf("smalls", [128, C], F32)
        rows2 = sbuf("rows2", [128, 2 * C], F32)
        carry = sbuf("carry", [128, NDC], F32)
        gamma = sbuf("gamma", [128, NDC], F32)
        ident = sbuf("ident", [128, 128], F32)
        onesc_bf = sbuf("onesc_bf", [128, 1], BF16)
        onesc_f = sbuf("onesc_f", [128, 1], F32)
        onesr_f = sbuf("onesr_f", [128, 128], F32)
        mstage = sbuf("mstage", [1, 4], F32)
        meanbf = sbuf("meanbf", [1, 4], BF16)
        meanf = sbuf("meanf", [1, 4], F32)
        mskc = sbuf("mskc", [128, nw1], F32)
        tausb = sbuf("tausb", [128, 16], F32)

        QW3 = NCB_W3 // NQ * BLK
        hof = [0]

        def hview(cols_bf16, dtype=BF16):
            a = hof[0]
            hof[0] += cols_bf16
            assert hof[0] <= NFC * C, "H scratch overflow"
            v = H[:, a:a + cols_bf16]
            return v.bitcast(dtype) if dtype != BF16 else v

        raw = hview(QW3)
        scaled = hview(QW3)
        uout = hview(QW3)
        acc = hview(2 * QW3, F32)
        accp = hview(2 * QW3, F32)
        tmpd = hview(2 * QW3, F32)
        tmpp = hview(2 * QW3, F32)
        val = hview(2 * QW3, F32)
        scl = hview(2 * NCB_W3, F32)
        rsc = hview(2 * NCB_W3, F32)
        dqc = hview(2 * NCB_W3, F32)
        svec = hview(2, F32)

        def hviewp(cols_bf16, dtype, parts):
            a = hof[0]
            hof[0] += cols_bf16
            assert hof[0] <= NFC * C, "H scratch overflow"
            v = H[0:parts, a:a + cols_bf16]
            return v.bitcast(dtype) if dtype != BF16 else v

        pc_in = [hviewp(512, F32, 128) for _ in range(2)]
        pc_out = [hviewp(256, BF16, 128) for _ in range(2)]
        scg = hviewp(512, F32, G_W3)
        ctg = hviewp(256, BF16, G_W3)
        qg = hviewp(512, F32, G_W3)
        dqg_bf = hviewp(256, BF16, G_W3)
        dqg = hviewp(512, F32, G_W3)
        gcol = hviewp(16, F32, G_W3)
        gbf = hviewp(4, BF16, G_W3)

        pbt = [psum(f"pb{i}", [128, C], F32) for i in range(8)]
        pb = pbt

        BX = [Buf(f"X{i}") for i in range(NDC)]
        BXN = [Buf(f"XN{i}") for i in range(NDC)]
        BO = [Buf(f"O{i}") for i in range(NDC)]
        BO2 = [Buf(f"O2{i}") for i in range(NDC)]
        BH = [Buf(f"H{i}") for i in range(NFC)]
        Bw3sv = [Buf("w3sA"), Buf("w3sB")]
        Bw3s = Bw3sv[0]
        Bws = [Buf(f"ws{i}") for i in range(4)]
        Bgt = [Buf(f"gt{i}") for i in range(5)]
        Bxbt = [Buf(f"xbt{i}") for i in range(2)]
        Bx2t = [Buf(f"x2t{i}") for i in range(2)]
        Bmbc, Brbc, Brows = Buf("mbc"), Buf("rbc"), Buf("rows")
        Bcar = [Buf(f"car{i}") for i in range(NDC)]
        Bconst = Buf("const")
        Bmean = Buf("mean")
        Bpb = [Buf(f"pb{i}") for i in range(8)]
        Bd = {nm: Buf("d_" + nm) for nm in
              ("raw", "scl", "rsc", "dqc", "scaled", "acc", "accp", "tmpd",
               "tmpp", "val", "uout", "scg", "ctg", "qg", "dqg_bf", "dqg",
               "gcol", "gbf", "svec")}
        Bpc = [Buf(f"pc{i}") for i in range(2)]
        Bblk = Buf("blk_all")
        Bagi_pc = Buf("agi_pc")
        BscD = {w: Buf("scD_" + w) for w in ("w1", "w2", "w3")}
        BdeqD = {w: Buf("deqD_" + w) for w in ("w1", "w2", "w3")}
        BwT = {w: Buf("wT_" + w) for w in ("w1", "w2", "w3")}
        Bagi = {w: Buf("agi_" + w) for w in ("w1", "w2", "w3")}
        Bred = Buf("red")
        Bx_dram = Buf("x_dram")
        Bxb = [Buf(f"xb{i}") for i in range(8)]
        By_dram = Buf("y_dram")

        with nc.Block() as block:
            @block.sync
            def _(sync):
                k.dmav("c_const", "sync", gamma[:], gamma_in, writes=[Bconst], waw=False)
                k.dmav("c_const", "sync", ident[:], ident_in, writes=[Bconst], waw=False)
                k.dmav("c_const", "sync", onesc_bf[:], onesc_bf_in, writes=[Bconst], waw=False)
                k.dmav("c_const", "sync", onesc_f[:], onesc_f_in, writes=[Bconst], waw=False)
                k.dmav("c_const", "sync", onesr_f[:], onesr_f_in, writes=[Bconst], waw=False)
                k.dmav("c_const", "sync", mskc[:], msk1, writes=[Bconst], waw=False)
                k.dmav("c_const", "sync", tausb[:], taus_in, writes=[Bconst], waw=False)

                pc_dst = agi_pc if coop else blk_all
                pc_dstb = Bagi_pc if coop else Bblk
                pi = 0
                for bl in range(npc):
                    for dc in range(NDC):
                        bq = pi % 2
                        k.dmav(f"c_pc{bq}", "sync", pc_in[bq],
                               my_pc[bl * 128:(bl + 1) * 128,
                                     dc * 256:(dc + 1) * 256],
                               writes=[Bpc[bq]])
                        k.op("act", lambda e, bq=bq: e.copy(pc_out[bq], pc_in[bq]),
                             reads=[Bpc[bq]], writes=[Bpc[bq]])
                        k.dmav(f"c_out{bq}", "sync",
                               pc_dst[bl * 128:(bl + 1) * 128,
                                      dc * 256:(dc + 1) * 256],
                               pc_out[bq],
                               reads=[Bpc[bq]], writes=[pc_dstb], waw=False)
                        pi += 1
                if coop:
                    k.collective("cc_pc", "AllGather", ALU.bypass,
                                 agi_pc.opt(), blk_all.opt(),
                                 reads=[Bagi_pc], writes=[Bblk])

                wcfg = {
                    "w1": dict(src=my_w1, units=nw1, cols=D, ncb=NCB_W1,
                               G=G_W1, nb=NB_W1, ntile=NDC, masked=True),
                    "w2": dict(src=my_w2, units=nw1, cols=D, ncb=NCB_W1,
                               G=G_W1, nb=NB_W1, ntile=NDC, masked=True),
                    "w3": dict(src=my_w3, units=nw3, cols=DFF, ncb=NCB_W3,
                               G=G_W3, nb=NB_W3, ntile=NFC, masked=False),
                }

                def dq_gen():
                    k.op("dve", lambda e: e.memset(mstage[:], 0.0), writes=[Bmean])
                    for wi_, (w, cfg) in enumerate(wcfg.items()):
                        src_, units, cols, ncb = (cfg["src"], cfg["units"],
                                                  cfg["cols"], cfg["ncb"])
                        for u in range(units):
                            qc = cols // NQ
                            qcb = ncb // NQ
                            for h in range(NQ):
                                k.dmav("c_raw", "pool", raw[:, 0:qc],
                                       src_[u * 128:(u + 1) * 128,
                                            h * qc:(h + 1) * qc],
                                       writes=[Bd["raw"]])
                                k.op("dve", lambda e, h=h, qcb=qcb, qc=qc: e.tensor_reduce(
                                        scl[:, h * qcb:(h + 1) * qcb].rearrange(
                                            "p (a o) -> p a o", o=1),
                                        raw[:, 0:qc].rearrange("p (a b) -> p a b", b=BLK),
                                        AX.X, ALU.max, apply_absolute_value=True),
                                     reads=[Bd["raw"]], writes=[Bd["scl"]])
                            k.dmav("c_scout", "pool",
                                   sc_d[w][u].rearrange("(p j) -> p j", j=ncb),
                                   scl[:, 0:ncb],
                                   reads=[Bd["scl"]], writes=[BscD[w]], waw=False)
                            k.op("dve", lambda e, ncb=ncb: e.reduce_sum(
                                    svec.rearrange("p (a o) -> p a o", o=1),
                                    scl[:, 0:ncb].rearrange("p (a b) -> p a b", b=ncb),
                                    AX.X),
                                 reads=[Bd["scl"]], writes=[Bd["svec"]])
                            lhs_ap = (mskc[:, u:u + 1] if cfg["masked"]
                                      else onesc_f[:, 0:1])
                            k.op("pe", lambda e, lhs_ap=lhs_ap: e.matmul(
                                    pb[5][0:1, 0:1], lhs_ap, svec,
                                    start=True, stop=True),
                                 reads=[Bd["svec"], Bconst], writes=[Bpb[5]])
                            k.op("dve", lambda e, wi_=wi_: e.tensor_tensor(
                                    mstage[0:1, wi_:wi_ + 1],
                                    mstage[0:1, wi_:wi_ + 1],
                                    pb[5][0:1, 0:1], ALU.add),
                                 reads=[Bpb[5], Bmean], writes=[Bmean])
                            yield

                    if coop:
                        k.dmav("p_red", "pool", red_in, mstage[:],
                               reads=[Bmean], writes=[Bred])
                        k.collective("cc_red", "AllReduce", ALU.add, red_in.opt(),
                                     red_out.opt(), reads=[Bred], writes=[Bred])
                        k.dmav("c_mst", "pool", mstage[:], red_out,
                               reads=[Bred], writes=[Bmean])
                    k.op("dve", lambda e: e.memset(meanbf[:], 0.0), writes=[Bmean])
                    for wi_, w in enumerate(("w1", "w2", "w3")):
                        k.op("dve", lambda e, wi_=wi_, w=w: e.tensor_scalar(
                                meanbf[0:1, wi_:wi_ + 1], mstage[0:1, wi_:wi_ + 1],
                                1.0 / wcfg[w]["nb"], None, ALU.mult),
                             reads=[Bmean], writes=[Bmean])
                    k.op("dve", lambda e: e.tensor_copy(meanf[:], meanbf[:]),
                         reads=[Bmean], writes=[Bmean])
                    yield

                    for wi_, (w, cfg) in enumerate(wcfg.items()):
                        units, ncb, G = cfg["units"], cfg["ncb"], cfg["G"]
                        for u in range(units):
                            k.dmav("c_scg", "pool", scg[0:G, :],
                                   sc_d[w][u].rearrange("(g j) -> g j", j=256),
                                   reads=[BscD[w]], writes=[Bd["scg"]])
                            k.op("pe", lambda e, G=G, wi_=wi_: e.matmul(
                                    pb[7][0:G, 0:1], onesr_f[0:1, 0:G],
                                    meanf[0:1, wi_:wi_ + 1], start=True, stop=True),
                                 reads=[Bmean, Bconst], writes=[Bpb[7]])
                            k.op("dve", lambda e, G=G: e.tensor_copy(
                                    gcol[0:G, 3:4], pb[7][0:G, 0:1]),
                                 reads=[Bpb[7]], writes=[Bd["gcol"]])
                            k.op("dve", lambda e, G=G: e.tensor_scalar(
                                    ctg[0:G, :], scg[0:G, :], gcol[0:G, 3:4], None,
                                    ALU.subtract),
                                 reads=[Bd["scg"], Bd["gcol"]], writes=[Bd["ctg"]])
                            k.op("dve", lambda e, G=G: e.tensor_reduce(
                                    gcol[0:G, 0:1].rearrange("p (a o) -> p a o", o=1),
                                    ctg[0:G, :].rearrange("p (a b) -> p a b", b=256),
                                    AX.X, ALU.max, apply_absolute_value=True),
                                 reads=[Bd["ctg"]], writes=[Bd["gcol"]])
                            k.op("dve", lambda e, G=G: e.reciprocal(
                                    gcol[0:G, 4:5], gcol[0:G, 0:1]),
                                 reads=[Bd["gcol"]], writes=[Bd["gcol"]])
                            k.op("dve", lambda e, G=G: e.tensor_scalar(
                                    gbf[0:G, 0:1], gcol[0:G, 4:5], 128.0, None,
                                    ALU.mult),
                                 reads=[Bd["gcol"]], writes=[Bd["gbf"]])
                            k.op("dve", lambda e, G=G: e.tensor_copy(
                                    gcol[0:G, 1:2], gbf[0:G, 0:1]),
                                 reads=[Bd["gbf"]], writes=[Bd["gcol"]])
                            k.op("dve", lambda e, G=G: e.reciprocal(
                                    gcol[0:G, 2:3], gcol[0:G, 1:2]),
                                 reads=[Bd["gcol"]], writes=[Bd["gcol"]])
                            k.op("dve", lambda e, G=G: e.tensor_scalar(
                                    ctg[0:G, :], ctg[0:G, :], gcol[0:G, 1:2], None,
                                    ALU.mult),
                                 reads=[Bd["gcol"], Bd["ctg"]], writes=[Bd["ctg"]])
                            k.op("dve", lambda e, G=G: e.tensor_scalar(
                                    qg[0:G, :], ctg[0:G, :], MAGIC, -MAGIC,
                                    ALU.add, ALU.add),
                                 reads=[Bd["ctg"]], writes=[Bd["qg"]])
                            k.op("dve", lambda e, G=G: e.tensor_scalar(
                                    qg[0:G, :], qg[0:G, :], -128.0, 127.0,
                                    ALU.max, ALU.min),
                                 reads=[Bd["qg"]], writes=[Bd["qg"]])
                            k.op("dve", lambda e, G=G: e.tensor_scalar(
                                    dqg_bf[0:G, :], qg[0:G, :], gcol[0:G, 2:3], None,
                                    ALU.mult),
                                 reads=[Bd["qg"], Bd["gcol"]], writes=[Bd["dqg_bf"]])
                            k.op("dve", lambda e, G=G: e.tensor_scalar(
                                    dqg_bf[0:G, :], dqg_bf[0:G, :], gcol[0:G, 3:4],
                                    None, ALU.add),
                                 reads=[Bd["dqg_bf"], Bd["gcol"]], writes=[Bd["dqg_bf"]])
                            k.op("dve", lambda e, G=G: e.tensor_copy(
                                    dqg[0:G, :], dqg_bf[0:G, :]),
                                 reads=[Bd["dqg_bf"]], writes=[Bd["dqg"]])
                            k.dmav("c_dqout", "pool",
                                   deq_d[w][u].rearrange("(g j) -> g j", j=256),
                                   dqg[0:G, :],
                                   reads=[Bd["dqg"]], writes=[BdeqD[w]], waw=False)
                            yield

                    pending = []

                    def flush_pending():
                        for f_ in pending:
                            f_()
                        pending.clear()

                    for w, cfg in wcfg.items():
                        src_, units, cols, ncb = (cfg["src"], cfg["units"],
                                                  cfg["cols"], cfg["ncb"])
                        ntile = cfg["ntile"]
                        for u in range(units):
                            k.dmav("c_scl", "pool", scl[:, 0:ncb],
                                   sc_d[w][u].rearrange("(p j) -> p j", j=ncb),
                                   reads=[BscD[w]], writes=[Bd["scl"]])
                            k.op("dve", lambda e, ncb=ncb: e.reciprocal(
                                    rsc[:, 0:ncb], scl[:, 0:ncb]),
                                 reads=[Bd["scl"]], writes=[Bd["rsc"]])
                            k.dmav("c_dqc", "pool", dqc[:, 0:ncb],
                                   deq_d[w][u].rearrange("(p j) -> p j", j=ncb),
                                   reads=[BdeqD[w]], writes=[Bd["dqc"]])
                            qc = cols // NQ
                            qcb = ncb // NQ
                            qt = ntile // NQ
                            for h in range(NQ):
                                flush_pending()
                                k.dmav("c_raw", "pool", raw[:, 0:qc],
                                       src_[u * 128:(u + 1) * 128,
                                            h * qc:(h + 1) * qc],
                                       writes=[Bd["raw"]])
                                k.op("dve", lambda e, h=h, qc=qc, qcb=qcb: e.tensor_tensor(
                                        scaled[:, 0:qc].rearrange("p (a b) -> p a b", b=BLK),
                                        raw[:, 0:qc].rearrange("p (a b) -> p a b", b=BLK),
                                        rsc[:, h * qcb:(h + 1) * qcb].rearrange(
                                            "p (a o) -> p a o", o=1).broadcast_to(
                                            [128, qcb, BLK]),
                                        ALU.mult),
                                     reads=[Bd["raw"], Bd["rsc"]], writes=[Bd["scaled"]])
                                k.op("dve", lambda e, qc=qc: e.memset(acc[:, 0:qc], ACC0),
                                     writes=[Bd["acc"]])
                                k.op("pool", lambda e, qc=qc: e.memset(accp[:, 0:qc], 0.0),
                                     writes=[Bd["accp"]])
                                yield
                                for kk in range(15):
                                    if kk % 3 == 0:
                                        yield
                                    sg = tmpd if (kk % 2 == 0) else tmpp
                                    bsg = Bd["tmpd"] if (kk % 2 == 0) else Bd["tmpp"]
                                    k.op("act", lambda e, kk=kk, qc=qc, sg=sg: e.activation(
                                            sg[:, 0:qc], scaled[:, 0:qc], AF.Sign,
                                            bias=tausb[:, kk:kk + 1], scale=1.0),
                                         reads=[Bd["scaled"], Bconst], writes=[bsg])
                                    if kk in K_POOL:
                                        k.op("dve", lambda e, kk=kk, qc=qc, sg=sg: e.tensor_scalar(
                                                val[:, 0:qc], sg[:, 0:qc],
                                                HDELTA[kk], None, ALU.mult),
                                             reads=[bsg], writes=[Bd["val"]])
                                        k.op("pool", lambda e, qc=qc: e.tensor_tensor(
                                                accp[:, 0:qc], accp[:, 0:qc],
                                                val[:, 0:qc], ALU.add),
                                             reads=[Bd["val"], Bd["accp"]],
                                             writes=[Bd["accp"]])
                                    else:
                                        k.op("dve", lambda e, kk=kk, qc=qc, sg=sg: e.scalar_tensor_tensor(
                                                acc[:, 0:qc], sg[:, 0:qc],
                                                HDELTA[kk], acc[:, 0:qc],
                                                ALU.mult, ALU.add),
                                             reads=[bsg, Bd["acc"]], writes=[Bd["acc"]])
                                k.op("dve", lambda e, qc=qc: e.tensor_tensor(
                                        acc[:, 0:qc], acc[:, 0:qc], accp[:, 0:qc],
                                        ALU.add),
                                     reads=[Bd["accp"], Bd["acc"]], writes=[Bd["acc"]])
                                yield
                                k.op("dve", lambda e, h=h, qc=qc, qcb=qcb: e.tensor_tensor(
                                        val[:, 0:qc].rearrange("p (a b) -> p a b", b=BLK),
                                        acc[:, 0:qc].rearrange("p (a b) -> p a b", b=BLK),
                                        dqc[:, h * qcb:(h + 1) * qcb].rearrange(
                                            "p (a o) -> p a o", o=1).broadcast_to(
                                            [128, qcb, BLK]),
                                        ALU.mult),
                                     reads=[Bd["acc"], Bd["dqc"]], writes=[Bd["val"]])
                                def deferred(w=w, u=u, h=h, qt=qt):
                                    for tq in range(0, qt, 4):
                                        ntq = min(4, qt - tq)
                                        fs = []
                                        for tt in range(ntq):
                                            fs.append(lambda e, t_=tq + tt, tt=tt: e.transpose(
                                                pb[7][:, tt * 128:(tt + 1) * 128],
                                                val[:, t_ * 128:(t_ + 1) * 128],
                                                ident[:]))
                                        k.group("pe", fs, reads=[Bd["val"], Bconst],
                                                writes=[Bpb[7]])
                                        k.op("dve", lambda e, tq=tq, ntq=ntq: e.tensor_copy(
                                                uout[:, tq * 128:tq * 128 + ntq * 128],
                                                pb[7][:, 0:ntq * 128]),
                                             reads=[Bpb[7]], writes=[Bd["uout"]])
                                    dst = agi[w] if coop else wT[w]
                                    dstb = Bagi[w] if coop else BwT[w]
                                    k.dmav("c_uout", "pool",
                                           dst[u * 128:(u + 1) * 128,
                                               h * qt * 128:(h + 1) * qt * 128],
                                           uout[:, 0:qt * 128],
                                           reads=[Bd["uout"]], writes=[dstb], waw=False)
                                pending.append(deferred)
                                yield
                        flush_pending()
                        if coop:
                            k.collective("cc_" + w, "AllGather", ALU.bypass,
                                         agi[w].opt(), wT[w].opt(),
                                         reads=[Bagi[w]], writes=[BwT[w]])

                    snap = k.snap()
                    for b_ in BH + Bw3sv + Bws:
                        b_.rs = b_.rs + list(snap)
                    yield

                dqit = dq_gen()

                def pump(n=1):
                    for _ in range(n):
                        if next(dqit, "DONE") == "DONE":
                            return

                def ln_head(src_bufs, src_sb, dc):
                    bq = dc % 2
                    sl = src_sb[:, dc * C:(dc + 1) * C]
                    k.op("dve", lambda e, sl=sl, bq=bq: e.tensor_copy(
                            xbt[bq][:], sl),
                         reads=[src_bufs[dc]], writes=[Bxbt[bq]])
                    k.op("act", lambda e, sl=sl, bq=bq: e.activation(
                            x2t[bq][:], sl, AF.Square, bias=0.0, scale=1.0),
                         reads=[src_bufs[dc]], writes=[Bx2t[bq]])
                    k.group("pe", [
                        lambda e, bq=bq, dc=dc: e.matmul(
                            pb[6][0:1, :], onesc_bf[:], xbt[bq][:],
                            start=(dc == 0), stop=(dc == NDC - 1)),
                        lambda e, bq=bq, dc=dc: e.matmul(
                            pb[7][0:1, :], onesc_bf[:], x2t[bq][:],
                            start=(dc == 0), stop=(dc == NDC - 1)),
                    ], reads=[Bxbt[bq], Bx2t[bq], Bconst],
                       writes=[Bpb[6], Bpb[7]])

                def emit_ln(src_bufs, src_sb, dst_bufs, dst_sb, skip_head=False):
                    mrow = rows2[0:1, 0:C]
                    vrow = rows2[0:1, C:2 * C]
                    trow = smalls[0:1, :]
                    rrow = vrow
                    if not skip_head:
                        for dc in range(NDC):
                            ln_head(src_bufs, src_sb, dc)
                    k.op("dve", lambda e: e.tensor_scalar(
                            mrow, pb[6][0:1, :], 1.0 / D, None, ALU.mult),
                         reads=[Bpb[6]], writes=[Brows])
                    k.op("dve", lambda e: e.tensor_scalar(
                            vrow, pb[7][0:1, :], 1.0 / D, None, ALU.mult),
                         reads=[Bpb[7]], writes=[Brows])
                    k.op("dve", lambda e: e.tensor_tensor(
                            trow, mrow, mrow, ALU.mult),
                         reads=[Brows], writes=[Brows])
                    k.op("dve", lambda e: e.tensor_scalar(
                            trow, trow, -1.0, EPS, ALU.mult, ALU.add),
                         reads=[Brows], writes=[Brows])
                    k.op("dve", lambda e: e.tensor_tensor(
                            vrow, vrow, trow, ALU.add),
                         reads=[Brows], writes=[Brows])
                    k.op("act", lambda e: e.activation(
                            trow, vrow, AF.Sqrt, bias=0.0, scale=1.0),
                         reads=[Brows], writes=[Brows])
                    k.op("dve", lambda e: e.reciprocal(rrow, trow),
                         reads=[Brows], writes=[Brows])
                    k.op("pe", lambda e: e.matmul(
                            pb[6][:], onesr_f[0:1, :], mrow, start=True, stop=True),
                         reads=[Brows, Bconst], writes=[Bpb[6]])
                    k.op("pe", lambda e: e.matmul(
                            pb[7][:], onesr_f[0:1, :], rrow, start=True, stop=True),
                         reads=[Brows, Bconst], writes=[Bpb[7]])
                    k.op("dve", lambda e: e.tensor_copy(mbc[:], pb[6][:]),
                         reads=[Bpb[6]], writes=[Bmbc])
                    k.op("dve", lambda e: e.tensor_copy(rbc[:], pb[7][:]),
                         reads=[Bpb[7]], writes=[Brbc])
                    for dc in range(NDC):
                        en = "pool" if dc % 3 == 2 else "dve"
                        bq = dc % 2 + (2 if en == "pool" else 0)
                        sl = src_sb[:, dc * C:(dc + 1) * C]
                        dl = dst_sb[:, dc * C:(dc + 1) * C]
                        k.op(en, lambda e, sl=sl, bq=bq: e.tensor_tensor(
                                gt[bq][:], sl, mbc[:], ALU.subtract),
                             reads=[src_bufs[dc], Bmbc], writes=[Bgt[bq]])
                        k.op(en, lambda e, dl=dl, bq=bq: e.tensor_tensor(
                                dl, gt[bq][:], rbc[:], ALU.mult),
                             reads=[Bgt[bq], Brbc], writes=[dst_bufs[dc]])

                WNUM = {"wi": 0, "wf": 1, "wg": 2, "wo": 3}

                def load_wave(nm, wv, wsl):
                    b = WNUM[nm] * 8 + wv
                    return k.dmav(f"c_ws{wsl}", "sync", ws[wsl],
                                  blk_all[b * 128:(b + 1) * 128, :],
                                  reads=[Bblk], writes=[Bws[wsl]])

                def emit_attn(ch, bounce):
                    seq = ch // CH_PER_SEQ
                    t0 = (ch % CH_PER_SEQ) * C
                    first = (ch % CH_PER_SEQ) == 0
                    pump_here = ch < KB

                    for dc in range(NDC):
                        k.dmav(f"c_x{dc}", "sync",
                               X[:, dc * C:(dc + 1) * C],
                               x_in[seq, dc * 128:(dc + 1) * 128, t0:t0 + C],
                               reads=[Bx_dram], writes=[BX[dc]])

                    emit_ln(BX, X, BXN, XN)

                    for wv in range(8):
                        if pump_here:
                            pump(1)
                        pf0 = (wv % 2) * 4
                        pi0 = pf0 + 2
                        wf_s = (wv % 2) * 2
                        wi_s = wf_s + 1
                        load_wave("wf", wv, wf_s)
                        load_wave("wi", wv, wi_s)
                        for wsl, pq in ((wf_s, pf0), (wi_s, pi0)):
                            fs = []
                            for dc in range(NDC):
                                for ecl in range(2):
                                    fs.append(lambda e, dc=dc, ecl=ecl, wsl=wsl,
                                              pq=pq: e.matmul(
                                        pb[pq + ecl][:],
                                        ws[wsl][:, dc * 256 + ecl * 128:
                                                dc * 256 + (ecl + 1) * 128],
                                        XN[:, dc * C:(dc + 1) * C],
                                        start=(dc == 0), stop=(dc == NDC - 1)))
                            k.group("pe", fs, reads=[Bws[wsl]] + BXN,
                                    writes=[Bpb[pq], Bpb[pq + 1]])
                        for ecl in range(2):
                            ec = 2 * wv + ecl
                            pf = pf0 + ecl
                            pi_ = pi0 + ecl
                            ta, tnf, tsi = gt[2 * ecl], gt[4], gt[2 * ecl + 1]
                            ga, gnf, gsi = Bgt[2 * ecl], Bgt[4], Bgt[2 * ecl + 1]
                            k.op("act", lambda e, pf=pf, ta=ta: e.activation(
                                    ta[:], pb[pf][:], AF.Sigmoid, bias=0.0, scale=1.0),
                                 reads=[Bpb[pf]], writes=[ga])
                            k.op("act", lambda e, pf=pf, tnf=tnf: e.activation(
                                    tnf[:], pb[pf][:], AF.Sigmoid, bias=0.0, scale=-1.0),
                                 reads=[Bpb[pf]], writes=[gnf])
                            k.op("act", lambda e, pi_=pi_, tsi=tsi: e.activation(
                                    tsi[:], pb[pi_][:], AF.Silu, bias=0.0, scale=1.0),
                                 reads=[Bpb[pi_]], writes=[gsi])
                            k.op("dve", lambda e, tsi=tsi, tnf=tnf: e.tensor_tensor(
                                    tsi[:], tsi[:], tnf[:], ALU.mult),
                                 reads=[gnf, gsi], writes=[gsi])
                            init = 0.0 if first else carry[:, ec:ec + 1]
                            rd = [ga, gsi] + ([] if first else [Bcar[ec]])
                            k.op("dve", lambda e, ec=ec, ta=ta, tsi=tsi, init=init:
                                    e.tensor_tensor_scan(
                                        O[:, ec * C:(ec + 1) * C], ta[:], tsi[:],
                                        init, ALU.mult, ALU.add),
                                 reads=rd, writes=[BO[ec]])
                            k.op("dve", lambda e, ec=ec: e.tensor_copy(
                                    carry[:, ec:ec + 1],
                                    O[:, ec * C + C - 1:ec * C + C]),
                                 reads=[BO[ec]], writes=[Bcar[ec]])
                            if pump_here:
                                pump(2)

                    for ec in range(NDC):
                        bq = ec % 2
                        k.op("act", lambda e, ec=ec, bq=bq: e.activation(
                                x2t[bq][:], O[:, ec * C:(ec + 1) * C],
                                AF.Square, bias=0.0, scale=1.0),
                             reads=[BO[ec]], writes=[Bx2t[bq]])
                        k.op("pe", lambda e, ec=ec, bq=bq: e.matmul(
                                pb[6][0:1, :], onesc_bf[:], x2t[bq][:],
                                start=(ec == 0), stop=(ec == NDC - 1)),
                             reads=[Bx2t[bq], Bconst], writes=[Bpb[6]])
                    trow = smalls[0:1, :]
                    rrow = rows2[0:1, C:2 * C]
                    k.op("dve", lambda e: e.tensor_scalar(
                            trow, pb[6][0:1, :], 1.0 / D, EPS, ALU.mult, ALU.add),
                         reads=[Bpb[6]], writes=[Brows])
                    k.op("act", lambda e: e.activation(
                            trow, trow, AF.Sqrt, bias=0.0, scale=1.0),
                         reads=[Brows], writes=[Brows])
                    k.op("dve", lambda e: e.reciprocal(rrow, trow),
                         reads=[Brows], writes=[Brows])
                    k.op("pe", lambda e: e.matmul(
                            pb[7][:], onesr_f[0:1, :], rrow, start=True, stop=True),
                         reads=[Brows, Bconst], writes=[Bpb[7]])
                    k.op("dve", lambda e: e.tensor_copy(rbc[:], pb[7][:]),
                         reads=[Bpb[7]], writes=[Brbc])

                    for wv in range(8):
                        if pump_here:
                            pump(1)
                        pg0 = (wv % 2) * 4
                        wsl = (wv % 2) * 2
                        load_wave("wg", wv, wsl)
                        fs = []
                        for dc in range(NDC):
                            for ecl in range(2):
                                fs.append(lambda e, dc=dc, ecl=ecl, wsl=wsl,
                                          pg0=pg0: e.matmul(
                                    pb[pg0 + ecl][:],
                                    ws[wsl][:, dc * 256 + ecl * 128:
                                            dc * 256 + (ecl + 1) * 128],
                                    XN[:, dc * C:(dc + 1) * C],
                                    start=(dc == 0), stop=(dc == NDC - 1)))
                        k.group("pe", fs, reads=[Bws[wsl]] + BXN,
                                writes=[Bpb[pg0], Bpb[pg0 + 1]])
                        for ecl in range(2):
                            ec = 2 * wv + ecl
                            tsg, ttmp = gt[2 * ecl], gt[2 * ecl + 1]
                            gsg, gtmp = Bgt[2 * ecl], Bgt[2 * ecl + 1]
                            k.op("act", lambda e, p_=pg0 + ecl, tsg=tsg: e.activation(
                                    tsg[:], pb[p_][:], AF.Silu, bias=0.0, scale=1.0),
                                 reads=[Bpb[pg0 + ecl]], writes=[gsg])
                            k.op("dve", lambda e, ec=ec, ttmp=ttmp:
                                    e.tensor_scalar(
                                        ttmp[:], O[:, ec * C:(ec + 1) * C],
                                        gamma[:, ec:ec + 1], None, ALU.mult),
                                 reads=[BO[ec], Bconst], writes=[gtmp])
                            k.op("pool", lambda e, tsg=tsg, ttmp=ttmp:
                                    e.tensor_tensor(
                                        ttmp[:], ttmp[:], tsg[:], ALU.mult),
                                 reads=[gsg, gtmp], writes=[gtmp])
                            k.op("dve", lambda e, ec=ec, ttmp=ttmp: e.tensor_tensor(
                                    O2[:, ec * C:(ec + 1) * C], ttmp[:], rbc[:],
                                    ALU.mult),
                                 reads=[gtmp, Brbc], writes=[BO2[ec]])
                            if pump_here:
                                pump(2)

                    for wv in range(8):
                        po0 = (wv % 2) * 4
                        wsl = (wv % 2) * 2 + 1
                        load_wave("wo", wv, wsl)
                        fs = []
                        for dc in range(NDC):
                            for ecl in range(2):
                                fs.append(lambda e, dc=dc, ecl=ecl, wsl=wsl,
                                          po0=po0: e.matmul(
                                    pb[po0 + ecl][:],
                                    ws[wsl][:, dc * 256 + ecl * 128:
                                            dc * 256 + (ecl + 1) * 128],
                                    O2[:, dc * C:(dc + 1) * C],
                                    start=(dc == 0), stop=(dc == NDC - 1)))
                        k.group("pe", fs, reads=[Bws[wsl]] + BO2,
                                writes=[Bpb[po0], Bpb[po0 + 1]])
                        for ecl in range(2):
                            ec = 2 * wv + ecl
                            k.op("dve", lambda e, ec=ec, p_=po0 + ecl: e.tensor_tensor(
                                    X[:, ec * C:(ec + 1) * C],
                                    X[:, ec * C:(ec + 1) * C],
                                    pb[p_][:], ALU.add),
                                 reads=[Bpb[po0 + ecl]], writes=[BX[ec]])

                    if bounce:
                        k.dmav("p_xb", "pool",
                               xb_dram[ch * 128:(ch + 1) * 128, :], X[:],
                               reads=BX, writes=[Bxb[ch]])

                def emit_ffn(ch, bounced):
                    seq = ch // CH_PER_SEQ
                    t0 = (ch % CH_PER_SEQ) * C

                    if bounced:
                        for dc in range(NDC):
                            k.dmav(f"c_x{dc}", "sync",
                                   X[:, dc * C:(dc + 1) * C],
                                   xb_dram[ch * 128:(ch + 1) * 128,
                                           dc * C:(dc + 1) * C],
                                   reads=[Bxb[ch]], writes=[BX[dc]])

                    emit_ln(BX, X, BXN, XN)

                    for wv in range(22):
                        pa0 = (wv % 2) * 4
                        pb0_ = pa0 + 2
                        ws1 = (wv % 2) * 2
                        ws2 = ws1 + 1
                        u0 = 2 * wv
                        k.dmav(f"c_ws{ws1}", "sync",
                               ws[ws1].rearrange("p (a f) -> p a f", a=2),
                               wT["w1"][u0 * 128:(u0 + 2) * 128, :].rearrange(
                                   "(a p) f -> p a f", p=128),
                               reads=[BwT["w1"]], writes=[Bws[ws1]])
                        k.dmav(f"c_ws{ws2}", "sync",
                               ws[ws2].rearrange("p (a f) -> p a f", a=2),
                               wT["w2"][u0 * 128:(u0 + 2) * 128, :].rearrange(
                                   "(a p) f -> p a f", p=128),
                               reads=[BwT["w2"]], writes=[Bws[ws2]])
                        for wsl, pq in ((ws1, pa0), (ws2, pb0_)):
                            fs = []
                            for dc in range(NDC):
                                for fcl in range(2):
                                    fs.append(lambda e, dc=dc, fcl=fcl, wsl=wsl,
                                              pq=pq: e.matmul(
                                        pb[pq + fcl][:],
                                        ws[wsl][:, fcl * 2048 + dc * 128:
                                                fcl * 2048 + (dc + 1) * 128],
                                        XN[:, dc * C:(dc + 1) * C],
                                        start=(dc == 0), stop=(dc == NDC - 1)))
                            k.group("pe", fs, reads=[Bws[wsl]] + BXN,
                                    writes=[Bpb[pq], Bpb[pq + 1]])
                        for fcl in range(2):
                            fc = 2 * wv + fcl
                            pa = pa0 + fcl
                            pbb = pb0_ + fcl
                            tsa = gt[2 * fcl]
                            k.op("act", lambda e, pa=pa, tsa=tsa: e.activation(
                                    tsa[:], pb[pa][:], AF.Silu, bias=0.0, scale=1.0),
                                 reads=[Bpb[pa]], writes=[Bgt[2 * fcl]])
                            k.op("dve", lambda e, fc=fc, tsa=tsa, pbb=pbb:
                                    e.tensor_tensor(
                                        H[:, fc * C:(fc + 1) * C], tsa[:],
                                        pb[pbb][:], ALU.mult),
                                 reads=[Bgt[2 * fcl], Bpb[pbb]], writes=[BH[fc]])

                    w3alias = [[Bw3sv[0], Bws[0], Bws[1]],
                               [Bw3sv[1], Bws[2], Bws[3]]]
                    for dc in range(NDC):
                        pv = pb[dc % 2]
                        sl3 = dc % 2
                        k.dmav(f"c_w3s{sl3}", "sync", w3sv[sl3],
                               wT["w3"][dc * 128:(dc + 1) * 128, :],
                               reads=[BwT["w3"]], writes=w3alias[sl3])
                        fs = []
                        for fc in range(NFC):
                            fs.append(lambda e, fc=fc, pv=pv, sl3=sl3: e.matmul(
                                pv[:], w3sv[sl3][:, fc * 128:(fc + 1) * 128],
                                H[:, fc * C:(fc + 1) * C],
                                start=(fc == 0), stop=(fc == NFC - 1)))
                        k.group("pe", fs, reads=w3alias[sl3] + BH,
                                writes=[Bpb[dc % 2]])
                        k.op("dve", lambda e, dc=dc, pv=pv: e.tensor_tensor(
                                X[:, dc * C:(dc + 1) * C],
                                X[:, dc * C:(dc + 1) * C], pv[:], ALU.add),
                             reads=[Bpb[dc % 2]], writes=[BX[dc]])

                    k.dmav("p_y", "pool",
                           y_out[seq, :, t0:t0 + C].rearrange(
                               "(dc p) t -> p dc t", p=128),
                           X[:].rearrange("p (dc t) -> p dc t", t=C),
                           reads=BX, writes=[By_dram], waw=False)

                for ch in range(KB):
                    emit_attn(ch, True)
                pump(10 ** 6)
                for ch in range(KB):
                    emit_ffn(ch, True)
                for ch in range(KB, NCH):
                    emit_attn(ch, False)
                    emit_ffn(ch, False)

                dy = k.dstream("p_y")
                k.eng["pool"].e.wait_ge(dy.sem, dy.n)
                k.eng["sync"].e.wait_ge(dy.sem, dy.n)

    return nc


_NC_CACHE = {}


def _get_nc(coop=COOP):
    if coop not in _NC_CACHE:
        _NC_CACHE[coop] = _build(coop)
    return _NC_CACHE[coop]


def kernel(L0_attn_hidden_states, L0_attn_Wi, L0_attn_Wf, L0_attn_Wg,
           L0_attn_Wo, L0_attn_g_norm_weight, L0_ffn_weight1, L0_ffn_weight2,
           L0_ffn_weight3):
    hs = np.asarray(L0_attn_hidden_states)
    wlist = [np.asarray(w, dtype=np.float32) for w in
             (L0_attn_Wi, L0_attn_Wf, L0_attn_Wg, L0_attn_Wo)]
    blk_imgs = []
    for w in wlist:
        a4 = w.reshape(NDC, 128, 8, 256)
        for wv in range(8):
            blk_imgs.append(np.ascontiguousarray(
                a4[:, :, wv, :].transpose(1, 0, 2).reshape(128, NDC * 256)))
    gnw = np.asarray(L0_attn_g_norm_weight, dtype=np.float32)
    w1 = np.asarray(L0_ffn_weight1)
    w2 = np.asarray(L0_ffn_weight2)
    w3 = np.asarray(L0_ffn_weight3)

    gamma_fm = np.ascontiguousarray(gnw.reshape(NDC, 128).T)
    ident_f32 = np.eye(128, dtype=np.float32)
    onesc_bf = np.ones((128, 1), dtype=ml_dtypes.bfloat16)
    onesc_f = np.ones((128, 1), dtype=np.float32)
    onesr_f = np.ones((128, 128), dtype=np.float32)
    taus_arr = np.zeros((128, 16), dtype=np.float32)
    for kk in range(15):
        taus_arr[:, kk] = np.float32(-TAU[kk])

    nc = _get_nc(COOP)
    in_maps = []
    for c in range(NCORE):
        x_fm = np.ascontiguousarray(
            hs[SEQ * c:SEQ * (c + 1)].transpose(0, 2, 1))
        if COOP:
            my_pc_arr = np.ascontiguousarray(
                np.concatenate(blk_imgs[4 * c:4 * c + 4], axis=0))
            r0 = c * UPC_W1 * 128
            r1 = min(DFF, r0 + UPC_W1 * 128)
            mw1 = np.ones((UPC_W1 * 128, D), dtype=ml_dtypes.bfloat16)
            mw2 = np.ones((UPC_W1 * 128, D), dtype=ml_dtypes.bfloat16)
            if r1 > r0:
                mw1[0:r1 - r0] = w1[r0:r1]
                mw2[0:r1 - r0] = w2[r0:r1]
            msk = np.zeros((128, UPC_W1), dtype=np.float32)
            for ul in range(UPC_W1):
                if (c * UPC_W1 + ul) < W1_UNITS:
                    msk[:, ul] = 1.0
            q0 = c * UPC_W3 * 128
            mw3 = np.ascontiguousarray(w3[q0:q0 + UPC_W3 * 128])
        else:
            my_pc_arr = np.ascontiguousarray(np.concatenate(blk_imgs, axis=0))
            mw1 = np.ascontiguousarray(w1)
            mw2 = np.ascontiguousarray(w2)
            mw3 = np.ascontiguousarray(w3)
            msk = np.ones((128, W1_UNITS), dtype=np.float32)
        in_maps.append({
            "x_in": x_fm,
            "my_pc": my_pc_arr,
            "my_w1": mw1, "my_w2": mw2, "my_w3": mw3, "msk1": msk,
            "in_gamma_fm": gamma_fm, "in_ident_f32": ident_f32,
            "in_onesc_bf": onesc_bf, "in_onesc_f": onesc_f, "in_onesr_f": onesr_f,
            "in_taus": taus_arr,
        })

    res = run_bass_kernel_spmd(nc, in_maps, core_ids=list(range(NCORE)))
    out = np.empty((B, T, D), dtype=np.float32)
    for c in range(NCORE):
        y = np.asarray(res.results[c]["y_out"])
        out[SEQ * c:SEQ * (c + 1)] = y.transpose(0, 2, 1)
    return out


# version 14
# speedup vs baseline: 1.0859x; 1.0387x over previous
import contextlib
import numpy as np
import ml_dtypes
import concourse.bass as bass
import concourse.mybir as mybir
from concourse.bass_utils import run_bass_kernel_spmd

F32 = mybir.dt.float32
BF16 = mybir.dt.bfloat16
AF = mybir.ActivationFunctionType
ALU = mybir.AluOpType
AX = mybir.AxisListType

D = 2048
DFF = 5632
B = 16
T = 2048
NCORE = 8
SEQ = B // NCORE
C = 512
NCH = (SEQ * T) // C
CH_PER_SEQ = T // C
NDC = D // 128
NFC = DFF // 128
EPS = 1e-5
MAGIC = 12582912.0

COOP = True

_NF4_RAW = [-1.0, -0.6962, -0.5251, -0.3949, -0.2844, -0.1848, -0.0911, 0.0,
            0.0796, 0.1609, 0.2461, 0.3379, 0.4407, 0.5626, 0.723, 1.0]
NF4 = np.array(_NF4_RAW, dtype=ml_dtypes.bfloat16).astype(np.float64)
L0 = float(np.float32(NF4[0]))
DELTA = [float(np.float32(NF4[k + 1] - NF4[k])) for k in range(15)]


def _bf16_threshold(mid):
    lo = ml_dtypes.bfloat16(mid)
    if np.float64(lo) > mid:
        up = np.float64(lo)
        dn = np.float64(np.nextafter(lo, ml_dtypes.bfloat16(-np.inf)))
    else:
        dn = np.float64(lo)
        up = np.float64(np.nextafter(lo, ml_dtypes.bfloat16(np.inf)))
    return float(np.float32((dn + up) / 2.0))


TAU = [_bf16_threshold((NF4[k] + NF4[k + 1]) / 2.0) for k in range(15)]

BLK = 64
W1_UNITS = DFF // 128
W3_UNITS = D // 128
NB_W1 = DFF * D // BLK
NB_W3 = D * DFF // BLK
NCB_W1 = D // BLK
NCB_W3 = DFF // BLK
G_W1 = 128 * NCB_W1 // 256
G_W3 = 128 * NCB_W3 // 256
K_POOL = [3, 7, 11, 14]
ACC0 = float(np.float32(np.float32(L0) + np.float32(sum(DELTA) / 2.0)))
HDELTA = [float(np.float32(d / 2.0)) for d in DELTA]
UPC_W1 = 6
UPC_W3 = 2
NQ = 4


class EngS:
    def __init__(self, stack, nc, name, e):
        self.name = name
        self.e = e
        self.sem = stack.enter_context(nc.semaphore("s_" + name))
        self.n = 0
        self.waited = {}


class DmaS:
    def __init__(self, stack, nc, name):
        self.name = name
        self.sem = stack.enter_context(nc.semaphore(name))
        self.n = 0


class Buf:
    __slots__ = ("name", "w", "rs")

    def __init__(self, name):
        self.name = name
        self.w = []
        self.rs = []


class K:
    def __init__(self, nc, stack):
        self.nc = nc
        self.eng = {
            "pe": EngS(stack, nc, "pe", nc.tensor),
            "dve": EngS(stack, nc, "dve", nc.vector),
            "act": EngS(stack, nc, "act", nc.scalar),
            "pool": EngS(stack, nc, "pool", nc.gpsimd),
            "sync": EngS(stack, nc, "sync", nc.sync),
        }
        self.stack = stack
        self.dmas = {}
        self.ccs = {}

    def dstream(self, name):
        if name not in self.dmas:
            self.dmas[name] = DmaS(self.stack, self.nc, name)
        return self.dmas[name]

    def ccstream(self, name):
        if name not in self.ccs:
            self.ccs[name] = DmaS(self.stack, self.nc, name)
        return self.ccs[name]

    def _semh(self, s):
        if s.startswith("s_"):
            return self.eng[s[2:]].sem
        if s in self.dmas:
            return self.dmas[s].sem
        if s in self.ccs:
            return self.ccs[s].sem
        raise KeyError(s)

    def _waits(self, en, toks):
        e = self.eng[en]
        need = {}
        for t in toks:
            if t is None:
                continue
            s, v = t
            if need.get(s, -1) < v:
                need[s] = v
        for s, v in need.items():
            if e.waited.get(s, -1) >= v:
                continue
            e.e.wait_ge(self._semh(s), v)
            e.waited[s] = v

    @staticmethod
    def _dep(reads, writes, waw=True):
        toks = []
        for b in reads:
            toks.extend(b.w)
        for b in writes:
            if waw:
                toks.extend(b.w)
            toks.extend(b.rs)
        return toks

    @staticmethod
    def _rec(t, reads, writes, waw=True):
        for b in writes:
            if waw:
                b.w = [t]
                b.rs = []
            else:
                b.w.append(t)
        for b in reads:
            b.rs.append(t)

    def op(self, en, f, reads=(), writes=()):
        e = self.eng[en]
        self._waits(en, self._dep(reads, writes))
        ins = f(e.e)
        e.n += 1
        ins.then_inc(e.sem, 1)
        t = ("s_" + en, e.n)
        self._rec(t, reads, writes)
        return t

    def group(self, en, fs, reads=(), writes=()):
        e = self.eng[en]
        self._waits(en, self._dep(reads, writes))
        last = None
        for f in fs:
            last = f(e.e)
        e.n += 1
        last.then_inc(e.sem, 1)
        t = ("s_" + en, e.n)
        self._rec(t, reads, writes)
        return t

    def dmav(self, stream, q, out_ap, in_ap, reads=(), writes=(), waw=True):
        e = self.eng[q]
        d = self.dstream(stream)
        self._waits(q, self._dep(reads, writes, waw))
        bb = self.nc.m.functions[0].blocks[-1]
        n0 = len(bb.instructions)
        ins = e.e.dma_start(out=out_ap, in_=in_ap)
        n1 = len(bb.instructions)
        pieces = sum(1 for i in range(n0, n1)
                     if "DMACopy" in type(bb.instructions[i]).__name__)
        assert pieces == 1, (stream, n0, n1)
        ins.then_inc(d.sem, 16)
        d.n += 16
        t = (d.name, d.n)
        self._rec(t, reads, writes, waw)
        return t

    def collective(self, stream, kind, alu, in_ap, out_ap, reads=(), writes=()):
        e = self.eng["pool"]
        cs = self.ccstream(stream)
        self._waits("pool", self._dep(reads, writes))
        ins = e.e.collective_compute(
            kind, alu, replica_groups=[list(range(NCORE))],
            ins=[in_ap], outs=[out_ap])
        cs.n += 1
        ins.then_inc(cs.sem)
        t = (cs.name, cs.n)
        self._rec(t, reads, writes)
        return t

    def snap(self):
        toks = [("s_" + n, s.n) for n, s in self.eng.items()]
        toks += [(d.name, d.n) for d in self.dmas.values()]
        toks += [(c.name, c.n) for c in self.ccs.values()]
        return toks


def _build(coop=COOP):
    nc = bass.Bass("TRN2", target_bir_lowering=False)

    x_in = nc.dram_tensor("x_in", [SEQ, D, T], F32, kind="ExternalInput").ap()
    y_out = nc.dram_tensor("y_out", [SEQ, D, T], F32, kind="ExternalOutput").ap()
    npc = 4 if coop else 32
    my_pc = nc.dram_tensor("my_pc", [npc * 128, NDC * 256], F32,
                           kind="ExternalInput").ap()
    nw1 = UPC_W1 if coop else W1_UNITS
    nw3 = UPC_W3 if coop else W3_UNITS
    my_w1 = nc.dram_tensor("my_w1", [nw1 * 128, D], BF16, kind="ExternalInput").ap()
    my_w2 = nc.dram_tensor("my_w2", [nw1 * 128, D], BF16, kind="ExternalInput").ap()
    my_w3 = nc.dram_tensor("my_w3", [nw3 * 128, DFF], BF16, kind="ExternalInput").ap()
    msk1 = nc.dram_tensor("msk1", [128, nw1], F32, kind="ExternalInput").ap()
    gamma_in = nc.dram_tensor("in_gamma_fm", [128, NDC], F32, kind="ExternalInput").ap()
    ident_in = nc.dram_tensor("in_ident_f32", [128, 128], F32, kind="ExternalInput").ap()
    onesc_bf_in = nc.dram_tensor("in_onesc_bf", [128, 1], BF16, kind="ExternalInput").ap()
    onesc_f_in = nc.dram_tensor("in_onesc_f", [128, 1], F32, kind="ExternalInput").ap()
    onesr_f_in = nc.dram_tensor("in_onesr_f", [128, 128], F32, kind="ExternalInput").ap()
    taus_in = nc.dram_tensor("in_taus", [128, 16], F32, kind="ExternalInput").ap()

    blk_all = nc.dram_tensor("blk_all", [32 * 128, NDC * 256], BF16).ap()
    KB = 8
    xb_dram = nc.dram_tensor("xb_dram", [KB * 128, NDC * C], F32).ap()
    if coop:
        agi_pc = nc.dram_tensor("agi_pc", [4 * 128, NDC * 256], BF16).ap()
    else:
        agi_pc = None
    sc_d = {"w1": nc.dram_tensor("sc_w1", [nw1, 128 * NCB_W1], F32).ap(),
            "w2": nc.dram_tensor("sc_w2", [nw1, 128 * NCB_W1], F32).ap(),
            "w3": nc.dram_tensor("sc_w3", [nw3, 128 * NCB_W3], F32).ap()}
    deq_d = {"w1": nc.dram_tensor("deq_w1", [nw1, 128 * NCB_W1], F32).ap(),
             "w2": nc.dram_tensor("deq_w2", [nw1, 128 * NCB_W1], F32).ap(),
             "w3": nc.dram_tensor("deq_w3", [nw3, 128 * NCB_W3], F32).ap()}
    if coop:
        red_in = nc.dram_tensor("red_in", [1, 4], F32).ap()
        red_out = nc.dram_tensor("red_out", [1, 4], F32).ap()
        agi = {"w1": nc.dram_tensor("agi_w1", [UPC_W1 * 128, D], BF16).ap(),
               "w2": nc.dram_tensor("agi_w2", [UPC_W1 * 128, D], BF16).ap(),
               "w3": nc.dram_tensor("agi_w3", [UPC_W3 * 128, DFF], BF16).ap()}
        wT = {"w1": nc.dram_tensor("ago_w1", [NCORE * UPC_W1 * 128, D], BF16).ap(),
              "w2": nc.dram_tensor("ago_w2", [NCORE * UPC_W1 * 128, D], BF16).ap(),
              "w3": nc.dram_tensor("ago_w3", [NCORE * UPC_W3 * 128, DFF], BF16).ap()}
    else:
        agi = None
        red_in = red_out = None
        wT = {"w1": nc.dram_tensor("w1T", [W1_UNITS * 128, D], BF16).ap(),
              "w2": nc.dram_tensor("w2T", [W1_UNITS * 128, D], BF16).ap(),
              "w3": nc.dram_tensor("w3T", [W3_UNITS * 128, DFF], BF16).ap()}

    stack = contextlib.ExitStack()
    with stack:
        k = K(nc, stack)
        sbuf = lambda name, shape, dt: stack.enter_context(nc.sbuf_tensor(name, shape, dt))
        psum = lambda name, shape, dt: stack.enter_context(nc.psum_tensor(name, shape, dt))

        X = sbuf("X", [128, NDC * C], F32)
        XN = sbuf("XN", [128, NDC * C], BF16)
        O = sbuf("O", [128, NDC * C], F32)
        O2 = sbuf("O2", [128, NDC * C], BF16)
        H = sbuf("H", [128, NFC * C], BF16)
        wsbig = sbuf("wsbig", [128, 4 * NDC * 256], BF16)
        ws = [wsbig[:, i * NDC * 256:(i + 1) * NDC * 256] for i in range(4)]
        w3sv = [wsbig[:, 0:NFC * 128],
                wsbig[:, 2 * NDC * 256:2 * NDC * 256 + NFC * 128]]
        w3s = w3sv[0]
        gt = [sbuf(f"gt{i}", [128, C], F32) for i in range(5)]
        xbt = [sbuf(f"xbt{i}", [128, C], BF16) for i in range(2)]
        x2t = [sbuf(f"x2t{i}", [128, C], BF16) for i in range(2)]
        mbc = sbuf("mbc", [128, C], F32)
        rbc = sbuf("rbc", [128, C], F32)
        smalls = sbuf("smalls", [128, C], F32)
        rows2 = sbuf("rows2", [128, 2 * C], F32)
        carry = sbuf("carry", [128, NDC], F32)
        gamma = sbuf("gamma", [128, NDC], F32)
        ident = sbuf("ident", [128, 128], F32)
        onesc_bf = sbuf("onesc_bf", [128, 1], BF16)
        onesc_f = sbuf("onesc_f", [128, 1], F32)
        onesr_f = sbuf("onesr_f", [128, 128], F32)
        mstage = sbuf("mstage", [1, 4], F32)
        meanbf = sbuf("meanbf", [1, 4], BF16)
        meanf = sbuf("meanf", [1, 4], F32)
        mskc = sbuf("mskc", [128, nw1], F32)
        tausb = sbuf("tausb", [128, 16], F32)

        QW3 = NCB_W3 // NQ * BLK
        hof = [0]

        def hview(cols_bf16, dtype=BF16):
            a = hof[0]
            hof[0] += cols_bf16
            assert hof[0] <= NFC * C, "H scratch overflow"
            v = H[:, a:a + cols_bf16]
            return v.bitcast(dtype) if dtype != BF16 else v

        raw = hview(QW3)
        scaled = hview(QW3)
        uout = hview(QW3)
        acc = hview(2 * QW3, F32)
        accp = hview(2 * QW3, F32)
        tmpd = hview(2 * QW3, F32)
        tmpp = hview(2 * QW3, F32)
        val = hview(2 * QW3, F32)
        scl = hview(2 * NCB_W3, F32)
        rsc = hview(2 * NCB_W3, F32)
        dqc = hview(2 * NCB_W3, F32)
        svec = hview(2, F32)

        def hviewp(cols_bf16, dtype, parts):
            a = hof[0]
            hof[0] += cols_bf16
            assert hof[0] <= NFC * C, "H scratch overflow"
            v = H[0:parts, a:a + cols_bf16]
            return v.bitcast(dtype) if dtype != BF16 else v

        pc_in = [hviewp(512, F32, 128) for _ in range(2)]
        pc_out = [hviewp(256, BF16, 128) for _ in range(2)]
        scg = hviewp(512, F32, G_W3)
        ctg = hviewp(256, BF16, G_W3)
        qg = hviewp(512, F32, G_W3)
        dqg_bf = hviewp(256, BF16, G_W3)
        dqg = hviewp(512, F32, G_W3)
        gcol = hviewp(16, F32, G_W3)
        gbf = hviewp(4, BF16, G_W3)

        pbt = [psum(f"pb{i}", [128, C], F32) for i in range(8)]
        pb = pbt

        BX = [Buf(f"X{i}") for i in range(NDC)]
        BXN = [Buf(f"XN{i}") for i in range(NDC)]
        BO = [Buf(f"O{i}") for i in range(NDC)]
        BO2 = [Buf(f"O2{i}") for i in range(NDC)]
        BH = [Buf(f"H{i}") for i in range(NFC)]
        Bw3sv = [Buf("w3sA"), Buf("w3sB")]
        Bw3s = Bw3sv[0]
        Bws = [Buf(f"ws{i}") for i in range(4)]
        Bgt = [Buf(f"gt{i}") for i in range(5)]
        Bxbt = [Buf(f"xbt{i}") for i in range(2)]
        Bx2t = [Buf(f"x2t{i}") for i in range(2)]
        Bmbc, Brbc, Brows = Buf("mbc"), Buf("rbc"), Buf("rows")
        Bcar = [Buf(f"car{i}") for i in range(NDC)]
        Bconst = Buf("const")
        Bmean = Buf("mean")
        Bpb = [Buf(f"pb{i}") for i in range(8)]
        Bd = {nm: Buf("d_" + nm) for nm in
              ("raw", "scl", "rsc", "dqc", "scaled", "acc", "accp", "tmpd",
               "tmpp", "val", "uout", "scg", "ctg", "qg", "dqg_bf", "dqg",
               "gcol", "gbf", "svec")}
        Bpc = [Buf(f"pc{i}") for i in range(2)]
        Bblk = Buf("blk_all")
        Bagi_pc = Buf("agi_pc")
        BscD = {w: Buf("scD_" + w) for w in ("w1", "w2", "w3")}
        BdeqD = {w: Buf("deqD_" + w) for w in ("w1", "w2", "w3")}
        BwT = {w: Buf("wT_" + w) for w in ("w1", "w2", "w3")}
        Bagi = {w: Buf("agi_" + w) for w in ("w1", "w2", "w3")}
        Bred = Buf("red")
        Bx_dram = Buf("x_dram")
        Bxb = [Buf(f"xb{i}") for i in range(8)]
        By_dram = Buf("y_dram")

        with nc.Block() as block:
            @block.sync
            def _(sync):
                k.dmav("c_const", "sync", gamma[:], gamma_in, writes=[Bconst], waw=False)
                k.dmav("c_const", "sync", ident[:], ident_in, writes=[Bconst], waw=False)
                k.dmav("c_const", "sync", onesc_bf[:], onesc_bf_in, writes=[Bconst], waw=False)
                k.dmav("c_const", "sync", onesc_f[:], onesc_f_in, writes=[Bconst], waw=False)
                k.dmav("c_const", "sync", onesr_f[:], onesr_f_in, writes=[Bconst], waw=False)
                k.dmav("c_const", "sync", mskc[:], msk1, writes=[Bconst], waw=False)
                k.dmav("c_const", "sync", tausb[:], taus_in, writes=[Bconst], waw=False)

                pc_dst = agi_pc if coop else blk_all
                pc_dstb = Bagi_pc if coop else Bblk
                pi = 0
                for bl in range(npc):
                    for dc in range(NDC):
                        bq = pi % 2
                        k.dmav(f"c_pc{bq}", "sync", pc_in[bq],
                               my_pc[bl * 128:(bl + 1) * 128,
                                     dc * 256:(dc + 1) * 256],
                               writes=[Bpc[bq]])
                        k.op("act", lambda e, bq=bq: e.copy(pc_out[bq], pc_in[bq]),
                             reads=[Bpc[bq]], writes=[Bpc[bq]])
                        k.dmav(f"c_out{bq}", "sync",
                               pc_dst[bl * 128:(bl + 1) * 128,
                                      dc * 256:(dc + 1) * 256],
                               pc_out[bq],
                               reads=[Bpc[bq]], writes=[pc_dstb], waw=False)
                        pi += 1
                if coop:
                    k.collective("cc_pc", "AllGather", ALU.bypass,
                                 agi_pc.opt(), blk_all.opt(),
                                 reads=[Bagi_pc], writes=[Bblk])

                wcfg = {
                    "w1": dict(src=my_w1, units=nw1, cols=D, ncb=NCB_W1,
                               G=G_W1, nb=NB_W1, ntile=NDC, masked=True),
                    "w2": dict(src=my_w2, units=nw1, cols=D, ncb=NCB_W1,
                               G=G_W1, nb=NB_W1, ntile=NDC, masked=True),
                    "w3": dict(src=my_w3, units=nw3, cols=DFF, ncb=NCB_W3,
                               G=G_W3, nb=NB_W3, ntile=NFC, masked=False),
                }

                def dq_gen():
                    k.op("dve", lambda e: e.memset(mstage[:], 0.0), writes=[Bmean])
                    for wi_, (w, cfg) in enumerate(wcfg.items()):
                        src_, units, cols, ncb = (cfg["src"], cfg["units"],
                                                  cfg["cols"], cfg["ncb"])
                        for u in range(units):
                            qc = cols // NQ
                            qcb = ncb // NQ
                            for h in range(NQ):
                                k.dmav("c_raw", "pool", raw[:, 0:qc],
                                       src_[u * 128:(u + 1) * 128,
                                            h * qc:(h + 1) * qc],
                                       writes=[Bd["raw"]])
                                k.op("dve", lambda e, h=h, qcb=qcb, qc=qc: e.tensor_reduce(
                                        scl[:, h * qcb:(h + 1) * qcb].rearrange(
                                            "p (a o) -> p a o", o=1),
                                        raw[:, 0:qc].rearrange("p (a b) -> p a b", b=BLK),
                                        AX.X, ALU.max, apply_absolute_value=True),
                                     reads=[Bd["raw"]], writes=[Bd["scl"]])
                            k.dmav("c_scout", "pool",
                                   sc_d[w][u].rearrange("(p j) -> p j", j=ncb),
                                   scl[:, 0:ncb],
                                   reads=[Bd["scl"]], writes=[BscD[w]], waw=False)
                            k.op("dve", lambda e, ncb=ncb: e.reduce_sum(
                                    svec.rearrange("p (a o) -> p a o", o=1),
                                    scl[:, 0:ncb].rearrange("p (a b) -> p a b", b=ncb),
                                    AX.X),
                                 reads=[Bd["scl"]], writes=[Bd["svec"]])
                            lhs_ap = (mskc[:, u:u + 1] if cfg["masked"]
                                      else onesc_f[:, 0:1])
                            k.op("pe", lambda e, lhs_ap=lhs_ap: e.matmul(
                                    pb[5][0:1, 0:1], lhs_ap, svec,
                                    start=True, stop=True),
                                 reads=[Bd["svec"], Bconst], writes=[Bpb[5]])
                            k.op("dve", lambda e, wi_=wi_: e.tensor_tensor(
                                    mstage[0:1, wi_:wi_ + 1],
                                    mstage[0:1, wi_:wi_ + 1],
                                    pb[5][0:1, 0:1], ALU.add),
                                 reads=[Bpb[5], Bmean], writes=[Bmean])
                            yield

                    if coop:
                        k.dmav("p_red", "pool", red_in, mstage[:],
                               reads=[Bmean], writes=[Bred])
                        k.collective("cc_red", "AllReduce", ALU.add, red_in.opt(),
                                     red_out.opt(), reads=[Bred], writes=[Bred])
                        k.dmav("c_mst", "pool", mstage[:], red_out,
                               reads=[Bred], writes=[Bmean])
                    k.op("dve", lambda e: e.memset(meanbf[:], 0.0), writes=[Bmean])
                    for wi_, w in enumerate(("w1", "w2", "w3")):
                        k.op("dve", lambda e, wi_=wi_, w=w: e.tensor_scalar(
                                meanbf[0:1, wi_:wi_ + 1], mstage[0:1, wi_:wi_ + 1],
                                1.0 / wcfg[w]["nb"], None, ALU.mult),
                             reads=[Bmean], writes=[Bmean])
                    k.op("dve", lambda e: e.tensor_copy(meanf[:], meanbf[:]),
                         reads=[Bmean], writes=[Bmean])
                    yield

                    for wi_, (w, cfg) in enumerate(wcfg.items()):
                        units, ncb, G = cfg["units"], cfg["ncb"], cfg["G"]
                        for u in range(units):
                            k.dmav("c_scg", "pool", scg[0:G, :],
                                   sc_d[w][u].rearrange("(g j) -> g j", j=256),
                                   reads=[BscD[w]], writes=[Bd["scg"]])
                            k.op("pe", lambda e, G=G, wi_=wi_: e.matmul(
                                    pb[7][0:G, 0:1], onesr_f[0:1, 0:G],
                                    meanf[0:1, wi_:wi_ + 1], start=True, stop=True),
                                 reads=[Bmean, Bconst], writes=[Bpb[7]])
                            k.op("dve", lambda e, G=G: e.tensor_copy(
                                    gcol[0:G, 3:4], pb[7][0:G, 0:1]),
                                 reads=[Bpb[7]], writes=[Bd["gcol"]])
                            k.op("dve", lambda e, G=G: e.tensor_scalar(
                                    ctg[0:G, :], scg[0:G, :], gcol[0:G, 3:4], None,
                                    ALU.subtract),
                                 reads=[Bd["scg"], Bd["gcol"]], writes=[Bd["ctg"]])
                            k.op("dve", lambda e, G=G: e.tensor_reduce(
                                    gcol[0:G, 0:1].rearrange("p (a o) -> p a o", o=1),
                                    ctg[0:G, :].rearrange("p (a b) -> p a b", b=256),
                                    AX.X, ALU.max, apply_absolute_value=True),
                                 reads=[Bd["ctg"]], writes=[Bd["gcol"]])
                            k.op("dve", lambda e, G=G: e.reciprocal(
                                    gcol[0:G, 4:5], gcol[0:G, 0:1]),
                                 reads=[Bd["gcol"]], writes=[Bd["gcol"]])
                            k.op("dve", lambda e, G=G: e.tensor_scalar(
                                    gbf[0:G, 0:1], gcol[0:G, 4:5], 128.0, None,
                                    ALU.mult),
                                 reads=[Bd["gcol"]], writes=[Bd["gbf"]])
                            k.op("dve", lambda e, G=G: e.tensor_copy(
                                    gcol[0:G, 1:2], gbf[0:G, 0:1]),
                                 reads=[Bd["gbf"]], writes=[Bd["gcol"]])
                            k.op("dve", lambda e, G=G: e.reciprocal(
                                    gcol[0:G, 2:3], gcol[0:G, 1:2]),
                                 reads=[Bd["gcol"]], writes=[Bd["gcol"]])
                            k.op("dve", lambda e, G=G: e.tensor_scalar(
                                    ctg[0:G, :], ctg[0:G, :], gcol[0:G, 1:2], None,
                                    ALU.mult),
                                 reads=[Bd["gcol"], Bd["ctg"]], writes=[Bd["ctg"]])
                            k.op("dve", lambda e, G=G: e.tensor_scalar(
                                    qg[0:G, :], ctg[0:G, :], MAGIC, -MAGIC,
                                    ALU.add, ALU.add),
                                 reads=[Bd["ctg"]], writes=[Bd["qg"]])
                            k.op("dve", lambda e, G=G: e.tensor_scalar(
                                    qg[0:G, :], qg[0:G, :], -128.0, 127.0,
                                    ALU.max, ALU.min),
                                 reads=[Bd["qg"]], writes=[Bd["qg"]])
                            k.op("dve", lambda e, G=G: e.tensor_scalar(
                                    dqg_bf[0:G, :], qg[0:G, :], gcol[0:G, 2:3], None,
                                    ALU.mult),
                                 reads=[Bd["qg"], Bd["gcol"]], writes=[Bd["dqg_bf"]])
                            k.op("dve", lambda e, G=G: e.tensor_scalar(
                                    dqg_bf[0:G, :], dqg_bf[0:G, :], gcol[0:G, 3:4],
                                    None, ALU.add),
                                 reads=[Bd["dqg_bf"], Bd["gcol"]], writes=[Bd["dqg_bf"]])
                            k.op("dve", lambda e, G=G: e.tensor_copy(
                                    dqg[0:G, :], dqg_bf[0:G, :]),
                                 reads=[Bd["dqg_bf"]], writes=[Bd["dqg"]])
                            k.dmav("c_dqout", "pool",
                                   deq_d[w][u].rearrange("(g j) -> g j", j=256),
                                   dqg[0:G, :],
                                   reads=[Bd["dqg"]], writes=[BdeqD[w]], waw=False)
                            yield

                    pending = []

                    def flush_pending():
                        for f_ in pending:
                            f_()
                        pending.clear()

                    for w, cfg in wcfg.items():
                        src_, units, cols, ncb = (cfg["src"], cfg["units"],
                                                  cfg["cols"], cfg["ncb"])
                        ntile = cfg["ntile"]
                        for u in range(units):
                            k.dmav("c_scl", "pool", scl[:, 0:ncb],
                                   sc_d[w][u].rearrange("(p j) -> p j", j=ncb),
                                   reads=[BscD[w]], writes=[Bd["scl"]])
                            k.op("dve", lambda e, ncb=ncb: e.reciprocal(
                                    rsc[:, 0:ncb], scl[:, 0:ncb]),
                                 reads=[Bd["scl"]], writes=[Bd["rsc"]])
                            k.dmav("c_dqc", "pool", dqc[:, 0:ncb],
                                   deq_d[w][u].rearrange("(p j) -> p j", j=ncb),
                                   reads=[BdeqD[w]], writes=[Bd["dqc"]])
                            qc = cols // NQ
                            qcb = ncb // NQ
                            qt = ntile // NQ
                            for h in range(NQ):
                                flush_pending()
                                k.dmav("c_raw", "pool", raw[:, 0:qc],
                                       src_[u * 128:(u + 1) * 128,
                                            h * qc:(h + 1) * qc],
                                       writes=[Bd["raw"]])
                                k.op("dve", lambda e, h=h, qc=qc, qcb=qcb: e.tensor_tensor(
                                        scaled[:, 0:qc].rearrange("p (a b) -> p a b", b=BLK),
                                        raw[:, 0:qc].rearrange("p (a b) -> p a b", b=BLK),
                                        rsc[:, h * qcb:(h + 1) * qcb].rearrange(
                                            "p (a o) -> p a o", o=1).broadcast_to(
                                            [128, qcb, BLK]),
                                        ALU.mult),
                                     reads=[Bd["raw"], Bd["rsc"]], writes=[Bd["scaled"]])
                                k.op("dve", lambda e, qc=qc: e.memset(acc[:, 0:qc], ACC0),
                                     writes=[Bd["acc"]])
                                k.op("pool", lambda e, qc=qc: e.memset(accp[:, 0:qc], 0.0),
                                     writes=[Bd["accp"]])
                                yield
                                for kk in range(15):
                                    yield
                                    sg = tmpd if (kk % 2 == 0) else tmpp
                                    bsg = Bd["tmpd"] if (kk % 2 == 0) else Bd["tmpp"]
                                    k.op("act", lambda e, kk=kk, qc=qc, sg=sg: e.activation(
                                            sg[:, 0:qc], scaled[:, 0:qc], AF.Sign,
                                            bias=tausb[:, kk:kk + 1], scale=1.0),
                                         reads=[Bd["scaled"], Bconst], writes=[bsg])
                                    if kk in K_POOL:
                                        k.op("dve", lambda e, kk=kk, qc=qc, sg=sg: e.tensor_scalar(
                                                val[:, 0:qc], sg[:, 0:qc],
                                                HDELTA[kk], None, ALU.mult),
                                             reads=[bsg], writes=[Bd["val"]])
                                        k.op("pool", lambda e, qc=qc: e.tensor_tensor(
                                                accp[:, 0:qc], accp[:, 0:qc],
                                                val[:, 0:qc], ALU.add),
                                             reads=[Bd["val"], Bd["accp"]],
                                             writes=[Bd["accp"]])
                                    else:
                                        k.op("dve", lambda e, kk=kk, qc=qc, sg=sg: e.scalar_tensor_tensor(
                                                acc[:, 0:qc], sg[:, 0:qc],
                                                HDELTA[kk], acc[:, 0:qc],
                                                ALU.mult, ALU.add),
                                             reads=[bsg, Bd["acc"]], writes=[Bd["acc"]])
                                k.op("dve", lambda e, qc=qc: e.tensor_tensor(
                                        acc[:, 0:qc], acc[:, 0:qc], accp[:, 0:qc],
                                        ALU.add),
                                     reads=[Bd["accp"], Bd["acc"]], writes=[Bd["acc"]])
                                yield
                                k.op("dve", lambda e, h=h, qc=qc, qcb=qcb: e.tensor_tensor(
                                        val[:, 0:qc].rearrange("p (a b) -> p a b", b=BLK),
                                        acc[:, 0:qc].rearrange("p (a b) -> p a b", b=BLK),
                                        dqc[:, h * qcb:(h + 1) * qcb].rearrange(
                                            "p (a o) -> p a o", o=1).broadcast_to(
                                            [128, qcb, BLK]),
                                        ALU.mult),
                                     reads=[Bd["acc"], Bd["dqc"]], writes=[Bd["val"]])
                                def deferred(w=w, u=u, h=h, qt=qt):
                                    for tq in range(0, qt, 4):
                                        ntq = min(4, qt - tq)
                                        fs = []
                                        for tt in range(ntq):
                                            fs.append(lambda e, t_=tq + tt, tt=tt: e.transpose(
                                                pb[7][:, tt * 128:(tt + 1) * 128],
                                                val[:, t_ * 128:(t_ + 1) * 128],
                                                ident[:]))
                                        k.group("pe", fs, reads=[Bd["val"], Bconst],
                                                writes=[Bpb[7]])
                                        k.op("dve", lambda e, tq=tq, ntq=ntq: e.tensor_copy(
                                                uout[:, tq * 128:tq * 128 + ntq * 128],
                                                pb[7][:, 0:ntq * 128]),
                                             reads=[Bpb[7]], writes=[Bd["uout"]])
                                    dst = agi[w] if coop else wT[w]
                                    dstb = Bagi[w] if coop else BwT[w]
                                    k.dmav("c_uout", "pool",
                                           dst[u * 128:(u + 1) * 128,
                                               h * qt * 128:(h + 1) * qt * 128],
                                           uout[:, 0:qt * 128],
                                           reads=[Bd["uout"]], writes=[dstb], waw=False)
                                pending.append(deferred)
                                yield
                        flush_pending()
                        if coop:
                            k.collective("cc_" + w, "AllGather", ALU.bypass,
                                         agi[w].opt(), wT[w].opt(),
                                         reads=[Bagi[w]], writes=[BwT[w]])

                    snap = k.snap()
                    for b_ in BH + Bw3sv + Bws:
                        b_.rs = b_.rs + list(snap)
                    yield

                dqit = dq_gen()

                def pump(n=1):
                    for _ in range(n):
                        if next(dqit, "DONE") == "DONE":
                            return

                def ln_head(src_bufs, src_sb, dc):
                    bq = dc % 2
                    sl = src_sb[:, dc * C:(dc + 1) * C]
                    k.op("dve", lambda e, sl=sl, bq=bq: e.tensor_copy(
                            xbt[bq][:], sl),
                         reads=[src_bufs[dc]], writes=[Bxbt[bq]])
                    k.op("act", lambda e, sl=sl, bq=bq: e.activation(
                            x2t[bq][:], sl, AF.Square, bias=0.0, scale=1.0),
                         reads=[src_bufs[dc]], writes=[Bx2t[bq]])
                    k.group("pe", [
                        lambda e, bq=bq, dc=dc: e.matmul(
                            pb[6][0:1, :], onesc_bf[:], xbt[bq][:],
                            start=(dc == 0), stop=(dc == NDC - 1)),
                        lambda e, bq=bq, dc=dc: e.matmul(
                            pb[7][0:1, :], onesc_bf[:], x2t[bq][:],
                            start=(dc == 0), stop=(dc == NDC - 1)),
                    ], reads=[Bxbt[bq], Bx2t[bq], Bconst],
                       writes=[Bpb[6], Bpb[7]])

                def emit_ln(src_bufs, src_sb, dst_bufs, dst_sb, skip_head=False):
                    mrow = rows2[0:1, 0:C]
                    vrow = rows2[0:1, C:2 * C]
                    trow = smalls[0:1, :]
                    rrow = vrow
                    if not skip_head:
                        for dc in range(NDC):
                            ln_head(src_bufs, src_sb, dc)
                    k.op("dve", lambda e: e.tensor_scalar(
                            mrow, pb[6][0:1, :], 1.0 / D, None, ALU.mult),
                         reads=[Bpb[6]], writes=[Brows])
                    k.op("dve", lambda e: e.tensor_scalar(
                            vrow, pb[7][0:1, :], 1.0 / D, None, ALU.mult),
                         reads=[Bpb[7]], writes=[Brows])
                    k.op("dve", lambda e: e.tensor_tensor(
                            trow, mrow, mrow, ALU.mult),
                         reads=[Brows], writes=[Brows])
                    k.op("dve", lambda e: e.tensor_scalar(
                            trow, trow, -1.0, EPS, ALU.mult, ALU.add),
                         reads=[Brows], writes=[Brows])
                    k.op("dve", lambda e: e.tensor_tensor(
                            vrow, vrow, trow, ALU.add),
                         reads=[Brows], writes=[Brows])
                    k.op("act", lambda e: e.activation(
                            trow, vrow, AF.Sqrt, bias=0.0, scale=1.0),
                         reads=[Brows], writes=[Brows])
                    k.op("dve", lambda e: e.reciprocal(rrow, trow),
                         reads=[Brows], writes=[Brows])
                    k.op("pe", lambda e: e.matmul(
                            pb[6][:], onesr_f[0:1, :], mrow, start=True, stop=True),
                         reads=[Brows, Bconst], writes=[Bpb[6]])
                    k.op("pe", lambda e: e.matmul(
                            pb[7][:], onesr_f[0:1, :], rrow, start=True, stop=True),
                         reads=[Brows, Bconst], writes=[Bpb[7]])
                    k.op("dve", lambda e: e.tensor_copy(mbc[:], pb[6][:]),
                         reads=[Bpb[6]], writes=[Bmbc])
                    k.op("dve", lambda e: e.tensor_copy(rbc[:], pb[7][:]),
                         reads=[Bpb[7]], writes=[Brbc])
                    for dc in range(NDC):
                        en = "pool" if dc % 3 == 2 else "dve"
                        bq = dc % 2 + (2 if en == "pool" else 0)
                        sl = src_sb[:, dc * C:(dc + 1) * C]
                        dl = dst_sb[:, dc * C:(dc + 1) * C]
                        k.op(en, lambda e, sl=sl, bq=bq: e.tensor_tensor(
                                gt[bq][:], sl, mbc[:], ALU.subtract),
                             reads=[src_bufs[dc], Bmbc], writes=[Bgt[bq]])
                        k.op(en, lambda e, dl=dl, bq=bq: e.tensor_tensor(
                                dl, gt[bq][:], rbc[:], ALU.mult),
                             reads=[Bgt[bq], Brbc], writes=[dst_bufs[dc]])

                WNUM = {"wi": 0, "wf": 1, "wg": 2, "wo": 3}

                def load_wave(nm, wv, wsl):
                    b = WNUM[nm] * 8 + wv
                    return k.dmav(f"c_ws{wsl}", "sync", ws[wsl],
                                  blk_all[b * 128:(b + 1) * 128, :],
                                  reads=[Bblk], writes=[Bws[wsl]])

                def emit_attn(ch, bounce):
                    seq = ch // CH_PER_SEQ
                    t0 = (ch % CH_PER_SEQ) * C
                    first = (ch % CH_PER_SEQ) == 0
                    pump_here = ch < KB

                    for dc in range(NDC):
                        k.dmav(f"c_x{dc}", "sync",
                               X[:, dc * C:(dc + 1) * C],
                               x_in[seq, dc * 128:(dc + 1) * 128, t0:t0 + C],
                               reads=[Bx_dram], writes=[BX[dc]])

                    emit_ln(BX, X, BXN, XN)

                    for wv in range(8):
                        if pump_here:
                            pump(1)
                        pf0 = (wv % 2) * 4
                        pi0 = pf0 + 2
                        wf_s = (wv % 2) * 2
                        wi_s = wf_s + 1
                        load_wave("wf", wv, wf_s)
                        load_wave("wi", wv, wi_s)
                        for wsl, pq in ((wf_s, pf0), (wi_s, pi0)):
                            fs = []
                            for dc in range(NDC):
                                for ecl in range(2):
                                    fs.append(lambda e, dc=dc, ecl=ecl, wsl=wsl,
                                              pq=pq: e.matmul(
                                        pb[pq + ecl][:],
                                        ws[wsl][:, dc * 256 + ecl * 128:
                                                dc * 256 + (ecl + 1) * 128],
                                        XN[:, dc * C:(dc + 1) * C],
                                        start=(dc == 0), stop=(dc == NDC - 1)))
                            k.group("pe", fs, reads=[Bws[wsl]] + BXN,
                                    writes=[Bpb[pq], Bpb[pq + 1]])
                        for ecl in range(2):
                            ec = 2 * wv + ecl
                            pf = pf0 + ecl
                            pi_ = pi0 + ecl
                            ta, tnf, tsi = gt[2 * ecl], gt[4], gt[2 * ecl + 1]
                            ga, gnf, gsi = Bgt[2 * ecl], Bgt[4], Bgt[2 * ecl + 1]
                            k.op("act", lambda e, pf=pf, ta=ta: e.activation(
                                    ta[:], pb[pf][:], AF.Sigmoid, bias=0.0, scale=1.0),
                                 reads=[Bpb[pf]], writes=[ga])
                            k.op("act", lambda e, pf=pf, tnf=tnf: e.activation(
                                    tnf[:], pb[pf][:], AF.Sigmoid, bias=0.0, scale=-1.0),
                                 reads=[Bpb[pf]], writes=[gnf])
                            k.op("act", lambda e, pi_=pi_, tsi=tsi: e.activation(
                                    tsi[:], pb[pi_][:], AF.Silu, bias=0.0, scale=1.0),
                                 reads=[Bpb[pi_]], writes=[gsi])
                            k.op("dve", lambda e, tsi=tsi, tnf=tnf: e.tensor_tensor(
                                    tsi[:], tsi[:], tnf[:], ALU.mult),
                                 reads=[gnf, gsi], writes=[gsi])
                            if pump_here:
                                pump(1)
                            init = 0.0 if first else carry[:, ec:ec + 1]
                            rd = [ga, gsi] + ([] if first else [Bcar[ec]])
                            k.op("dve", lambda e, ec=ec, ta=ta, tsi=tsi, init=init:
                                    e.tensor_tensor_scan(
                                        O[:, ec * C:(ec + 1) * C], ta[:], tsi[:],
                                        init, ALU.mult, ALU.add),
                                 reads=rd, writes=[BO[ec]])
                            k.op("dve", lambda e, ec=ec: e.tensor_copy(
                                    carry[:, ec:ec + 1],
                                    O[:, ec * C + C - 1:ec * C + C]),
                                 reads=[BO[ec]], writes=[Bcar[ec]])
                            if pump_here:
                                pump(2)

                    for ec in range(NDC):
                        bq = ec % 2
                        k.op("act", lambda e, ec=ec, bq=bq: e.activation(
                                x2t[bq][:], O[:, ec * C:(ec + 1) * C],
                                AF.Square, bias=0.0, scale=1.0),
                             reads=[BO[ec]], writes=[Bx2t[bq]])
                        k.op("pe", lambda e, ec=ec, bq=bq: e.matmul(
                                pb[6][0:1, :], onesc_bf[:], x2t[bq][:],
                                start=(ec == 0), stop=(ec == NDC - 1)),
                             reads=[Bx2t[bq], Bconst], writes=[Bpb[6]])
                    trow = smalls[0:1, :]
                    rrow = rows2[0:1, C:2 * C]
                    k.op("dve", lambda e: e.tensor_scalar(
                            trow, pb[6][0:1, :], 1.0 / D, EPS, ALU.mult, ALU.add),
                         reads=[Bpb[6]], writes=[Brows])
                    k.op("act", lambda e: e.activation(
                            trow, trow, AF.Sqrt, bias=0.0, scale=1.0),
                         reads=[Brows], writes=[Brows])
                    k.op("dve", lambda e: e.reciprocal(rrow, trow),
                         reads=[Brows], writes=[Brows])
                    k.op("pe", lambda e: e.matmul(
                            pb[7][:], onesr_f[0:1, :], rrow, start=True, stop=True),
                         reads=[Brows, Bconst], writes=[Bpb[7]])
                    k.op("dve", lambda e: e.tensor_copy(rbc[:], pb[7][:]),
                         reads=[Bpb[7]], writes=[Brbc])

                    for wv in range(8):
                        if pump_here:
                            pump(1)
                        pg0 = (wv % 2) * 4
                        wsl = (wv % 2) * 2
                        load_wave("wg", wv, wsl)
                        fs = []
                        for dc in range(NDC):
                            for ecl in range(2):
                                fs.append(lambda e, dc=dc, ecl=ecl, wsl=wsl,
                                          pg0=pg0: e.matmul(
                                    pb[pg0 + ecl][:],
                                    ws[wsl][:, dc * 256 + ecl * 128:
                                            dc * 256 + (ecl + 1) * 128],
                                    XN[:, dc * C:(dc + 1) * C],
                                    start=(dc == 0), stop=(dc == NDC - 1)))
                        k.group("pe", fs, reads=[Bws[wsl]] + BXN,
                                writes=[Bpb[pg0], Bpb[pg0 + 1]])
                        for ecl in range(2):
                            ec = 2 * wv + ecl
                            tsg, ttmp = gt[2 * ecl], gt[2 * ecl + 1]
                            gsg, gtmp = Bgt[2 * ecl], Bgt[2 * ecl + 1]
                            k.op("act", lambda e, p_=pg0 + ecl, tsg=tsg: e.activation(
                                    tsg[:], pb[p_][:], AF.Silu, bias=0.0, scale=1.0),
                                 reads=[Bpb[pg0 + ecl]], writes=[gsg])
                            if pump_here:
                                pump(1)
                            k.op("dve", lambda e, ec=ec, ttmp=ttmp:
                                    e.tensor_scalar(
                                        ttmp[:], O[:, ec * C:(ec + 1) * C],
                                        gamma[:, ec:ec + 1], None, ALU.mult),
                                 reads=[BO[ec], Bconst], writes=[gtmp])
                            k.op("pool", lambda e, tsg=tsg, ttmp=ttmp:
                                    e.tensor_tensor(
                                        ttmp[:], ttmp[:], tsg[:], ALU.mult),
                                 reads=[gsg, gtmp], writes=[gtmp])
                            k.op("dve", lambda e, ec=ec, ttmp=ttmp: e.tensor_tensor(
                                    O2[:, ec * C:(ec + 1) * C], ttmp[:], rbc[:],
                                    ALU.mult),
                                 reads=[gtmp, Brbc], writes=[BO2[ec]])
                            if pump_here:
                                pump(2)

                    for wv in range(8):
                        if pump_here:
                            pump(1)
                        po0 = (wv % 2) * 4
                        wsl = (wv % 2) * 2 + 1
                        load_wave("wo", wv, wsl)
                        fs = []
                        for dc in range(NDC):
                            for ecl in range(2):
                                fs.append(lambda e, dc=dc, ecl=ecl, wsl=wsl,
                                          po0=po0: e.matmul(
                                    pb[po0 + ecl][:],
                                    ws[wsl][:, dc * 256 + ecl * 128:
                                            dc * 256 + (ecl + 1) * 128],
                                    O2[:, dc * C:(dc + 1) * C],
                                    start=(dc == 0), stop=(dc == NDC - 1)))
                        k.group("pe", fs, reads=[Bws[wsl]] + BO2,
                                writes=[Bpb[po0], Bpb[po0 + 1]])
                        for ecl in range(2):
                            ec = 2 * wv + ecl
                            k.op("dve", lambda e, ec=ec, p_=po0 + ecl: e.tensor_tensor(
                                    X[:, ec * C:(ec + 1) * C],
                                    X[:, ec * C:(ec + 1) * C],
                                    pb[p_][:], ALU.add),
                                 reads=[Bpb[po0 + ecl]], writes=[BX[ec]])

                    if bounce:
                        k.dmav("p_xb", "pool",
                               xb_dram[ch * 128:(ch + 1) * 128, :], X[:],
                               reads=BX, writes=[Bxb[ch]])

                def emit_ffn(ch, bounced):
                    seq = ch // CH_PER_SEQ
                    t0 = (ch % CH_PER_SEQ) * C

                    if bounced:
                        for dc in range(NDC):
                            k.dmav(f"c_x{dc}", "sync",
                                   X[:, dc * C:(dc + 1) * C],
                                   xb_dram[ch * 128:(ch + 1) * 128,
                                           dc * C:(dc + 1) * C],
                                   reads=[Bxb[ch]], writes=[BX[dc]])

                    emit_ln(BX, X, BXN, XN)

                    for wv in range(22):
                        pa0 = (wv % 2) * 4
                        pb0_ = pa0 + 2
                        ws1 = (wv % 2) * 2
                        ws2 = ws1 + 1
                        u0 = 2 * wv
                        k.dmav(f"c_ws{ws1}", "sync",
                               ws[ws1].rearrange("p (a f) -> p a f", a=2),
                               wT["w1"][u0 * 128:(u0 + 2) * 128, :].rearrange(
                                   "(a p) f -> p a f", p=128),
                               reads=[BwT["w1"]], writes=[Bws[ws1]])
                        k.dmav(f"c_ws{ws2}", "sync",
                               ws[ws2].rearrange("p (a f) -> p a f", a=2),
                               wT["w2"][u0 * 128:(u0 + 2) * 128, :].rearrange(
                                   "(a p) f -> p a f", p=128),
                               reads=[BwT["w2"]], writes=[Bws[ws2]])
                        for wsl, pq in ((ws1, pa0), (ws2, pb0_)):
                            fs = []
                            for dc in range(NDC):
                                for fcl in range(2):
                                    fs.append(lambda e, dc=dc, fcl=fcl, wsl=wsl,
                                              pq=pq: e.matmul(
                                        pb[pq + fcl][:],
                                        ws[wsl][:, fcl * 2048 + dc * 128:
                                                fcl * 2048 + (dc + 1) * 128],
                                        XN[:, dc * C:(dc + 1) * C],
                                        start=(dc == 0), stop=(dc == NDC - 1)))
                            k.group("pe", fs, reads=[Bws[wsl]] + BXN,
                                    writes=[Bpb[pq], Bpb[pq + 1]])
                        for fcl in range(2):
                            fc = 2 * wv + fcl
                            pa = pa0 + fcl
                            pbb = pb0_ + fcl
                            tsa = gt[2 * fcl]
                            k.op("act", lambda e, pa=pa, tsa=tsa: e.activation(
                                    tsa[:], pb[pa][:], AF.Silu, bias=0.0, scale=1.0),
                                 reads=[Bpb[pa]], writes=[Bgt[2 * fcl]])
                            k.op("dve", lambda e, fc=fc, tsa=tsa, pbb=pbb:
                                    e.tensor_tensor(
                                        H[:, fc * C:(fc + 1) * C], tsa[:],
                                        pb[pbb][:], ALU.mult),
                                 reads=[Bgt[2 * fcl], Bpb[pbb]], writes=[BH[fc]])

                    w3alias = [[Bw3sv[0], Bws[0], Bws[1]],
                               [Bw3sv[1], Bws[2], Bws[3]]]
                    for dc in range(NDC):
                        pv = pb[dc % 2]
                        sl3 = dc % 2
                        k.dmav(f"c_w3s{sl3}", "sync", w3sv[sl3],
                               wT["w3"][dc * 128:(dc + 1) * 128, :],
                               reads=[BwT["w3"]], writes=w3alias[sl3])
                        fs = []
                        for fc in range(NFC):
                            fs.append(lambda e, fc=fc, pv=pv, sl3=sl3: e.matmul(
                                pv[:], w3sv[sl3][:, fc * 128:(fc + 1) * 128],
                                H[:, fc * C:(fc + 1) * C],
                                start=(fc == 0), stop=(fc == NFC - 1)))
                        k.group("pe", fs, reads=w3alias[sl3] + BH,
                                writes=[Bpb[dc % 2]])
                        k.op("dve", lambda e, dc=dc, pv=pv: e.tensor_tensor(
                                X[:, dc * C:(dc + 1) * C],
                                X[:, dc * C:(dc + 1) * C], pv[:], ALU.add),
                             reads=[Bpb[dc % 2]], writes=[BX[dc]])

                    k.dmav("p_y", "pool",
                           y_out[seq, :, t0:t0 + C].rearrange(
                               "(dc p) t -> p dc t", p=128),
                           X[:].rearrange("p (dc t) -> p dc t", t=C),
                           reads=BX, writes=[By_dram], waw=False)

                for ch in range(KB):
                    emit_attn(ch, True)
                pump(10 ** 6)
                for ch in range(KB):
                    emit_ffn(ch, True)
                for ch in range(KB, NCH):
                    emit_attn(ch, False)
                    emit_ffn(ch, False)

                dy = k.dstream("p_y")
                k.eng["pool"].e.wait_ge(dy.sem, dy.n)
                k.eng["sync"].e.wait_ge(dy.sem, dy.n)

    return nc


_NC_CACHE = {}


def _get_nc(coop=COOP):
    if coop not in _NC_CACHE:
        _NC_CACHE[coop] = _build(coop)
    return _NC_CACHE[coop]


def kernel(L0_attn_hidden_states, L0_attn_Wi, L0_attn_Wf, L0_attn_Wg,
           L0_attn_Wo, L0_attn_g_norm_weight, L0_ffn_weight1, L0_ffn_weight2,
           L0_ffn_weight3):
    hs = np.asarray(L0_attn_hidden_states)
    wlist = [np.asarray(w, dtype=np.float32) for w in
             (L0_attn_Wi, L0_attn_Wf, L0_attn_Wg, L0_attn_Wo)]
    blk_imgs = []
    for w in wlist:
        a4 = w.reshape(NDC, 128, 8, 256)
        for wv in range(8):
            blk_imgs.append(np.ascontiguousarray(
                a4[:, :, wv, :].transpose(1, 0, 2).reshape(128, NDC * 256)))
    gnw = np.asarray(L0_attn_g_norm_weight, dtype=np.float32)
    w1 = np.asarray(L0_ffn_weight1)
    w2 = np.asarray(L0_ffn_weight2)
    w3 = np.asarray(L0_ffn_weight3)

    gamma_fm = np.ascontiguousarray(gnw.reshape(NDC, 128).T)
    ident_f32 = np.eye(128, dtype=np.float32)
    onesc_bf = np.ones((128, 1), dtype=ml_dtypes.bfloat16)
    onesc_f = np.ones((128, 1), dtype=np.float32)
    onesr_f = np.ones((128, 128), dtype=np.float32)
    taus_arr = np.zeros((128, 16), dtype=np.float32)
    for kk in range(15):
        taus_arr[:, kk] = np.float32(-TAU[kk])

    nc = _get_nc(COOP)
    in_maps = []
    for c in range(NCORE):
        x_fm = np.ascontiguousarray(
            hs[SEQ * c:SEQ * (c + 1)].transpose(0, 2, 1))
        if COOP:
            my_pc_arr = np.ascontiguousarray(
                np.concatenate(blk_imgs[4 * c:4 * c + 4], axis=0))
            r0 = c * UPC_W1 * 128
            r1 = min(DFF, r0 + UPC_W1 * 128)
            mw1 = np.ones((UPC_W1 * 128, D), dtype=ml_dtypes.bfloat16)
            mw2 = np.ones((UPC_W1 * 128, D), dtype=ml_dtypes.bfloat16)
            if r1 > r0:
                mw1[0:r1 - r0] = w1[r0:r1]
                mw2[0:r1 - r0] = w2[r0:r1]
            msk = np.zeros((128, UPC_W1), dtype=np.float32)
            for ul in range(UPC_W1):
                if (c * UPC_W1 + ul) < W1_UNITS:
                    msk[:, ul] = 1.0
            q0 = c * UPC_W3 * 128
            mw3 = np.ascontiguousarray(w3[q0:q0 + UPC_W3 * 128])
        else:
            my_pc_arr = np.ascontiguousarray(np.concatenate(blk_imgs, axis=0))
            mw1 = np.ascontiguousarray(w1)
            mw2 = np.ascontiguousarray(w2)
            mw3 = np.ascontiguousarray(w3)
            msk = np.ones((128, W1_UNITS), dtype=np.float32)
        in_maps.append({
            "x_in": x_fm,
            "my_pc": my_pc_arr,
            "my_w1": mw1, "my_w2": mw2, "my_w3": mw3, "msk1": msk,
            "in_gamma_fm": gamma_fm, "in_ident_f32": ident_f32,
            "in_onesc_bf": onesc_bf, "in_onesc_f": onesc_f, "in_onesr_f": onesr_f,
            "in_taus": taus_arr,
        })

    res = run_bass_kernel_spmd(nc, in_maps, core_ids=list(range(NCORE)))
    out = np.empty((B, T, D), dtype=np.float32)
    for c in range(NCORE):
        y = np.asarray(res.results[c]["y_out"])
        out[SEQ * c:SEQ * (c + 1)] = y.transpose(0, 2, 1)
    return out
